# version 03
# baseline (speedup 1.0000x reference)
import contextlib
import numpy as np
import ml_dtypes
import concourse.bass as bass
import concourse.mybir as mybir
from concourse.bass_utils import run_bass_kernel_spmd

F32 = mybir.dt.float32
BF16 = mybir.dt.bfloat16
U8 = mybir.dt.uint8
AF = mybir.ActivationFunctionType
ALU = mybir.AluOpType

D = 2048
I = 8192
TPC = 2048
CH = 512
NCH = TPC // CH
NHT = D // 128
NIT = I // 128
HQ, HKV, HD = 16, 4, 128
SEQ = 4096
SCALE = 1.0 / float(np.sqrt(128.0))
EXP_OFF = 60.0

OFF_XCH = 0x04000
OFF_XN = 0x0C000
OFF_HB = 0x10000
OFF_KT = 0x10000
OFF_VT = 0x18000
OFF_XSTG = 0x10000
OFF_GW0 = 0x20000
OFF_GW1 = 0x22000
OFF_UW0 = 0x24000
OFF_UW1 = 0x26000
OFF_DW0 = 0x28000
OFF_DW1 = 0x2C000
OFF_PT = 0x20000
OFF_VTMP = 0x28000
OFF_QSQ = 0x28000
OFF_WQ0 = 0x30000
OFF_WQ1 = 0x31000
OFF_OSTG = 0x30000
OFF_IDF = 0x32000
OFF_IDB = 0x32200
OFF_ONEF = 0x32300
OFF_ONEB = 0x32500
OFF_MROW = 0x32600
OFF_BQ = 0x32700
OFF_BQS = 0x32740
OFF_BK = 0x32780
OFF_BV = 0x327A0
OFF_GSC = 0x327C0
OFF_USC = 0x328C0
OFF_DSC = 0x329C0
OFF_DSCH = 0x32A00
OFF_BEXP = 0x32A40
OFF_ROWSB = 0x32A80
OFF_ROWS = 0x33000
OFF_SG0 = 0x34800
OFF_SG1 = 0x35000
OFF_UU0 = 0x35800
OFF_UU1 = 0x36000
OFF_PST0 = 0x36800
OFF_PST1 = 0x36C00
OFF_SQ0 = 0x37000
OFF_SQ1 = 0x37400

SEM_NAMES = [
    "d_x", "d_c", "d_pk", "d_out", "d_kv",
    "d_q0", "d_q1", "d_gu0", "d_gu1", "d_dw0", "d_dw1", "d_wq0", "d_wq1",
    "d_qh0", "d_qh1", "d_x10", "d_x11",
    "s_cc", "c_pe", "c_act", "c_dve",
]


def build(stage=3):
    eff = 2 if stage == 12 else stage
    nc = bass.Bass("TRN2", target_bir_lowering=False)

    x_in = nc.dram_tensor("x_in", [TPC, D], F32, kind="ExternalInput").ap()
    wq = nc.dram_tensor("wq", [D, D], F32, kind="ExternalInput").ap()
    wk = nc.dram_tensor("wk", [D, 512], F32, kind="ExternalInput").ap()
    wv = nc.dram_tensor("wv", [D, 512], F32, kind="ExternalInput").ap()
    bq_in = nc.dram_tensor("bq_in", [128, 16], F32, kind="ExternalInput").ap()
    bk_in = nc.dram_tensor("bk_in", [128, 4], F32, kind="ExternalInput").ap()
    bv_in = nc.dram_tensor("bv_in", [128, 4], F32, kind="ExternalInput").ap()
    g4 = nc.dram_tensor("g4", [32, 128, 16, 256], U8, kind="ExternalInput").ap()
    u4 = nc.dram_tensor("u4", [32, 128, 16, 256], U8, kind="ExternalInput").ap()
    d4 = nc.dram_tensor("d4", [16, 128, 64, 128], U8, kind="ExternalInput").ap()
    gsc_in = nc.dram_tensor("gsc_in", [128, 64], F32, kind="ExternalInput").ap()
    usc_in = nc.dram_tensor("usc_in", [128, 64], F32, kind="ExternalInput").ap()
    dsc_in = nc.dram_tensor("dsc_in", [128, 16], F32, kind="ExternalInput").ap()
    idf_in = nc.dram_tensor("idf_in", [128, 128], F32, kind="ExternalInput").ap()
    idb_in = nc.dram_tensor("idb_in", [128, 128], BF16, kind="ExternalInput").ap()
    onef_in = nc.dram_tensor("onef_in", [128, 128], F32, kind="ExternalInput").ap()
    oneb_in = nc.dram_tensor("oneb_in", [128, 128], BF16, kind="ExternalInput").ap()
    mrow_in = nc.dram_tensor("mrow_in", [1, 128], BF16, kind="ExternalInput").ap()
    out = nc.dram_tensor("out", [TPC, D], F32, kind="ExternalOutput").ap()
    if stage == 11:
        dbg_xn = nc.dram_tensor("dbg_xn", [128, NHT * CH], BF16, kind="ExternalOutput").ap()
        dbg_hb = nc.dram_tensor("dbg_hb", [128, NIT * CH], BF16, kind="ExternalOutput").ap()
        dbg_x = nc.dram_tensor("dbg_x", [128, NHT * CH], F32, kind="ExternalOutput").ap()
    if stage == 12:
        dbg_q = nc.dram_tensor("dbg_q", [D, TPC], BF16, kind="ExternalOutput").ap()
        dbg_kv = nc.dram_tensor("dbg_kv", [2048, TPC], BF16, kind="ExternalOutput").ap()
        dbg_bexp = nc.dram_tensor("dbg_bexp", [128, 16], F32, kind="ExternalOutput").ap()
        dbg_xp = nc.dram_tensor("dbg_xp", [D, TPC], F32, kind="ExternalOutput").ap()
    xp = nc.dram_tensor("xp", [D, TPC], F32, kind="Internal").ap()
    xp2 = nc.dram_tensor("xp2", [D, TPC], F32, kind="Internal").ap()
    qpark = nc.dram_tensor("qpark", [D, TPC], BF16, kind="Internal").ap()
    kv_send = nc.dram_tensor("kv_send", [1024, TPC], BF16, kind="Internal").ap()
    kv_recv = nc.dram_tensor("kv_recv", [2048, TPC], BF16, kind="Internal").ap()

    SH = 0x80
    _MAP_END = 0x37800 + SH
    nc.alloc_sbuf_tensor("mapspacer", [128, _MAP_END - nc.sbuf_base], U8)
    assert nc.sbuf_base >= _MAP_END, hex(nc.sbuf_base)

    def A(name, shape, dtype, offset):
        return nc.alloc_sbuf_tensor_at(name, shape, dtype, offset=offset + SH)

    xch = A("xch", [128, NHT * CH], F32, offset=OFF_XCH)
    xn = A("xn", [128, NHT * CH], BF16, offset=OFF_XN)
    hb = A("hb", [128, NIT * CH], BF16, offset=OFF_HB)
    kT = A("kTs", [128, 4 * SEQ], BF16, offset=OFF_KT)
    v_t = A("v_ts", [128, 4 * SEQ], BF16, offset=OFF_VT)
    xstg = A("xstg", [128, 4 * D], F32, offset=OFF_XSTG)
    gw = [A("gw0", [128, 4096], BF16, offset=OFF_GW0),
          A("gw1", [128, 4096], BF16, offset=OFF_GW1)]
    uw = [A("uw0", [128, 4096], BF16, offset=OFF_UW0),
          A("uw1", [128, 4096], BF16, offset=OFF_UW1)]
    dw = [A("dw0", [128, 8192], BF16, offset=OFF_DW0),
          A("dw1", [128, 8192], BF16, offset=OFF_DW1)]
    pT = A("pTs", [128, 32 * CH], BF16, offset=OFF_PT)
    vtmp = A("vtmp", [128, 4 * SEQ], BF16, offset=OFF_VTMP)
    qsq = A("qsq", [128, SEQ], BF16, offset=OFF_QSQ)
    wqb = [A("wqb0", [128, 2048], BF16, offset=OFF_WQ0),
           A("wqb1", [128, 2048], BF16, offset=OFF_WQ1)]
    qh = wqb
    ostg = A("ostg", [128, D], F32, offset=OFF_OSTG)
    idf = A("idf", [128, 128], F32, offset=OFF_IDF)
    idb = A("idb", [128, 128], BF16, offset=OFF_IDB)
    onef = A("onef", [128, 128], F32, offset=OFF_ONEF)
    oneb = A("oneb", [128, 128], BF16, offset=OFF_ONEB)
    mrow = A("mrow", [1, 128], BF16, offset=OFF_MROW)
    bq_s = A("bq_s", [128, 16], F32, offset=OFF_BQ)
    bqs = A("bqs", [128, 16], F32, offset=OFF_BQS)
    bk_s = A("bk_s", [128, 4], F32, offset=OFF_BK)
    bv_s = A("bv_s", [128, 4], F32, offset=OFF_BV)
    gsc = A("gsc", [128, 64], F32, offset=OFF_GSC)
    usc = A("usc", [128, 64], F32, offset=OFF_USC)
    dsc = A("dsc", [128, 16], F32, offset=OFF_DSC)
    dsch = A("dsch", [128, 16], F32, offset=OFF_DSCH)
    bexp = A("bexp", [128, 16], F32, offset=OFF_BEXP)
    rowsb = A("rowsb", [128, 512], BF16, offset=OFF_ROWSB)
    rows = A("rows", [128, 1536], F32, offset=OFF_ROWS)
    sg = [A("sg0", [128, CH], F32, offset=OFF_SG0),
          A("sg1", [128, CH], F32, offset=OFF_SG1)]
    uu = [A("uu0", [128, CH], F32, offset=OFF_UU0),
          A("uu1", [128, CH], F32, offset=OFF_UU1)]
    pst = [A("pst0", [128, CH], BF16, offset=OFF_PST0),
           A("pst1", [128, CH], BF16, offset=OFF_PST1)]
    sqst = [A("sq0", [128, CH], BF16, offset=OFF_SQ0),
            A("sq1", [128, CH], BF16, offset=OFF_SQ1)]

    es = contextlib.ExitStack()
    psum = [es.enter_context(nc.psum_tensor(f"pb{i}", [128, 512], F32)) for i in range(7)]
    psb = es.enter_context(nc.psum_tensor("pb7", [128, 1024], BF16))
    SEM = {n: es.enter_context(nc.semaphore(n)) for n in SEM_NAMES}
    CNT = {n: 0 for n in SEM_NAMES}
    LASTW = {}
    BANKR = [{"c_act": 0, "c_dve": 0} for _ in range(8)]

    def inc(ins, sem, k=1):
        ins.then_inc(SEM[sem], k)
        CNT[sem] += k
        return CNT[sem]

    def wt(eng, sem, val):
        if val is None or val <= 0:
            return
        key = (id(eng), sem)
        if LASTW.get(key, -1) < val:
            eng.wait_ge(SEM[sem], val)
            LASTW[key] = val

    def bank_guard(tensor, b):
        wt(tensor, "c_act", BANKR[b]["c_act"])
        wt(tensor, "c_dve", BANKR[b]["c_dve"])

    def read_bank(b, sem, tag):
        BANKR[b][sem] = max(BANKR[b][sem], tag)

    def xch_t(ht):
        return xch[:, ht * CH:(ht + 1) * CH]

    def xn_t(ht):
        return xn[:, ht * CH:(ht + 1) * CH]

    def hb_t(it):
        return hb[:, it * CH:(it + 1) * CH]

    def row(i):
        return rows[0:1, i * 512:(i + 1) * 512]

    def rcell(c0, n=1):
        return rows[0:1, c0:c0 + n]

    blockctx = nc.Block()
    block = blockctx.__enter__()
    state = {"xn_ready": 0, "ffn_done_dve": 0, "wq_free_pe": 0,
             "gu_prev_free": 0, "x2_store_tag": 0, "scores_done_pe": 0}

    def ph_consts():
        def _(sync):
            for dst, src in [(idf, idf_in), (idb, idb_in), (onef, onef_in),
                             (oneb, oneb_in), (bq_s, bq_in), (bk_s, bk_in),
                             (bv_s, bv_in), (gsc, gsc_in), (usc, usc_in),
                             (dsc, dsc_in)]:
                inc(sync.dma_start(out=dst[:, :], in_=src), "d_c", 16)
            inc(sync.dma_start(out=mrow[0:1, :], in_=mrow_in), "d_c", 16)
        block.sync(_)
        state["consts"] = CNT["d_c"]

        def _(vector):
            wt(vector, "d_c", state["consts"])
            inc(vector.tensor_scalar_mul(bqs[:, :], bq_s[:, :], SCALE), "c_dve")
            inc(vector.tensor_scalar_mul(dsch[:, :], dsc[:, :], 0.5), "c_dve")
        block.vector(_)
        state["consts_dve"] = CNT["c_dve"]

    def ph_ln(x_ready):
        A0 = CNT["c_act"]
        P0 = CNT["c_pe"]

        def _(scalar):
            for ht in range(NHT):
                s, v = x_ready(ht)
                wt(scalar, s, v)
                if ht >= 2:
                    wt(scalar, "c_pe", P0 + 2 * (ht - 2) + 2)
                inc(scalar.activation(out=xn_t(ht), in_=xch_t(ht), func=AF.Copy), "c_act")
                inc(scalar.activation(out=sqst[ht % 2][:, :], in_=xch_t(ht),
                                      func=AF.Square), "c_act")
        block.scalar(_)

        def _(tensor):
            wt(tensor, "d_c", state["consts"])
            s, v = x_ready(NHT - 1)
            wt(tensor, s, v)
            bank_guard(tensor, 6)
            bank_guard(tensor, 4)
            for ht in range(NHT):
                wt(tensor, "c_act", A0 + 2 * ht + 1)
                inc(tensor.matmul(psum[6][0:1, :], oneb[:, 0:1], xn_t(ht),
                                  start=(ht == 0), stop=(ht == NHT - 1)), "c_pe")
                wt(tensor, "c_act", A0 + 2 * ht + 2)
                inc(tensor.matmul(psum[4][0:1, :], oneb[:, 0:1], sqst[ht % 2][:, :],
                                  start=(ht == 0), stop=(ht == NHT - 1)), "c_pe")
        block.tensor(_)
        pe_stats = CNT["c_pe"]
        assert pe_stats == P0 + 2 * NHT

        def _(vector):
            wt(vector, "c_pe", pe_stats)
            if stage == 11:
                inc(vector.tensor_copy(out=ostg[0:1, 0:512], in_=psum[6][0:1, :]), "c_dve")
                inc(vector.tensor_copy(out=ostg[0:1, 512:1024], in_=psum[4][0:1, :]), "c_dve")
            t1 = inc(vector.tensor_scalar_mul(row(0), psum[6][0:1, :], 1.0 / D), "c_dve")
            t2 = inc(vector.tensor_scalar_mul(row(1), psum[4][0:1, :], 1.0 / D), "c_dve")
            wt(vector, "c_dve", t1)
            t3 = inc(vector.tensor_mul(row(2), row(0), row(0)), "c_dve")
            wt(vector, "c_dve", t3)
            inc(vector.scalar_tensor_tensor(out=row(1), in0=row(1), scalar=1e-5,
                                            in1=row(2), op0=ALU.add, op1=ALU.subtract),
                "c_dve")
        block.vector(_)
        dve_var = CNT["c_dve"]
        read_bank(6, "c_dve", dve_var)
        read_bank(4, "c_dve", dve_var)
        if stage == 11:
            def _(vector):
                wt(vector, "c_dve", dve_var)
                inc(vector.tensor_copy(out=ostg[0:1, 1024:1536], in_=row(1)), "c_dve")
            block.vector(_)
            dve_var = CNT["c_dve"]

        def _(scalar):
            wt(scalar, "c_dve", dve_var)
            a1 = inc(scalar.activation(out=row(2), in_=row(1), func=AF.Ln), "c_act")
            wt(scalar, "c_act", a1)
            inc(scalar.activation(out=row(1), in_=row(2), func=AF.Exp, scale=-0.5), "c_act")
        block.scalar(_)
        if stage == 11:
            def _(vector):
                wt(vector, "c_act", CNT["c_act"])
                inc(vector.tensor_copy(out=sg[1][0:1, :], in_=row(2)), "c_dve")
            block.vector(_)
        act_rstd = CNT["c_act"]
        if stage == 11:
            def _(vector):
                wt(vector, "c_act", act_rstd)
                inc(vector.tensor_copy(out=ostg[0:1, 1536:2048], in_=row(1)), "c_dve")
            block.vector(_)

        def _(vector):
            wt(vector, "c_act", act_rstd)
            inc(vector.scalar_tensor_tensor(out=row(2), in0=row(0), scalar=-1.0,
                                            in1=row(1), op0=ALU.mult, op1=ALU.mult), "c_dve")
        block.vector(_)
        dve_nmr = CNT["c_dve"]

        def _(tensor):
            wt(tensor, "c_dve", dve_nmr)
            bank_guard(tensor, 6)
            bank_guard(tensor, 4)
            inc(tensor.matmul(psum[6][:, :], onef[0:1, :], row(1), start=True, stop=True), "c_pe")
            inc(tensor.matmul(psum[4][:, :], onef[0:1, :], row(2), start=True, stop=True), "c_pe")
        block.tensor(_)
        pe_bc = CNT["c_pe"]

        def _(vector):
            wt(vector, "c_pe", pe_bc)
            for ht in range(NHT):
                t = inc(vector.tensor_mul(sg[0][:, :], xch_t(ht), psum[6][:, :]), "c_dve")
                wt(vector, "c_dve", t)
                inc(vector.tensor_add(xn_t(ht), sg[0][:, :], psum[4][:, :]), "c_dve")
        block.vector(_)
        state["xn_ready"] = CNT["c_dve"]
        read_bank(6, "c_dve", state["xn_ready"])
        read_bank(4, "c_dve", state["xn_ready"])

    def ph_ffn(warena_free_pe):
        NB = 32
        P0 = CNT["c_pe"]
        A0 = CNT["c_act"]
        V0 = CNT["c_dve"]
        gu_base = [CNT["d_gu0"], CNT["d_gu1"]]

        def tag_g(it):
            return P0 + 2 * it + 1

        def tag_u(it):
            return P0 + 2 * it + 2

        def tag_silu(it):
            return A0 + it + 1

        def tag_uu(it):
            return V0 + 2 * it + 1

        def tag_h(it):
            return V0 + 2 * it + 2

        def _(gpsimd):
            for ib in range(NB):
                if ib >= 2:
                    wt(gpsimd, "c_pe", tag_u(2 * ib - 3))
                else:
                    wt(gpsimd, "c_pe", warena_free_pe)
                sem = "d_gu0" if ib % 2 == 0 else "d_gu1"
                inc(gpsimd.dma_start(
                    out=gw[ib % 2][:].rearrange("p (n c) -> p n c", c=256), in_=g4[ib]), sem, 16)
                inc(gpsimd.dma_start(
                    out=uw[ib % 2][:].rearrange("p (n c) -> p n c", c=256), in_=u4[ib]), sem, 16)
        block.gpsimd(_)

        def _(tensor):
            wt(tensor, "c_dve", state["xn_ready"])
            for it in range(2 * NB):
                ib = it // 2
                sem = "d_gu0" if ib % 2 == 0 else "d_gu1"
                wt(tensor, sem, gu_base[ib % 2] + 32 * (ib // 2 + 1))
                if it >= 2:
                    wt(tensor, "c_act", tag_silu(it - 2))
                    wt(tensor, "c_dve", tag_uu(it - 2))
                else:
                    bank_guard(tensor, it % 2)
                    bank_guard(tensor, 2 + it % 2)
                sl = (it % 2) * 128
                for ht in range(NHT):
                    ins = tensor.matmul(
                        psum[it % 2][:, :],
                        gw[ib % 2][:, ht * 256 + sl:ht * 256 + sl + 128],
                        xn_t(ht), start=(ht == 0), stop=(ht == NHT - 1))
                inc(ins, "c_pe")
                for ht in range(NHT):
                    ins = tensor.matmul(
                        psum[2 + it % 2][:, :],
                        uw[ib % 2][:, ht * 256 + sl:ht * 256 + sl + 128],
                        xn_t(ht), start=(ht == 0), stop=(ht == NHT - 1))
                inc(ins, "c_pe")
        block.tensor(_)
        assert CNT["c_pe"] == P0 + 4 * NB

        def _(scalar):
            wt(scalar, "d_c", state["consts"])
            for it in range(2 * NB):
                wt(scalar, "c_pe", tag_g(it))
                if it >= 2:
                    wt(scalar, "c_dve", tag_h(it - 2))
                inc(scalar.activation(out=sg[it % 2][:, :], in_=psum[it % 2][:, :],
                                      func=AF.Silu, scale=gsc[:, it:it + 1]), "c_act")
        block.scalar(_)
        assert CNT["c_act"] == A0 + 2 * NB

        def _(vector):
            wt(vector, "d_c", state["consts"])
            for it in range(2 * NB):
                wt(vector, "c_pe", tag_u(it))
                t = inc(vector.tensor_scalar_mul(uu[it % 2][:, :], psum[2 + it % 2][:, :],
                                                 usc[:, it:it + 1]), "c_dve")
                wt(vector, "c_act", tag_silu(it))
                wt(vector, "c_dve", t)
                inc(vector.tensor_mul(hb_t(it), sg[it % 2][:, :], uu[it % 2][:, :]), "c_dve")
        block.vector(_)
        assert CNT["c_dve"] == V0 + 4 * NB
        for b in range(4):
            read_bank(b, "c_act", A0 + 2 * NB)
            read_bank(b, "c_dve", V0 + 4 * NB)

        def _(tensor):
            bank_guard(tensor, 6)
            for it in range(2 * NB):
                wt(tensor, "c_dve", tag_h(it))
                inc(tensor.matmul(psum[6][0:1, :], oneb[:, 0:1], hb_t(it),
                                  start=(it == 0), stop=(it == 2 * NB - 1)), "c_pe")
        block.tensor(_)
        pe_csh = CNT["c_pe"]

        def _(vector):
            wt(vector, "c_pe", pe_csh)
            t1 = inc(vector.tensor_copy(out=row(0), in_=psum[6][0:1, :]), "c_dve")
            wt(vector, "c_dve", t1)
            t2 = inc(vector.tensor_copy(out=rowsb[0:1, :], in_=row(0)), "c_dve")
            wt(vector, "c_dve", t2)
            inc(vector.tensor_sub(sqst[0][0:1, :], row(0), rowsb[0:1, :]), "c_dve")
        block.vector(_)
        csh_ready = CNT["c_dve"]
        read_bank(6, "c_dve", csh_ready)

        Pd0 = CNT["c_pe"]
        Vd0 = CNT["c_dve"]
        dw_base = [CNT["d_dw0"], CNT["d_dw1"]]

        def tag_d(hht):
            return Pd0 + hht + 1

        def tag_stt(hht):
            return Vd0 + hht + 1

        def _(gpsimd):
            for hht in range(NHT):
                if hht >= 2:
                    wt(gpsimd, "c_pe", tag_d(hht - 2))
                sem = "d_dw0" if hht % 2 == 0 else "d_dw1"
                inc(gpsimd.dma_start(
                    out=dw[hht % 2][:].rearrange("p (n c) -> p n c", c=128), in_=d4[hht]), sem, 16)
        block.gpsimd(_)

        def _(tensor):
            wt(tensor, "c_dve", csh_ready)
            bank_guard(tensor, 4)
            bank_guard(tensor, 5)
            for hht in range(NHT):
                sem = "d_dw0" if hht % 2 == 0 else "d_dw1"
                wt(tensor, sem, dw_base[hht % 2] + 16 * (hht // 2 + 1))
                if hht >= 2:
                    wt(tensor, "c_dve", tag_stt(hht - 2))
                tensor.matmul(psum[4 + hht % 2][:, :], mrow[0:1, :], rowsb[0:1, :],
                              start=True, stop=False)
                tensor.matmul(psum[4 + hht % 2][:, :], mrow[0:1, :], sqst[0][0:1, :],
                              start=False, stop=False)
                for it in range(NIT):
                    ins = tensor.matmul(
                        psum[4 + hht % 2][:, :],
                        dw[hht % 2][:, it * 128:(it + 1) * 128],
                        hb_t(it), start=False, stop=(it == NIT - 1))
                inc(ins, "c_pe")
        block.tensor(_)
        assert CNT["c_pe"] == Pd0 + NHT

        def _(vector):
            for hht in range(NHT):
                wt(vector, "c_pe", tag_d(hht))
                inc(vector.scalar_tensor_tensor(
                    out=xch_t(hht), in0=psum[4 + hht % 2][:, :], scalar=dsch[:, hht:hht + 1],
                    in1=xch_t(hht), op0=ALU.mult, op1=ALU.add), "c_dve")
        block.vector(_)
        assert CNT["c_dve"] == Vd0 + NHT
        state["ffn_done_dve"] = CNT["c_dve"]
        read_bank(4, "c_dve", Vd0 + NHT)
        read_bank(5, "c_dve", Vd0 + NHT)
        state["gu_free_pe"] = tag_u(2 * NB - 1)
        state["dw_free_pe"] = CNT["c_pe"]

    def ph_load_x_transposed(ch, xch_free_pe, park_tag):
        def _(sync):
            wt(sync, "c_pe", xch_free_pe)
            wt(sync, "d_pk", park_tag)
            for tt in range(4):
                inc(sync.dma_start(out=xstg[:, tt * D:(tt + 1) * D],
                                   in_=x_in[ch * CH + tt * 128: ch * CH + (tt + 1) * 128, :]),
                    "d_x", 16)
        block.sync(_)
        xload = CNT["d_x"]
        P0 = CNT["c_pe"]
        V0 = CNT["c_dve"]

        def _(tensor):
            wt(tensor, "d_c", state["consts"])
            wt(tensor, "d_x", xload)
            bank_guard(tensor, 4)
            bank_guard(tensor, 5)
            for ht in range(NHT):
                if ht >= 2:
                    wt(tensor, "c_dve", V0 + ht - 1)
                for tt in range(4):
                    ins = tensor.transpose(
                        psum[4 + ht % 2][:, tt * 128:(tt + 1) * 128],
                        xstg[:, tt * D + ht * 128: tt * D + (ht + 1) * 128],
                        idf[:, :])
                inc(ins, "c_pe")
        block.tensor(_)

        def _(vector):
            for ht in range(NHT):
                wt(vector, "c_pe", P0 + ht + 1)
                inc(vector.tensor_copy(out=xch_t(ht), in_=psum[4 + ht % 2][:, :]), "c_dve")
        block.vector(_)
        read_bank(4, "c_dve", V0 + NHT)
        read_bank(5, "c_dve", V0 + NHT)
        return lambda ht: ("c_dve", V0 + ht + 1)

    def ph_load_x_hmajor(ch, src, xch_free_pe):
        def _(sync):
            wt(sync, "c_pe", xch_free_pe)
            wt(sync, "d_pk", state["att_done_pk"])
            inc(sync.dma_start(
                out=xch[:].rearrange("p (n c) -> p n c", c=CH),
                in_=src.rearrange("(n p) t -> p n t", p=128)[:, :, ch * CH:(ch + 1) * CH]),
                "d_x", 16)
        block.sync(_)
        xload = CNT["d_x"]
        return lambda ht: ("d_x", xload)

    def ph_proj(ch):
        NS = 24
        P0 = CNT["c_pe"]
        wq_base = [CNT["d_wq0"], CNT["d_wq1"]]
        q_base = [CNT["d_q0"], CNT["d_q1"]]

        def tag_p(s):
            return P0 + s + 1

        def _(gpsimd):
            for s in range(NS):
                if s >= 2:
                    wt(gpsimd, "c_pe", tag_p(s - 2))
                else:
                    wt(gpsimd, "c_pe", state["wq_free_pe"])
                sem = "d_wq0" if s % 2 == 0 else "d_wq1"
                if s < 16:
                    src = wq.rearrange("(n p) c -> p n c", p=128)[:, :, s * 128:(s + 1) * 128]
                elif s < 20:
                    dt = s - 16
                    src = wk.rearrange("(n p) c -> p n c", p=128)[:, :, dt * 128:(dt + 1) * 128]
                else:
                    dt = s - 20
                    src = wv.rearrange("(n p) c -> p n c", p=128)[:, :, dt * 128:(dt + 1) * 128]
                inc(gpsimd.dma_start(
                    out=wqb[s % 2][:].rearrange("p (n c) -> p n c", c=128), in_=src), sem, 16)
        block.gpsimd(_)

        V0 = CNT["c_dve"]

        def tag_e(s):
            return V0 + s + 1

        def _(tensor):
            wt(tensor, "c_dve", state["xn_ready"])
            for s in range(NS):
                sem = "d_wq0" if s % 2 == 0 else "d_wq1"
                wt(tensor, sem, wq_base[s % 2] + 16 * (s // 2 + 1))
                if s >= 2:
                    wt(tensor, "c_dve", tag_e(s - 2))
                else:
                    bank_guard(tensor, s % 2)
                for ht in range(NHT):
                    ins = tensor.matmul(
                        psum[s % 2][:, :],
                        wqb[s % 2][:, ht * 128:(ht + 1) * 128],
                        xn_t(ht), start=(ht == 0), stop=(ht == NHT - 1))
                inc(ins, "c_pe")
        block.tensor(_)

        def tag_ev(s):
            return tag_e(s)

        def _(vector):
            wt(vector, "c_dve", state["consts_dve"])
            for s in range(NS):
                wt(vector, "c_pe", tag_p(s))
                semd = "d_q0" if s % 2 == 0 else "d_q1"
                wt(vector, semd, q_base[s % 2] + 16 * (s // 2))
                if s < 16:
                    inc(vector.tensor_scalar(out=pst[s % 2][:, :], in0=psum[s % 2][:, :],
                                             scalar1=SCALE, scalar2=bqs[:, s:s + 1],
                                             op0=ALU.mult, op1=ALU.add), "c_dve")
                elif s < 20:
                    inc(vector.tensor_scalar_add(pst[s % 2][:, :], psum[s % 2][:, :],
                                                 bk_s[:, s - 16:s - 15]), "c_dve")
                else:
                    inc(vector.tensor_scalar_add(pst[s % 2][:, :], psum[s % 2][:, :],
                                                 bv_s[:, s - 20:s - 19]), "c_dve")
        block.vector(_)

        def _(scalar):
            for s in range(NS):
                wt(scalar, "c_dve", tag_ev(s))
                sem = "d_q0" if s % 2 == 0 else "d_q1"
                if s < 16:
                    dst = qpark[s * 128:(s + 1) * 128, ch * CH:(ch + 1) * CH]
                elif s < 20:
                    dst = kv_send[(s - 16) * 128:(s - 15) * 128, ch * CH:(ch + 1) * CH]
                else:
                    dst = kv_send[512 + (s - 20) * 128:512 + (s - 19) * 128,
                                  ch * CH:(ch + 1) * CH]
                inc(scalar.dma_start(out=dst, in_=pst[s % 2][:, :]), sem, 16)
        block.scalar(_)
        read_bank(0, "c_dve", tag_ev(NS - 2))
        read_bank(1, "c_dve", tag_ev(NS - 1))
        state["wq_free_pe"] = tag_p(NS - 1)
        state["proj_last_pe"] = tag_p(NS - 1)

    def ph_park(ch):
        def _(sync):
            wt(sync, "c_dve", state["ffn_done_dve"])
            inc(sync.dma_start(
                out=xp.rearrange("(n p) t -> p n t", p=128)[:, :, ch * CH:(ch + 1) * CH],
                in_=xch[:].rearrange("p (n c) -> p n c", c=CH)), "d_pk", 16)
        block.sync(_)
        return CNT["d_pk"]

    def ph_out(ch):
        P0 = CNT["c_pe"]
        V0 = CNT["c_dve"]
        O0 = CNT["d_out"]

        def _(tensor):
            wt(tensor, "d_c", state["consts"])
            wt(tensor, "c_dve", state["ffn_done_dve"])
            bank_guard(tensor, 4)
            bank_guard(tensor, 5)
            gi = 0
            for tt in range(4):
                for grp in range(4):
                    if gi >= 2:
                        wt(tensor, "c_dve", V0 + gi - 1)
                    bank = 4 + gi % 2
                    for j in range(4):
                        ht = grp * 4 + j
                        ins = tensor.transpose(
                            psum[bank][:, j * 128:(j + 1) * 128],
                            xch_t(ht)[:, tt * 128:(tt + 1) * 128],
                            idf[:, :])
                    inc(ins, "c_pe")
                    gi += 1
        block.tensor(_)

        def _(vector):
            gi = 0
            for tt in range(4):
                wt(vector, "d_out", O0 + 16 * tt)
                for grp in range(4):
                    wt(vector, "c_pe", P0 + gi + 1)
                    inc(vector.tensor_copy(out=ostg[:, grp * 512:(grp + 1) * 512],
                                           in_=psum[4 + gi % 2][:, :]), "c_dve")
                    gi += 1
        block.vector(_)

        def _(sync):
            for tt in range(4):
                wt(sync, "c_dve", V0 + 4 * tt + 4)
                inc(sync.dma_start(out=out[ch * CH + tt * 128: ch * CH + (tt + 1) * 128, :],
                                   in_=ostg[:, :]), "d_out", 16)
        block.sync(_)
        read_bank(4, "c_dve", V0 + 16)
        read_bank(5, "c_dve", V0 + 16)
        state["out_pe"] = CNT["c_pe"]
        state["out_last_dve"] = CNT["c_dve"]

    def ph_attention():
        def _(gpsimd):
            wt(gpsimd, "d_q0", CNT["d_q0"])
            wt(gpsimd, "d_q1", CNT["d_q1"])
            inc(gpsimd.collective_compute(
                "AllGather", ALU.bypass,
                replica_groups=[[0, 1], [2, 3], [4, 5], [6, 7]],
                ins=[kv_send[:, :]], outs=[kv_recv[:, :]]), "s_cc", 1)
        block.gpsimd(_)

        qh_base = [CNT["d_qh0"], CNT["d_qh1"]]

        def _(sync):
            wt(sync, "s_cc", CNT["s_cc"])
            inc(sync.dma_start(
                out=kT[:].rearrange("p (g t) -> p g t", t=SEQ)[:, :, 0:TPC],
                in_=kv_recv[0:512, :].rearrange("(g p) t -> p g t", p=128)), "d_kv", 16)
            inc(sync.dma_start(
                out=kT[:].rearrange("p (g t) -> p g t", t=SEQ)[:, :, TPC:SEQ],
                in_=kv_recv[1024:1536, :].rearrange("(g p) t -> p g t", p=128)), "d_kv", 16)
            inc(sync.dma_start(
                out=vtmp[:].rearrange("p (g t) -> p g t", t=SEQ)[:, :, 0:TPC],
                in_=kv_recv[512:1024, :].rearrange("(g p) t -> p g t", p=128)), "d_kv", 16)
            inc(sync.dma_start(
                out=vtmp[:].rearrange("p (g t) -> p g t", t=SEQ)[:, :, TPC:SEQ],
                in_=kv_recv[1536:2048, :].rearrange("(g p) t -> p g t", p=128)), "d_kv", 16)
            inc(sync.dma_start(out=qh[0][:, :], in_=qpark[0:128, :]), "d_qh0", 16)
        block.sync(_)
        kvload = CNT["d_kv"]

        P0 = CNT["c_pe"]
        V0 = CNT["c_dve"]

        def _(tensor):
            wt(tensor, "d_c", state["consts"])
            wt(tensor, "d_kv", kvload)
            for grp in range(16):
                if grp >= 1:
                    wt(tensor, "c_dve", V0 + 8 * grp)
                for j in range(8):
                    tile = grp * 8 + j
                    g, tkt = tile // 32, tile % 32
                    ins = tensor.transpose(
                        psb[:, j * 128:(j + 1) * 128],
                        vtmp[:, g * SEQ + tkt * 128: g * SEQ + (tkt + 1) * 128],
                        idb[:, :])
                inc(ins, "c_pe")
        block.tensor(_)

        def _(vector):
            for grp in range(16):
                wt(vector, "c_pe", P0 + grp + 1)
                for j in range(8):
                    tile = grp * 8 + j
                    g, tkt = tile // 32, tile % 32
                    inc(vector.tensor_copy(
                        out=v_t[:, g * SEQ + tkt * 128:g * SEQ + (tkt + 1) * 128],
                        in_=psb[:, j * 128:(j + 1) * 128]), "c_dve")
        block.vector(_)
        vt_done = CNT["c_dve"]
        read_bank(7, "c_dve", vt_done)

        prev_mm = 0
        for g in range(4):
            Vs = CNT["c_dve"]

            def _(vector, g=g, prev_mm=prev_mm):
                wt(vector, "c_dve", vt_done)
                wt(vector, "c_pe", prev_mm)
                inc(vector.tensor_mul(qsq[:, :], kT[:, g * SEQ:(g + 1) * SEQ],
                                      kT[:, g * SEQ:(g + 1) * SEQ]), "c_dve")
            block.vector(_)
            Pn = CNT["c_pe"]
            Vr0 = CNT["c_dve"]

            def _(tensor, Vs=Vs, Vr0=Vr0):
                wt(tensor, "c_dve", Vs + 1)
                bank_guard(tensor, 6)
                for q4 in range(8):
                    if q4 >= 1:
                        wt(tensor, "c_dve", Vr0 + q4)
                    inc(tensor.matmul(psum[6][0:1, :], oneb[:, 0:1],
                                      qsq[:, q4 * 512:(q4 + 1) * 512],
                                      start=True, stop=True), "c_pe")
            block.tensor(_)
            prev_mm = CNT["c_pe"]

            def _(vector, g=g, Pn=Pn):
                t = 0
                for q4 in range(8):
                    wt(vector, "c_pe", Pn + q4 + 1)
                    t = inc(vector.reduce_max(rcell(1024 + q4), psum[6][0:1, :],
                                              axis=mybir.AxisListType.X), "c_dve")
                wt(vector, "c_dve", t)
                inc(vector.reduce_max(rcell(1045 + g), rows[0:1, 1024:1032],
                                      axis=mybir.AxisListType.X), "c_dve")
            block.vector(_)
            read_bank(6, "c_dve", CNT["c_dve"])
        knorm_done = CNT["c_dve"]

        for hd in range(HQ):
            g = hd % HKV
            par = hd % 2
            semq = "d_qh0" if par == 0 else "d_qh1"
            qtag = qh_base[par] + 16 * (hd // 2 + 1)

            Vq0 = CNT["c_dve"]

            def _(vector, hd=hd, qtag=qtag, semq=semq):
                wt(vector, semq, qtag)
                wt(vector, "c_dve", knorm_done)
                inc(vector.tensor_mul(qsq[:, 0:TPC], qh[hd % 2][:, :], qh[hd % 2][:, :]),
                    "c_dve")
            block.vector(_)
            Pq = CNT["c_pe"]
            Vr0 = CNT["c_dve"]

            def _(tensor, Vq0=Vq0, Vr0=Vr0):
                wt(tensor, "c_dve", Vq0 + 1)
                bank_guard(tensor, 6)
                for q4 in range(4):
                    if q4 >= 1:
                        wt(tensor, "c_dve", Vr0 + q4)
                    inc(tensor.matmul(psum[6][0:1, :], oneb[:, 0:1],
                                      qsq[:, q4 * 512:(q4 + 1) * 512],
                                      start=True, stop=True), "c_pe")
            block.tensor(_)

            def _(vector, g=g, Pq=Pq):
                t = 0
                for q4 in range(4):
                    wt(vector, "c_pe", Pq + q4 + 1)
                    t = inc(vector.reduce_max(rcell(1024 + q4), psum[6][0:1, :],
                                              axis=mybir.AxisListType.X), "c_dve")
                wt(vector, "c_dve", t)
                t = inc(vector.reduce_max(rcell(1040), rows[0:1, 1024:1028],
                                          axis=mybir.AxisListType.X), "c_dve")
                wt(vector, "c_dve", t)
                inc(vector.tensor_mul(rcell(1041), rcell(1040), rcell(1045 + g)), "c_dve")
            block.vector(_)
            Vb = CNT["c_dve"]
            read_bank(6, "c_dve", Vb)

            def _(scalar, Vb=Vb):
                wt(scalar, "c_dve", Vb)
                a = inc(scalar.activation(out=rcell(1042), in_=rcell(1041), func=AF.Ln),
                        "c_act")
                wt(scalar, "c_act", a)
                inc(scalar.activation(out=rcell(1043), in_=rcell(1042), func=AF.Exp,
                                      scale=0.5), "c_act")
            block.scalar(_)
            Ab = CNT["c_act"]

            def _(vector, Ab=Ab):
                wt(vector, "c_act", Ab)
                inc(vector.tensor_scalar(out=rcell(1044), in0=rcell(1043),
                                         scalar1=-1.001, scalar2=EXP_OFF,
                                         op0=ALU.mult, op1=ALU.add), "c_dve")
            block.vector(_)
            Vb2 = CNT["c_dve"]

            def _(tensor, Vb2=Vb2):
                wt(tensor, "c_dve", Vb2)
                bank_guard(tensor, 6)
                inc(tensor.matmul(psum[6][:, 0:1], onef[0:1, :], rcell(1044),
                                  start=True, stop=True), "c_pe")
            block.tensor(_)
            Pb = CNT["c_pe"]

            def _(vector, hd=hd, Pb=Pb):
                wt(vector, "c_pe", Pb)
                inc(vector.tensor_copy(out=bexp[:, hd:hd + 1], in_=psum[6][:, 0:1]), "c_dve")
            block.vector(_)
            bexp_done = CNT["c_dve"]
            read_bank(6, "c_dve", bexp_done)

            if hd + 1 < HQ:
                def _(sync, hd=hd):
                    wt(sync, "c_pe", state["scores_done_pe"])
                    sem2 = "d_qh0" if (hd + 1) % 2 == 0 else "d_qh1"
                    inc(sync.dma_start(out=qh[(hd + 1) % 2][:, :],
                                       in_=qpark[(hd + 1) * 128:(hd + 2) * 128, :]), sem2, 16)
                block.sync(_)

            for blk in range(NCH):
                bpar = (hd * NCH + blk) % 2
                semx = "d_x10" if bpar == 0 else "d_x11"

                def _(sync, hd=hd, blk=blk, bpar=bpar, semx=semx):
                    wt(sync, "c_dve", state.get(f"x1_free_{bpar}", 0))
                    inc(sync.dma_start(
                        out=sg[bpar][:, :],
                        in_=xp[hd * 128:(hd + 1) * 128, blk * CH:(blk + 1) * CH]), semx, 16)
                block.sync(_)
                x1tag = CNT[semx]

                P0s = CNT["c_pe"]
                A0s = CNT["c_act"]

                def _(tensor, hd=hd, g=g, blk=blk, A0s=A0s):
                    bank_guard(tensor, 0)
                    bank_guard(tensor, 1)
                    for tkt in range(32):
                        if tkt >= 2:
                            wt(tensor, "c_act", A0s + tkt - 1)
                        inc(tensor.matmul(
                            psum[tkt % 2][:, :],
                            kT[:, g * SEQ + tkt * 128:g * SEQ + (tkt + 1) * 128],
                            qh[hd % 2][:, blk * CH:(blk + 1) * CH],
                            start=True, stop=True), "c_pe")
                block.tensor(_)
                state["scores_done_pe"] = CNT["c_pe"]

                def _(scalar, hd=hd, P0s=P0s, bexp_done=bexp_done):
                    wt(scalar, "c_dve", bexp_done)
                    for tkt in range(32):
                        wt(scalar, "c_pe", P0s + tkt + 1)
                        inc(scalar.activation(
                            out=pT[:, tkt * CH:(tkt + 1) * CH], in_=psum[tkt % 2][:, :],
                            func=AF.Exp, bias=bexp[:, hd:hd + 1]), "c_act")
                block.scalar(_)
                exp_base = A0s
                read_bank(0, "c_act", exp_base + 31)
                read_bank(1, "c_act", exp_base + 32)

                def _(tensor, g=g, exp_base=exp_base):
                    bank_guard(tensor, 2)
                    bank_guard(tensor, 3)
                    for tkt in range(32):
                        wt(tensor, "c_act", exp_base + tkt + 1)
                        tensor.matmul(psum[2][:, :],
                                      v_t[:, g * SEQ + tkt * 128:g * SEQ + (tkt + 1) * 128],
                                      pT[:, tkt * CH:(tkt + 1) * CH],
                                      start=(tkt == 0), stop=(tkt == 31))
                        ins = tensor.matmul(psum[3][:, :], oneb[:, :],
                                            pT[:, tkt * CH:(tkt + 1) * CH],
                                            start=(tkt == 0), stop=(tkt == 31))
                    inc(ins, "c_pe")
                block.tensor(_)
                Pctx = CNT["c_pe"]

                def _(vector, bpar=bpar, semx=semx, x1tag=x1tag, Pctx=Pctx):
                    wt(vector, "c_pe", Pctx)
                    wt(vector, "d_pk", state["x2_store_tag"])
                    t = inc(vector.reciprocal(uu[0][:, :], psum[3][:, :]), "c_dve")
                    wt(vector, "c_dve", t)
                    t = inc(vector.tensor_mul(uu[1][:, :], psum[2][:, :], uu[0][:, :]),
                            "c_dve")
                    wt(vector, semx, x1tag)
                    wt(vector, "c_dve", t)
                    inc(vector.tensor_add(uu[0][:, :], uu[1][:, :], sg[bpar][:, :]), "c_dve")
                block.vector(_)
                state[f"x1_free_{bpar}"] = CNT["c_dve"]
                x2ready = CNT["c_dve"]
                read_bank(2, "c_dve", x2ready)
                read_bank(3, "c_dve", x2ready)

                def _(sync, hd=hd, blk=blk, x2ready=x2ready):
                    wt(sync, "c_dve", x2ready)
                    inc(sync.dma_start(
                        out=xp2[hd * 128:(hd + 1) * 128, blk * CH:(blk + 1) * CH],
                        in_=uu[0][:, :]), "d_pk", 16)
                block.sync(_)
                state["x2_store_tag"] = CNT["d_pk"]

        state["att_done_pk"] = CNT["d_pk"]
        state["att_done_pe"] = CNT["c_pe"]

    ph_consts()
    if stage == 11:
        x_ready = ph_load_x_transposed(0, 0, 0)
        ph_ln(x_ready)
        ph_ffn(0)
        def _(sync):
            wt(sync, "c_dve", state["ffn_done_dve"])
            inc(sync.dma_start(out=dbg_xn, in_=xn[:, :]), "d_out", 16)
            inc(sync.dma_start(out=dbg_hb, in_=hb[:, :]), "d_out", 16)
            inc(sync.dma_start(out=dbg_x, in_=xch[:, :]), "d_out", 16)
            inc(sync.dma_start(out=out[0:128, :], in_=xstg[:, 0:D]), "d_out", 16)
            inc(sync.dma_start(out=out[200:201, 0:2048], in_=ostg[0:1, :]), "d_out", 16)
            inc(sync.dma_start(out=out[210:211, 0:512], in_=sg[1][0:1, :]), "d_out", 16)
            wt(sync, "d_out", CNT["d_out"])
        block.sync(_)
        blockctx.__exit__(None, None, None)
        es.close()
        return nc
    park_tag = 0
    xch_free_pe = 0
    for ch in range(NCH):
        x_ready = ph_load_x_transposed(ch, xch_free_pe, park_tag)
        ph_ln(x_ready)
        ph_ffn(state["gu_prev_free"])
        state["gu_prev_free"] = state["gu_free_pe"]
        if eff >= 2:
            ph_ln(lambda ht: ("c_dve", state["ffn_done_dve"]))
            ph_proj(ch)
            xch_free_pe = state["proj_last_pe"]
        else:
            xch_free_pe = state["dw_free_pe"]
        park_tag = ph_park(ch)
        if eff == 1:
            ph_out(ch)
            xch_free_pe = state["out_pe"]

    if eff >= 2:
        ph_attention()
        state["gu_prev_free"] = state["att_done_pe"]
        for ch in range(NCH):
            x_ready = ph_load_x_hmajor(ch, xp2, xch_free_pe)
            if eff >= 3:
                ph_ln(x_ready)
                ph_ffn(state["gu_prev_free"])
                state["gu_prev_free"] = state["gu_free_pe"]
            else:
                def _(vector, xr=x_ready):
                    s, v = xr(0)
                    wt(vector, s, v)
                    inc(vector.tensor_copy(out=sg[0][:, 0:8], in_=xch[:, 0:8]), "c_dve")
                block.vector(_)
                state["ffn_done_dve"] = CNT["c_dve"]
            ph_out(ch)
            xch_free_pe = state["out_pe"]

    def _(sync):
        if stage == 12:
            wt(sync, "d_pk", state["att_done_pk"])
            inc(sync.dma_start(out=dbg_q, in_=qpark[:, :]), "d_out", 16)
            inc(sync.dma_start(out=dbg_kv, in_=kv_recv[:, :]), "d_out", 16)
            inc(sync.dma_start(out=dbg_xp, in_=xp[:, :]), "d_out", 16)
        wt(sync, "d_out", CNT["d_out"])
        wt(sync, "d_pk", CNT["d_pk"])
    block.sync(_)

    if stage == 12:
        def _(vector):
            inc(vector.tensor_copy(out=sg[0][:, 0:16], in_=bexp[:, :]), "c_dve")
        block.vector(_)

        def _(sync):
            wt(sync, "c_dve", CNT["c_dve"])
            inc(sync.dma_start(out=dbg_bexp, in_=sg[0][:, 0:16]), "d_out", 16)
            wt(sync, "d_out", CNT["d_out"])
        block.sync(_)

    blockctx.__exit__(None, None, None)
    es.close()
    return nc



def _byte0(w):
    a = np.ascontiguousarray(w)
    assert a.dtype == np.int32
    return a.view(np.uint8).reshape(a.shape[0], a.shape[1], 4)[:, :, 0]


def kernel(L0_attn_hidden_states, L0_attn_Wq, L0_attn_bq, L0_attn_Wk, L0_attn_bk,
           L0_attn_Wv, L0_attn_bv, L0_ffn_gate_w, L0_ffn_gate_scale, L0_ffn_up_w,
           L0_ffn_up_scale, L0_ffn_down_w, L0_ffn_down_scale, _stage=3):
    bf = ml_dtypes.bfloat16
    x = np.ascontiguousarray(np.asarray(L0_attn_hidden_states, dtype=np.float32)).reshape(16384, D)
    wq = np.ascontiguousarray(np.asarray(L0_attn_Wq, dtype=np.float32))
    wk = np.ascontiguousarray(np.asarray(L0_attn_Wk, dtype=np.float32))
    wv = np.ascontiguousarray(np.asarray(L0_attn_Wv, dtype=np.float32))
    bq = np.ascontiguousarray(np.asarray(L0_attn_bq, dtype=np.float32).reshape(16, 128).T)
    bk = np.ascontiguousarray(np.asarray(L0_attn_bk, dtype=np.float32).reshape(4, 128).T)
    bv = np.ascontiguousarray(np.asarray(L0_attn_bv, dtype=np.float32).reshape(4, 128).T)
    gT = np.ascontiguousarray(_byte0(np.asarray(L0_ffn_gate_w)).T)
    uT = np.ascontiguousarray(_byte0(np.asarray(L0_ffn_up_w)).T)
    dT = np.ascontiguousarray(_byte0(np.asarray(L0_ffn_down_w)).T)
    g4 = np.ascontiguousarray(gT.reshape(16, 128, 32, 256).transpose(2, 1, 0, 3))
    u4 = np.ascontiguousarray(uT.reshape(16, 128, 32, 256).transpose(2, 1, 0, 3))
    d4 = np.ascontiguousarray(dT.reshape(64, 128, 16, 128).transpose(2, 1, 0, 3))
    gsc = np.ascontiguousarray(np.asarray(L0_ffn_gate_scale, np.float32).reshape(64, 128).T)
    usc = np.ascontiguousarray(np.asarray(L0_ffn_up_scale, np.float32).reshape(64, 128).T)
    dsc = np.ascontiguousarray(np.asarray(L0_ffn_down_scale, np.float32).reshape(16, 128).T)
    idf = np.eye(128, dtype=np.float32)
    idb = np.eye(128, dtype=np.float32).astype(bf)
    onef = np.ones((128, 128), np.float32)
    oneb = np.ones((128, 128), np.float32).astype(bf)
    mrow = np.full((1, 128), -128.0, np.float32).astype(bf)

    nc = build(stage=_stage)
    shared = {"wq": wq, "wk": wk, "wv": wv, "bq_in": bq, "bk_in": bk, "bv_in": bv,
              "g4": g4, "u4": u4, "d4": d4, "gsc_in": gsc, "usc_in": usc,
              "dsc_in": dsc, "idf_in": idf, "idb_in": idb, "onef_in": onef,
              "oneb_in": oneb, "mrow_in": mrow}
    in_maps = [dict(shared, x_in=np.ascontiguousarray(x[c * TPC:(c + 1) * TPC]))
               for c in range(8)]
    res = run_bass_kernel_spmd(nc, in_maps, core_ids=list(range(8)))
    full = np.concatenate([np.asarray(r["out"], np.float32) for r in res.results], axis=0)
    return full.reshape(4, 4096, D)


# version 04
# speedup vs baseline: 1.0001x; 1.0001x over previous
import contextlib
import numpy as np
import ml_dtypes
import concourse.bass as bass
import concourse.mybir as mybir
from concourse.bass_utils import run_bass_kernel_spmd

F32 = mybir.dt.float32
BF16 = mybir.dt.bfloat16
U8 = mybir.dt.uint8
AF = mybir.ActivationFunctionType
ALU = mybir.AluOpType

D = 2048
I = 8192
TPC = 2048
CH = 512
NCH = TPC // CH
NHT = D // 128
NIT = I // 128
HQ, HKV, HD = 16, 4, 128
SEQ = 4096
SCALE = 1.0 / float(np.sqrt(128.0))
EXP_OFF = 60.0

OFF_XCH = 0x04000
OFF_XN = 0x0C000
OFF_HB = 0x10000
OFF_KT = 0x10000
OFF_VT = 0x18000
OFF_XSTG = 0x10000
OFF_GW0 = 0x20000
OFF_GW1 = 0x22000
OFF_UW0 = 0x24000
OFF_UW1 = 0x26000
OFF_DW0 = 0x28000
OFF_DW1 = 0x2C000
OFF_PT = 0x20000
OFF_VTMP = 0x28000
OFF_QSQ = 0x28000
OFF_WQ0 = 0x30000
OFF_WQ1 = 0x31000
OFF_OSTG = 0x30000
OFF_IDF = 0x32000
OFF_IDB = 0x32200
OFF_ONEF = 0x32300
OFF_ONEB = 0x32500
OFF_MROW = 0x32600
OFF_BQ = 0x32700
OFF_BQS = 0x32740
OFF_BK = 0x32780
OFF_BV = 0x327A0
OFF_GSC = 0x327C0
OFF_USC = 0x328C0
OFF_DSC = 0x329C0
OFF_DSCH = 0x32A00
OFF_BEXP = 0x32A40
OFF_ROWSB = 0x32A80
OFF_ROWS = 0x33000
OFF_SG0 = 0x34800
OFF_SG1 = 0x35000
OFF_UU0 = 0x35800
OFF_UU1 = 0x36000
OFF_PST0 = 0x36800
OFF_PST1 = 0x36C00
OFF_SQ0 = 0x37000
OFF_SQ1 = 0x37400

SEM_NAMES = [
    "d_x", "d_c", "d_pk", "d_out", "d_kv",
    "d_q0", "d_q1", "d_gu0", "d_gu1", "d_dw0", "d_dw1", "d_wq0", "d_wq1",
    "d_qh0", "d_qh1", "d_x10", "d_x11",
    "s_cc", "c_pe", "c_act", "c_dve",
]


def build(stage=3):
    eff = 2 if stage == 12 else stage
    nc = bass.Bass("TRN2", target_bir_lowering=False)

    x_in = nc.dram_tensor("x_in", [TPC, D], F32, kind="ExternalInput").ap()
    wq = nc.dram_tensor("wq", [D, D], F32, kind="ExternalInput").ap()
    wk = nc.dram_tensor("wk", [D, 512], F32, kind="ExternalInput").ap()
    wv = nc.dram_tensor("wv", [D, 512], F32, kind="ExternalInput").ap()
    bq_in = nc.dram_tensor("bq_in", [128, 16], F32, kind="ExternalInput").ap()
    bk_in = nc.dram_tensor("bk_in", [128, 4], F32, kind="ExternalInput").ap()
    bv_in = nc.dram_tensor("bv_in", [128, 4], F32, kind="ExternalInput").ap()
    g4 = nc.dram_tensor("g4", [32, 128, 16, 256], U8, kind="ExternalInput").ap()
    u4 = nc.dram_tensor("u4", [32, 128, 16, 256], U8, kind="ExternalInput").ap()
    d4 = nc.dram_tensor("d4", [16, 128, 64, 128], U8, kind="ExternalInput").ap()
    gsc_in = nc.dram_tensor("gsc_in", [128, 64], F32, kind="ExternalInput").ap()
    usc_in = nc.dram_tensor("usc_in", [128, 64], F32, kind="ExternalInput").ap()
    dsc_in = nc.dram_tensor("dsc_in", [128, 16], F32, kind="ExternalInput").ap()
    idf_in = nc.dram_tensor("idf_in", [128, 128], F32, kind="ExternalInput").ap()
    idb_in = nc.dram_tensor("idb_in", [128, 128], BF16, kind="ExternalInput").ap()
    onef_in = nc.dram_tensor("onef_in", [128, 128], F32, kind="ExternalInput").ap()
    oneb_in = nc.dram_tensor("oneb_in", [128, 128], BF16, kind="ExternalInput").ap()
    mrow_in = nc.dram_tensor("mrow_in", [1, 128], BF16, kind="ExternalInput").ap()
    out = nc.dram_tensor("out", [TPC, D], F32, kind="ExternalOutput").ap()
    if stage == 11:
        dbg_xn = nc.dram_tensor("dbg_xn", [128, NHT * CH], BF16, kind="ExternalOutput").ap()
        dbg_hb = nc.dram_tensor("dbg_hb", [128, NIT * CH], BF16, kind="ExternalOutput").ap()
        dbg_x = nc.dram_tensor("dbg_x", [128, NHT * CH], F32, kind="ExternalOutput").ap()
    if stage == 12:
        dbg_q = nc.dram_tensor("dbg_q", [D, TPC], BF16, kind="ExternalOutput").ap()
        dbg_kv = nc.dram_tensor("dbg_kv", [2048, TPC], BF16, kind="ExternalOutput").ap()
        dbg_bexp = nc.dram_tensor("dbg_bexp", [128, 16], F32, kind="ExternalOutput").ap()
        dbg_xp = nc.dram_tensor("dbg_xp", [D, TPC], F32, kind="ExternalOutput").ap()
    xp = nc.dram_tensor("xp", [D, TPC], F32, kind="Internal").ap()
    xp2 = nc.dram_tensor("xp2", [D, TPC], F32, kind="Internal").ap()
    qpark = nc.dram_tensor("qpark", [D, TPC], BF16, kind="Internal").ap()
    kv_send = nc.dram_tensor("kv_send", [1024, TPC], BF16, kind="Internal").ap()
    kv_recv = nc.dram_tensor("kv_recv", [2048, TPC], BF16, kind="Internal").ap()

    SH = 0x80
    _MAP_END = 0x37800 + SH
    nc.alloc_sbuf_tensor("mapspacer", [128, _MAP_END - nc.sbuf_base], U8)
    assert nc.sbuf_base >= _MAP_END, hex(nc.sbuf_base)

    def A(name, shape, dtype, offset):
        return nc.alloc_sbuf_tensor_at(name, shape, dtype, offset=offset + SH)

    xch = A("xch", [128, NHT * CH], F32, offset=OFF_XCH)
    xn = A("xn", [128, NHT * CH], BF16, offset=OFF_XN)
    hb = A("hb", [128, NIT * CH], BF16, offset=OFF_HB)
    kT = A("kTs", [128, 4 * SEQ], BF16, offset=OFF_KT)
    v_t = A("v_ts", [128, 4 * SEQ], BF16, offset=OFF_VT)
    xstg = A("xstg", [128, 4 * D], F32, offset=OFF_XSTG)
    gw = [A("gw0", [128, 4096], BF16, offset=OFF_GW0),
          A("gw1", [128, 4096], BF16, offset=OFF_GW1)]
    uw = [A("uw0", [128, 4096], BF16, offset=OFF_UW0),
          A("uw1", [128, 4096], BF16, offset=OFF_UW1)]
    dw = [A("dw0", [128, 8192], BF16, offset=OFF_DW0),
          A("dw1", [128, 8192], BF16, offset=OFF_DW1)]
    pT = A("pTs", [128, 32 * CH], BF16, offset=OFF_PT)
    vtmp = A("vtmp", [128, 4 * SEQ], BF16, offset=OFF_VTMP)
    qsq = A("qsq", [128, SEQ], BF16, offset=OFF_QSQ)
    wqb = [A("wqb0", [128, 2048], BF16, offset=OFF_WQ0),
           A("wqb1", [128, 2048], BF16, offset=OFF_WQ1)]
    qh = wqb
    ostg = A("ostg", [128, D], F32, offset=OFF_OSTG)
    idf = A("idf", [128, 128], F32, offset=OFF_IDF)
    idb = A("idb", [128, 128], BF16, offset=OFF_IDB)
    onef = A("onef", [128, 128], F32, offset=OFF_ONEF)
    oneb = A("oneb", [128, 128], BF16, offset=OFF_ONEB)
    mrow = A("mrow", [1, 128], BF16, offset=OFF_MROW)
    bq_s = A("bq_s", [128, 16], F32, offset=OFF_BQ)
    bqs = A("bqs", [128, 16], F32, offset=OFF_BQS)
    bk_s = A("bk_s", [128, 4], F32, offset=OFF_BK)
    bv_s = A("bv_s", [128, 4], F32, offset=OFF_BV)
    gsc = A("gsc", [128, 64], F32, offset=OFF_GSC)
    usc = A("usc", [128, 64], F32, offset=OFF_USC)
    dsc = A("dsc", [128, 16], F32, offset=OFF_DSC)
    dsch = A("dsch", [128, 16], F32, offset=OFF_DSCH)
    bexp = A("bexp", [128, 16], F32, offset=OFF_BEXP)
    rowsb = A("rowsb", [128, 512], BF16, offset=OFF_ROWSB)
    rows = A("rows", [128, 1536], F32, offset=OFF_ROWS)
    sg = [A("sg0", [128, CH], F32, offset=OFF_SG0),
          A("sg1", [128, CH], F32, offset=OFF_SG1)]
    uu = [A("uu0", [128, CH], F32, offset=OFF_UU0),
          A("uu1", [128, CH], F32, offset=OFF_UU1)]
    pst = [A("pst0", [128, CH], BF16, offset=OFF_PST0),
           A("pst1", [128, CH], BF16, offset=OFF_PST1)]
    sqst = [A("sq0", [128, CH], BF16, offset=OFF_SQ0),
            A("sq1", [128, CH], BF16, offset=OFF_SQ1)]

    es = contextlib.ExitStack()
    psum = [es.enter_context(nc.psum_tensor(f"pb{i}", [128, 512], F32)) for i in range(7)]
    psb = es.enter_context(nc.psum_tensor("pb7", [128, 1024], BF16))
    SEM = {n: es.enter_context(nc.semaphore(n)) for n in SEM_NAMES}
    CNT = {n: 0 for n in SEM_NAMES}
    LASTW = {}
    BANKR = [{"c_act": 0, "c_dve": 0} for _ in range(8)]

    def inc(ins, sem, k=1):
        ins.then_inc(SEM[sem], k)
        CNT[sem] += k
        return CNT[sem]

    def wt(eng, sem, val):
        if val is None or val <= 0:
            return
        key = (id(eng), sem)
        if LASTW.get(key, -1) < val:
            eng.wait_ge(SEM[sem], val)
            LASTW[key] = val

    def bank_guard(tensor, b):
        wt(tensor, "c_act", BANKR[b]["c_act"])
        wt(tensor, "c_dve", BANKR[b]["c_dve"])

    def read_bank(b, sem, tag):
        BANKR[b][sem] = max(BANKR[b][sem], tag)

    def xch_t(ht):
        return xch[:, ht * CH:(ht + 1) * CH]

    def xn_t(ht):
        return xn[:, ht * CH:(ht + 1) * CH]

    def hb_t(it):
        return hb[:, it * CH:(it + 1) * CH]

    def row(i):
        return rows[0:1, i * 512:(i + 1) * 512]

    def rcell(c0, n=1):
        return rows[0:1, c0:c0 + n]

    blockctx = nc.Block()
    block = blockctx.__enter__()
    state = {"xn_ready": 0, "ffn_done_dve": 0, "wq_free_pe": 0,
             "gu_prev_free": 0, "x2_store_tag": 0, "scores_done_pe": 0}

    def ph_consts():
        def _(sync):
            for dst, src in [(idf, idf_in), (idb, idb_in), (onef, onef_in),
                             (oneb, oneb_in), (bq_s, bq_in), (bk_s, bk_in),
                             (bv_s, bv_in), (gsc, gsc_in), (usc, usc_in),
                             (dsc, dsc_in)]:
                inc(sync.dma_start(out=dst[:, :], in_=src), "d_c", 16)
            inc(sync.dma_start(out=mrow[0:1, :], in_=mrow_in), "d_c", 16)
        block.sync(_)
        state["consts"] = CNT["d_c"]

        def _(vector):
            wt(vector, "d_c", state["consts"])
            inc(vector.tensor_scalar_mul(bqs[:, :], bq_s[:, :], SCALE), "c_dve")
            inc(vector.tensor_scalar_mul(dsch[:, :], dsc[:, :], 0.5), "c_dve")
        block.vector(_)
        state["consts_dve"] = CNT["c_dve"]

    def ph_ln(x_ready):
        A0 = CNT["c_act"]
        P0 = CNT["c_pe"]

        def _(scalar):
            for ht in range(NHT):
                s, v = x_ready(ht)
                wt(scalar, s, v)
                if ht >= 2:
                    wt(scalar, "c_pe", P0 + 2 * (ht - 2) + 2)
                inc(scalar.activation(out=xn_t(ht), in_=xch_t(ht), func=AF.Copy), "c_act")
                inc(scalar.activation(out=sqst[ht % 2][:, :], in_=xch_t(ht),
                                      func=AF.Square), "c_act")
        block.scalar(_)

        def _(tensor):
            wt(tensor, "d_c", state["consts"])
            s, v = x_ready(NHT - 1)
            wt(tensor, s, v)
            bank_guard(tensor, 6)
            bank_guard(tensor, 4)
            for ht in range(NHT):
                wt(tensor, "c_act", A0 + 2 * ht + 1)
                inc(tensor.matmul(psum[6][0:1, :], oneb[:, 0:1], xn_t(ht),
                                  start=(ht == 0), stop=(ht == NHT - 1)), "c_pe")
                wt(tensor, "c_act", A0 + 2 * ht + 2)
                inc(tensor.matmul(psum[4][0:1, :], oneb[:, 0:1], sqst[ht % 2][:, :],
                                  start=(ht == 0), stop=(ht == NHT - 1)), "c_pe")
        block.tensor(_)
        pe_stats = CNT["c_pe"]
        assert pe_stats == P0 + 2 * NHT

        def _(vector):
            wt(vector, "c_pe", pe_stats)
            if stage == 11:
                inc(vector.tensor_copy(out=ostg[0:1, 0:512], in_=psum[6][0:1, :]), "c_dve")
                inc(vector.tensor_copy(out=ostg[0:1, 512:1024], in_=psum[4][0:1, :]), "c_dve")
            t1 = inc(vector.tensor_scalar_mul(row(0), psum[6][0:1, :], 1.0 / D), "c_dve")
            t2 = inc(vector.tensor_scalar_mul(row(1), psum[4][0:1, :], 1.0 / D), "c_dve")
            wt(vector, "c_dve", t1)
            t3 = inc(vector.tensor_mul(row(2), row(0), row(0)), "c_dve")
            wt(vector, "c_dve", t3)
            inc(vector.scalar_tensor_tensor(out=row(1), in0=row(1), scalar=1e-5,
                                            in1=row(2), op0=ALU.add, op1=ALU.subtract),
                "c_dve")
        block.vector(_)
        dve_var = CNT["c_dve"]
        read_bank(6, "c_dve", dve_var)
        read_bank(4, "c_dve", dve_var)
        if stage == 11:
            def _(vector):
                wt(vector, "c_dve", dve_var)
                inc(vector.tensor_copy(out=ostg[0:1, 1024:1536], in_=row(1)), "c_dve")
            block.vector(_)
            dve_var = CNT["c_dve"]

        def _(scalar):
            wt(scalar, "c_dve", dve_var)
            a1 = inc(scalar.activation(out=row(2), in_=row(1), func=AF.Ln), "c_act")
            wt(scalar, "c_act", a1)
            inc(scalar.activation(out=row(1), in_=row(2), func=AF.Exp, scale=-0.5), "c_act")
        block.scalar(_)
        if stage == 11:
            def _(vector):
                wt(vector, "c_act", CNT["c_act"])
                inc(vector.tensor_copy(out=sg[1][0:1, :], in_=row(2)), "c_dve")
            block.vector(_)
        act_rstd = CNT["c_act"]
        if stage == 11:
            def _(vector):
                wt(vector, "c_act", act_rstd)
                inc(vector.tensor_copy(out=ostg[0:1, 1536:2048], in_=row(1)), "c_dve")
            block.vector(_)

        def _(vector):
            wt(vector, "c_act", act_rstd)
            inc(vector.scalar_tensor_tensor(out=row(2), in0=row(0), scalar=-1.0,
                                            in1=row(1), op0=ALU.mult, op1=ALU.mult), "c_dve")
        block.vector(_)
        dve_nmr = CNT["c_dve"]

        def _(tensor):
            wt(tensor, "c_dve", dve_nmr)
            bank_guard(tensor, 6)
            bank_guard(tensor, 4)
            inc(tensor.matmul(psum[6][:, :], onef[0:1, :], row(1), start=True, stop=True), "c_pe")
            inc(tensor.matmul(psum[4][:, :], onef[0:1, :], row(2), start=True, stop=True), "c_pe")
        block.tensor(_)
        pe_bc = CNT["c_pe"]

        def _(vector):
            wt(vector, "c_pe", pe_bc)
            prev_add = 0
            for ht in range(NHT):
                wt(vector, "c_dve", prev_add)
                t = inc(vector.tensor_mul(sg[0][:, :], xch_t(ht), psum[6][:, :]), "c_dve")
                wt(vector, "c_dve", t)
                prev_add = inc(vector.tensor_add(xn_t(ht), sg[0][:, :], psum[4][:, :]), "c_dve")
        block.vector(_)
        state["xn_ready"] = CNT["c_dve"]
        read_bank(6, "c_dve", state["xn_ready"])
        read_bank(4, "c_dve", state["xn_ready"])

    def ph_ffn(warena_free_pe):
        NB = 32
        P0 = CNT["c_pe"]
        A0 = CNT["c_act"]
        V0 = CNT["c_dve"]
        gu_base = [CNT["d_gu0"], CNT["d_gu1"]]

        def tag_g(it):
            return P0 + 2 * it + 1

        def tag_u(it):
            return P0 + 2 * it + 2

        def tag_silu(it):
            return A0 + it + 1

        def tag_uu(it):
            return V0 + 2 * it + 1

        def tag_h(it):
            return V0 + 2 * it + 2

        def _(gpsimd):
            for ib in range(NB):
                if ib >= 2:
                    wt(gpsimd, "c_pe", tag_u(2 * ib - 3))
                else:
                    wt(gpsimd, "c_pe", warena_free_pe)
                sem = "d_gu0" if ib % 2 == 0 else "d_gu1"
                inc(gpsimd.dma_start(
                    out=gw[ib % 2][:].rearrange("p (n c) -> p n c", c=256), in_=g4[ib]), sem, 16)
                inc(gpsimd.dma_start(
                    out=uw[ib % 2][:].rearrange("p (n c) -> p n c", c=256), in_=u4[ib]), sem, 16)
        block.gpsimd(_)

        def _(tensor):
            wt(tensor, "c_dve", state["xn_ready"])
            for it in range(2 * NB):
                ib = it // 2
                sem = "d_gu0" if ib % 2 == 0 else "d_gu1"
                wt(tensor, sem, gu_base[ib % 2] + 32 * (ib // 2 + 1))
                if it >= 2:
                    wt(tensor, "c_act", tag_silu(it - 2))
                    wt(tensor, "c_dve", tag_uu(it - 2))
                else:
                    bank_guard(tensor, it % 2)
                    bank_guard(tensor, 2 + it % 2)
                sl = (it % 2) * 128
                for ht in range(NHT):
                    ins = tensor.matmul(
                        psum[it % 2][:, :],
                        gw[ib % 2][:, ht * 256 + sl:ht * 256 + sl + 128],
                        xn_t(ht), start=(ht == 0), stop=(ht == NHT - 1))
                inc(ins, "c_pe")
                for ht in range(NHT):
                    ins = tensor.matmul(
                        psum[2 + it % 2][:, :],
                        uw[ib % 2][:, ht * 256 + sl:ht * 256 + sl + 128],
                        xn_t(ht), start=(ht == 0), stop=(ht == NHT - 1))
                inc(ins, "c_pe")
        block.tensor(_)
        assert CNT["c_pe"] == P0 + 4 * NB

        def _(scalar):
            wt(scalar, "d_c", state["consts"])
            for it in range(2 * NB):
                wt(scalar, "c_pe", tag_g(it))
                if it >= 2:
                    wt(scalar, "c_dve", tag_h(it - 2))
                inc(scalar.activation(out=sg[it % 2][:, :], in_=psum[it % 2][:, :],
                                      func=AF.Silu, scale=gsc[:, it:it + 1]), "c_act")
        block.scalar(_)
        assert CNT["c_act"] == A0 + 2 * NB

        def _(vector):
            wt(vector, "d_c", state["consts"])
            for it in range(2 * NB):
                wt(vector, "c_pe", tag_u(it))
                if it >= 2:
                    wt(vector, "c_dve", tag_h(it - 2))
                t = inc(vector.tensor_scalar_mul(uu[it % 2][:, :], psum[2 + it % 2][:, :],
                                                 usc[:, it:it + 1]), "c_dve")
                wt(vector, "c_act", tag_silu(it))
                wt(vector, "c_dve", t)
                inc(vector.tensor_mul(hb_t(it), sg[it % 2][:, :], uu[it % 2][:, :]), "c_dve")
        block.vector(_)
        assert CNT["c_dve"] == V0 + 4 * NB
        for b in range(4):
            read_bank(b, "c_act", A0 + 2 * NB)
            read_bank(b, "c_dve", V0 + 4 * NB)

        def _(tensor):
            bank_guard(tensor, 6)
            for it in range(2 * NB):
                wt(tensor, "c_dve", tag_h(it))
                inc(tensor.matmul(psum[6][0:1, :], oneb[:, 0:1], hb_t(it),
                                  start=(it == 0), stop=(it == 2 * NB - 1)), "c_pe")
        block.tensor(_)
        pe_csh = CNT["c_pe"]

        def _(vector):
            wt(vector, "c_pe", pe_csh)
            t1 = inc(vector.tensor_copy(out=row(0), in_=psum[6][0:1, :]), "c_dve")
            wt(vector, "c_dve", t1)
            t2 = inc(vector.tensor_copy(out=rowsb[0:1, :], in_=row(0)), "c_dve")
            wt(vector, "c_dve", t2)
            inc(vector.tensor_sub(sqst[0][0:1, :], row(0), rowsb[0:1, :]), "c_dve")
        block.vector(_)
        csh_ready = CNT["c_dve"]
        read_bank(6, "c_dve", csh_ready)

        Pd0 = CNT["c_pe"]
        Vd0 = CNT["c_dve"]
        dw_base = [CNT["d_dw0"], CNT["d_dw1"]]

        def tag_d(hht):
            return Pd0 + hht + 1

        def tag_stt(hht):
            return Vd0 + hht + 1

        def _(gpsimd):
            for hht in range(NHT):
                if hht >= 2:
                    wt(gpsimd, "c_pe", tag_d(hht - 2))
                sem = "d_dw0" if hht % 2 == 0 else "d_dw1"
                inc(gpsimd.dma_start(
                    out=dw[hht % 2][:].rearrange("p (n c) -> p n c", c=128), in_=d4[hht]), sem, 16)
        block.gpsimd(_)

        def _(tensor):
            wt(tensor, "c_dve", csh_ready)
            bank_guard(tensor, 4)
            bank_guard(tensor, 5)
            for hht in range(NHT):
                sem = "d_dw0" if hht % 2 == 0 else "d_dw1"
                wt(tensor, sem, dw_base[hht % 2] + 16 * (hht // 2 + 1))
                if hht >= 2:
                    wt(tensor, "c_dve", tag_stt(hht - 2))
                tensor.matmul(psum[4 + hht % 2][:, :], mrow[0:1, :], rowsb[0:1, :],
                              start=True, stop=False)
                tensor.matmul(psum[4 + hht % 2][:, :], mrow[0:1, :], sqst[0][0:1, :],
                              start=False, stop=False)
                for it in range(NIT):
                    ins = tensor.matmul(
                        psum[4 + hht % 2][:, :],
                        dw[hht % 2][:, it * 128:(it + 1) * 128],
                        hb_t(it), start=False, stop=(it == NIT - 1))
                inc(ins, "c_pe")
        block.tensor(_)
        assert CNT["c_pe"] == Pd0 + NHT

        def _(vector):
            for hht in range(NHT):
                wt(vector, "c_pe", tag_d(hht))
                inc(vector.scalar_tensor_tensor(
                    out=xch_t(hht), in0=psum[4 + hht % 2][:, :], scalar=dsch[:, hht:hht + 1],
                    in1=xch_t(hht), op0=ALU.mult, op1=ALU.add), "c_dve")
        block.vector(_)
        assert CNT["c_dve"] == Vd0 + NHT
        state["ffn_done_dve"] = CNT["c_dve"]
        read_bank(4, "c_dve", Vd0 + NHT)
        read_bank(5, "c_dve", Vd0 + NHT)
        state["gu_free_pe"] = tag_u(2 * NB - 1)
        state["dw_free_pe"] = CNT["c_pe"]

    def ph_load_x_transposed(ch, xch_free_pe, park_tag):
        def _(sync):
            wt(sync, "c_pe", xch_free_pe)
            wt(sync, "d_pk", park_tag)
            for tt in range(4):
                inc(sync.dma_start(out=xstg[:, tt * D:(tt + 1) * D],
                                   in_=x_in[ch * CH + tt * 128: ch * CH + (tt + 1) * 128, :]),
                    "d_x", 16)
        block.sync(_)
        xload = CNT["d_x"]
        P0 = CNT["c_pe"]
        V0 = CNT["c_dve"]

        def _(tensor):
            wt(tensor, "d_c", state["consts"])
            wt(tensor, "d_x", xload)
            bank_guard(tensor, 4)
            bank_guard(tensor, 5)
            for ht in range(NHT):
                if ht >= 2:
                    wt(tensor, "c_dve", V0 + ht - 1)
                for tt in range(4):
                    ins = tensor.transpose(
                        psum[4 + ht % 2][:, tt * 128:(tt + 1) * 128],
                        xstg[:, tt * D + ht * 128: tt * D + (ht + 1) * 128],
                        idf[:, :])
                inc(ins, "c_pe")
        block.tensor(_)

        def _(vector):
            for ht in range(NHT):
                wt(vector, "c_pe", P0 + ht + 1)
                inc(vector.tensor_copy(out=xch_t(ht), in_=psum[4 + ht % 2][:, :]), "c_dve")
        block.vector(_)
        read_bank(4, "c_dve", V0 + NHT)
        read_bank(5, "c_dve", V0 + NHT)
        return lambda ht: ("c_dve", V0 + ht + 1)

    def ph_load_x_hmajor(ch, src, xch_free_pe):
        def _(sync):
            wt(sync, "c_pe", xch_free_pe)
            wt(sync, "d_pk", state["att_done_pk"])
            inc(sync.dma_start(
                out=xch[:].rearrange("p (n c) -> p n c", c=CH),
                in_=src.rearrange("(n p) t -> p n t", p=128)[:, :, ch * CH:(ch + 1) * CH]),
                "d_x", 16)
        block.sync(_)
        xload = CNT["d_x"]
        return lambda ht: ("d_x", xload)

    def ph_proj(ch):
        NS = 24
        P0 = CNT["c_pe"]
        wq_base = [CNT["d_wq0"], CNT["d_wq1"]]
        q_base = [CNT["d_q0"], CNT["d_q1"]]

        def tag_p(s):
            return P0 + s + 1

        def _(gpsimd):
            for s in range(NS):
                if s >= 2:
                    wt(gpsimd, "c_pe", tag_p(s - 2))
                else:
                    wt(gpsimd, "c_pe", state["wq_free_pe"])
                sem = "d_wq0" if s % 2 == 0 else "d_wq1"
                if s < 16:
                    src = wq.rearrange("(n p) c -> p n c", p=128)[:, :, s * 128:(s + 1) * 128]
                elif s < 20:
                    dt = s - 16
                    src = wk.rearrange("(n p) c -> p n c", p=128)[:, :, dt * 128:(dt + 1) * 128]
                else:
                    dt = s - 20
                    src = wv.rearrange("(n p) c -> p n c", p=128)[:, :, dt * 128:(dt + 1) * 128]
                inc(gpsimd.dma_start(
                    out=wqb[s % 2][:].rearrange("p (n c) -> p n c", c=128), in_=src), sem, 16)
        block.gpsimd(_)

        V0 = CNT["c_dve"]

        def tag_e(s):
            return V0 + s + 1

        def _(tensor):
            wt(tensor, "c_dve", state["xn_ready"])
            for s in range(NS):
                sem = "d_wq0" if s % 2 == 0 else "d_wq1"
                wt(tensor, sem, wq_base[s % 2] + 16 * (s // 2 + 1))
                if s >= 2:
                    wt(tensor, "c_dve", tag_e(s - 2))
                else:
                    bank_guard(tensor, s % 2)
                for ht in range(NHT):
                    ins = tensor.matmul(
                        psum[s % 2][:, :],
                        wqb[s % 2][:, ht * 128:(ht + 1) * 128],
                        xn_t(ht), start=(ht == 0), stop=(ht == NHT - 1))
                inc(ins, "c_pe")
        block.tensor(_)

        def tag_ev(s):
            return tag_e(s)

        def _(vector):
            wt(vector, "c_dve", state["consts_dve"])
            for s in range(NS):
                wt(vector, "c_pe", tag_p(s))
                semd = "d_q0" if s % 2 == 0 else "d_q1"
                wt(vector, semd, q_base[s % 2] + 16 * (s // 2))
                if s < 16:
                    inc(vector.tensor_scalar(out=pst[s % 2][:, :], in0=psum[s % 2][:, :],
                                             scalar1=SCALE, scalar2=bqs[:, s:s + 1],
                                             op0=ALU.mult, op1=ALU.add), "c_dve")
                elif s < 20:
                    inc(vector.tensor_scalar_add(pst[s % 2][:, :], psum[s % 2][:, :],
                                                 bk_s[:, s - 16:s - 15]), "c_dve")
                else:
                    inc(vector.tensor_scalar_add(pst[s % 2][:, :], psum[s % 2][:, :],
                                                 bv_s[:, s - 20:s - 19]), "c_dve")
        block.vector(_)

        def _(scalar):
            for s in range(NS):
                wt(scalar, "c_dve", tag_ev(s))
                sem = "d_q0" if s % 2 == 0 else "d_q1"
                if s < 16:
                    dst = qpark[s * 128:(s + 1) * 128, ch * CH:(ch + 1) * CH]
                elif s < 20:
                    dst = kv_send[(s - 16) * 128:(s - 15) * 128, ch * CH:(ch + 1) * CH]
                else:
                    dst = kv_send[512 + (s - 20) * 128:512 + (s - 19) * 128,
                                  ch * CH:(ch + 1) * CH]
                inc(scalar.dma_start(out=dst, in_=pst[s % 2][:, :]), sem, 16)
        block.scalar(_)
        read_bank(0, "c_dve", tag_ev(NS - 2))
        read_bank(1, "c_dve", tag_ev(NS - 1))
        state["wq_free_pe"] = tag_p(NS - 1)
        state["proj_last_pe"] = tag_p(NS - 1)

    def ph_park(ch):
        def _(sync):
            wt(sync, "c_dve", state["ffn_done_dve"])
            inc(sync.dma_start(
                out=xp.rearrange("(n p) t -> p n t", p=128)[:, :, ch * CH:(ch + 1) * CH],
                in_=xch[:].rearrange("p (n c) -> p n c", c=CH)), "d_pk", 16)
        block.sync(_)
        return CNT["d_pk"]

    def ph_out(ch):
        P0 = CNT["c_pe"]
        V0 = CNT["c_dve"]
        O0 = CNT["d_out"]

        def _(tensor):
            wt(tensor, "d_c", state["consts"])
            wt(tensor, "c_dve", state["ffn_done_dve"])
            bank_guard(tensor, 4)
            bank_guard(tensor, 5)
            gi = 0
            for tt in range(4):
                for grp in range(4):
                    if gi >= 2:
                        wt(tensor, "c_dve", V0 + gi - 1)
                    bank = 4 + gi % 2
                    for j in range(4):
                        ht = grp * 4 + j
                        ins = tensor.transpose(
                            psum[bank][:, j * 128:(j + 1) * 128],
                            xch_t(ht)[:, tt * 128:(tt + 1) * 128],
                            idf[:, :])
                    inc(ins, "c_pe")
                    gi += 1
        block.tensor(_)

        def _(vector):
            gi = 0
            for tt in range(4):
                wt(vector, "d_out", O0 + 16 * tt)
                for grp in range(4):
                    wt(vector, "c_pe", P0 + gi + 1)
                    inc(vector.tensor_copy(out=ostg[:, grp * 512:(grp + 1) * 512],
                                           in_=psum[4 + gi % 2][:, :]), "c_dve")
                    gi += 1
        block.vector(_)

        def _(sync):
            for tt in range(4):
                wt(sync, "c_dve", V0 + 4 * tt + 4)
                inc(sync.dma_start(out=out[ch * CH + tt * 128: ch * CH + (tt + 1) * 128, :],
                                   in_=ostg[:, :]), "d_out", 16)
        block.sync(_)
        read_bank(4, "c_dve", V0 + 16)
        read_bank(5, "c_dve", V0 + 16)
        state["out_pe"] = CNT["c_pe"]
        state["out_last_dve"] = CNT["c_dve"]

    def ph_attention():
        def _(gpsimd):
            wt(gpsimd, "d_q0", CNT["d_q0"])
            wt(gpsimd, "d_q1", CNT["d_q1"])
            inc(gpsimd.collective_compute(
                "AllGather", ALU.bypass,
                replica_groups=[[0, 1], [2, 3], [4, 5], [6, 7]],
                ins=[kv_send[:, :]], outs=[kv_recv[:, :]]), "s_cc", 1)
        block.gpsimd(_)

        qh_base = [CNT["d_qh0"], CNT["d_qh1"]]

        def _(sync):
            wt(sync, "s_cc", CNT["s_cc"])
            inc(sync.dma_start(
                out=kT[:].rearrange("p (g t) -> p g t", t=SEQ)[:, :, 0:TPC],
                in_=kv_recv[0:512, :].rearrange("(g p) t -> p g t", p=128)), "d_kv", 16)
            inc(sync.dma_start(
                out=kT[:].rearrange("p (g t) -> p g t", t=SEQ)[:, :, TPC:SEQ],
                in_=kv_recv[1024:1536, :].rearrange("(g p) t -> p g t", p=128)), "d_kv", 16)
            inc(sync.dma_start(
                out=vtmp[:].rearrange("p (g t) -> p g t", t=SEQ)[:, :, 0:TPC],
                in_=kv_recv[512:1024, :].rearrange("(g p) t -> p g t", p=128)), "d_kv", 16)
            inc(sync.dma_start(
                out=vtmp[:].rearrange("p (g t) -> p g t", t=SEQ)[:, :, TPC:SEQ],
                in_=kv_recv[1536:2048, :].rearrange("(g p) t -> p g t", p=128)), "d_kv", 16)
            inc(sync.dma_start(out=qh[0][:, :], in_=qpark[0:128, :]), "d_qh0", 16)
        block.sync(_)
        kvload = CNT["d_kv"]

        P0 = CNT["c_pe"]
        V0 = CNT["c_dve"]

        def _(tensor):
            wt(tensor, "d_c", state["consts"])
            wt(tensor, "d_kv", kvload)
            for grp in range(16):
                if grp >= 1:
                    wt(tensor, "c_dve", V0 + 8 * grp)
                for j in range(8):
                    tile = grp * 8 + j
                    g, tkt = tile // 32, tile % 32
                    ins = tensor.transpose(
                        psb[:, j * 128:(j + 1) * 128],
                        vtmp[:, g * SEQ + tkt * 128: g * SEQ + (tkt + 1) * 128],
                        idb[:, :])
                inc(ins, "c_pe")
        block.tensor(_)

        def _(vector):
            for grp in range(16):
                wt(vector, "c_pe", P0 + grp + 1)
                for j in range(8):
                    tile = grp * 8 + j
                    g, tkt = tile // 32, tile % 32
                    inc(vector.tensor_copy(
                        out=v_t[:, g * SEQ + tkt * 128:g * SEQ + (tkt + 1) * 128],
                        in_=psb[:, j * 128:(j + 1) * 128]), "c_dve")
        block.vector(_)
        vt_done = CNT["c_dve"]
        read_bank(7, "c_dve", vt_done)

        prev_mm = 0
        for g in range(4):
            Vs = CNT["c_dve"]

            def _(vector, g=g, prev_mm=prev_mm):
                wt(vector, "c_dve", vt_done)
                wt(vector, "c_pe", prev_mm)
                inc(vector.tensor_mul(qsq[:, :], kT[:, g * SEQ:(g + 1) * SEQ],
                                      kT[:, g * SEQ:(g + 1) * SEQ]), "c_dve")
            block.vector(_)
            Pn = CNT["c_pe"]
            Vr0 = CNT["c_dve"]

            def _(tensor, Vs=Vs, Vr0=Vr0):
                wt(tensor, "c_dve", Vs + 1)
                bank_guard(tensor, 6)
                for q4 in range(8):
                    if q4 >= 1:
                        wt(tensor, "c_dve", Vr0 + q4)
                    inc(tensor.matmul(psum[6][0:1, :], oneb[:, 0:1],
                                      qsq[:, q4 * 512:(q4 + 1) * 512],
                                      start=True, stop=True), "c_pe")
            block.tensor(_)
            prev_mm = CNT["c_pe"]

            def _(vector, g=g, Pn=Pn):
                t = 0
                for q4 in range(8):
                    wt(vector, "c_pe", Pn + q4 + 1)
                    t = inc(vector.reduce_max(rcell(1024 + q4), psum[6][0:1, :],
                                              axis=mybir.AxisListType.X), "c_dve")
                wt(vector, "c_dve", t)
                inc(vector.reduce_max(rcell(1045 + g), rows[0:1, 1024:1032],
                                      axis=mybir.AxisListType.X), "c_dve")
            block.vector(_)
            read_bank(6, "c_dve", CNT["c_dve"])
        knorm_done = CNT["c_dve"]

        for hd in range(HQ):
            g = hd % HKV
            par = hd % 2
            semq = "d_qh0" if par == 0 else "d_qh1"
            qtag = qh_base[par] + 16 * (hd // 2 + 1)

            Vq0 = CNT["c_dve"]

            def _(vector, hd=hd, qtag=qtag, semq=semq):
                wt(vector, semq, qtag)
                wt(vector, "c_dve", knorm_done)
                inc(vector.tensor_mul(qsq[:, 0:TPC], qh[hd % 2][:, :], qh[hd % 2][:, :]),
                    "c_dve")
            block.vector(_)
            Pq = CNT["c_pe"]
            Vr0 = CNT["c_dve"]

            def _(tensor, Vq0=Vq0, Vr0=Vr0):
                wt(tensor, "c_dve", Vq0 + 1)
                bank_guard(tensor, 6)
                for q4 in range(4):
                    if q4 >= 1:
                        wt(tensor, "c_dve", Vr0 + q4)
                    inc(tensor.matmul(psum[6][0:1, :], oneb[:, 0:1],
                                      qsq[:, q4 * 512:(q4 + 1) * 512],
                                      start=True, stop=True), "c_pe")
            block.tensor(_)

            def _(vector, g=g, Pq=Pq):
                t = 0
                for q4 in range(4):
                    wt(vector, "c_pe", Pq + q4 + 1)
                    t = inc(vector.reduce_max(rcell(1024 + q4), psum[6][0:1, :],
                                              axis=mybir.AxisListType.X), "c_dve")
                wt(vector, "c_dve", t)
                t = inc(vector.reduce_max(rcell(1040), rows[0:1, 1024:1028],
                                          axis=mybir.AxisListType.X), "c_dve")
                wt(vector, "c_dve", t)
                inc(vector.tensor_mul(rcell(1041), rcell(1040), rcell(1045 + g)), "c_dve")
            block.vector(_)
            Vb = CNT["c_dve"]
            read_bank(6, "c_dve", Vb)

            def _(scalar, Vb=Vb):
                wt(scalar, "c_dve", Vb)
                a = inc(scalar.activation(out=rcell(1042), in_=rcell(1041), func=AF.Ln),
                        "c_act")
                wt(scalar, "c_act", a)
                inc(scalar.activation(out=rcell(1043), in_=rcell(1042), func=AF.Exp,
                                      scale=0.5), "c_act")
            block.scalar(_)
            Ab = CNT["c_act"]

            def _(vector, Ab=Ab):
                wt(vector, "c_act", Ab)
                inc(vector.tensor_scalar(out=rcell(1044), in0=rcell(1043),
                                         scalar1=-1.001, scalar2=EXP_OFF,
                                         op0=ALU.mult, op1=ALU.add), "c_dve")
            block.vector(_)
            Vb2 = CNT["c_dve"]

            def _(tensor, Vb2=Vb2):
                wt(tensor, "c_dve", Vb2)
                bank_guard(tensor, 6)
                inc(tensor.matmul(psum[6][:, 0:1], onef[0:1, :], rcell(1044),
                                  start=True, stop=True), "c_pe")
            block.tensor(_)
            Pb = CNT["c_pe"]

            def _(vector, hd=hd, Pb=Pb):
                wt(vector, "c_pe", Pb)
                inc(vector.tensor_copy(out=bexp[:, hd:hd + 1], in_=psum[6][:, 0:1]), "c_dve")
            block.vector(_)
            bexp_done = CNT["c_dve"]
            read_bank(6, "c_dve", bexp_done)

            if hd + 1 < HQ:
                def _(sync, hd=hd):
                    wt(sync, "c_pe", state["scores_done_pe"])
                    sem2 = "d_qh0" if (hd + 1) % 2 == 0 else "d_qh1"
                    inc(sync.dma_start(out=qh[(hd + 1) % 2][:, :],
                                       in_=qpark[(hd + 1) * 128:(hd + 2) * 128, :]), sem2, 16)
                block.sync(_)

            for blk in range(NCH):
                bpar = (hd * NCH + blk) % 2
                semx = "d_x10" if bpar == 0 else "d_x11"

                def _(sync, hd=hd, blk=blk, bpar=bpar, semx=semx):
                    wt(sync, "c_dve", state.get(f"x1_free_{bpar}", 0))
                    inc(sync.dma_start(
                        out=sg[bpar][:, :],
                        in_=xp[hd * 128:(hd + 1) * 128, blk * CH:(blk + 1) * CH]), semx, 16)
                block.sync(_)
                x1tag = CNT[semx]

                P0s = CNT["c_pe"]
                A0s = CNT["c_act"]

                def _(tensor, hd=hd, g=g, blk=blk, A0s=A0s):
                    bank_guard(tensor, 0)
                    bank_guard(tensor, 1)
                    for tkt in range(32):
                        if tkt >= 2:
                            wt(tensor, "c_act", A0s + tkt - 1)
                        inc(tensor.matmul(
                            psum[tkt % 2][:, :],
                            kT[:, g * SEQ + tkt * 128:g * SEQ + (tkt + 1) * 128],
                            qh[hd % 2][:, blk * CH:(blk + 1) * CH],
                            start=True, stop=True), "c_pe")
                block.tensor(_)
                state["scores_done_pe"] = CNT["c_pe"]

                def _(scalar, hd=hd, P0s=P0s, bexp_done=bexp_done):
                    wt(scalar, "c_dve", bexp_done)
                    for tkt in range(32):
                        wt(scalar, "c_pe", P0s + tkt + 1)
                        inc(scalar.activation(
                            out=pT[:, tkt * CH:(tkt + 1) * CH], in_=psum[tkt % 2][:, :],
                            func=AF.Exp, bias=bexp[:, hd:hd + 1]), "c_act")
                block.scalar(_)
                exp_base = A0s
                read_bank(0, "c_act", exp_base + 31)
                read_bank(1, "c_act", exp_base + 32)

                def _(tensor, g=g, exp_base=exp_base):
                    bank_guard(tensor, 2)
                    bank_guard(tensor, 3)
                    for tkt in range(32):
                        wt(tensor, "c_act", exp_base + tkt + 1)
                        tensor.matmul(psum[2][:, :],
                                      v_t[:, g * SEQ + tkt * 128:g * SEQ + (tkt + 1) * 128],
                                      pT[:, tkt * CH:(tkt + 1) * CH],
                                      start=(tkt == 0), stop=(tkt == 31))
                        ins = tensor.matmul(psum[3][:, :], oneb[:, :],
                                            pT[:, tkt * CH:(tkt + 1) * CH],
                                            start=(tkt == 0), stop=(tkt == 31))
                    inc(ins, "c_pe")
                block.tensor(_)
                Pctx = CNT["c_pe"]

                def _(vector, bpar=bpar, semx=semx, x1tag=x1tag, Pctx=Pctx):
                    wt(vector, "c_pe", Pctx)
                    wt(vector, "d_pk", state["x2_store_tag"])
                    wt(vector, "c_dve", state.get("epi_prev", 0))
                    t = inc(vector.reciprocal(uu[0][:, :], psum[3][:, :]), "c_dve")
                    wt(vector, "c_dve", t)
                    t = inc(vector.tensor_mul(uu[1][:, :], psum[2][:, :], uu[0][:, :]),
                            "c_dve")
                    wt(vector, semx, x1tag)
                    wt(vector, "c_dve", t)
                    inc(vector.tensor_add(uu[0][:, :], uu[1][:, :], sg[bpar][:, :]), "c_dve")
                block.vector(_)
                state[f"x1_free_{bpar}"] = CNT["c_dve"]
                state["epi_prev"] = CNT["c_dve"]
                x2ready = CNT["c_dve"]
                read_bank(2, "c_dve", x2ready)
                read_bank(3, "c_dve", x2ready)

                def _(sync, hd=hd, blk=blk, x2ready=x2ready):
                    wt(sync, "c_dve", x2ready)
                    inc(sync.dma_start(
                        out=xp2[hd * 128:(hd + 1) * 128, blk * CH:(blk + 1) * CH],
                        in_=uu[0][:, :]), "d_pk", 16)
                block.sync(_)
                state["x2_store_tag"] = CNT["d_pk"]

        state["att_done_pk"] = CNT["d_pk"]
        state["att_done_pe"] = CNT["c_pe"]

    ph_consts()
    if stage == 11:
        x_ready = ph_load_x_transposed(0, 0, 0)
        ph_ln(x_ready)
        ph_ffn(0)
        def _(sync):
            wt(sync, "c_dve", state["ffn_done_dve"])
            inc(sync.dma_start(out=dbg_xn, in_=xn[:, :]), "d_out", 16)
            inc(sync.dma_start(out=dbg_hb, in_=hb[:, :]), "d_out", 16)
            inc(sync.dma_start(out=dbg_x, in_=xch[:, :]), "d_out", 16)
            inc(sync.dma_start(out=out[0:128, :], in_=xstg[:, 0:D]), "d_out", 16)
            inc(sync.dma_start(out=out[200:201, 0:2048], in_=ostg[0:1, :]), "d_out", 16)
            inc(sync.dma_start(out=out[210:211, 0:512], in_=sg[1][0:1, :]), "d_out", 16)
            wt(sync, "d_out", CNT["d_out"])
        block.sync(_)
        blockctx.__exit__(None, None, None)
        es.close()
        return nc
    park_tag = 0
    xch_free_pe = 0
    for ch in range(NCH):
        x_ready = ph_load_x_transposed(ch, xch_free_pe, park_tag)
        ph_ln(x_ready)
        ph_ffn(state["gu_prev_free"])
        state["gu_prev_free"] = state["gu_free_pe"]
        if eff >= 2:
            ph_ln(lambda ht: ("c_dve", state["ffn_done_dve"]))
            ph_proj(ch)
            xch_free_pe = state["proj_last_pe"]
        else:
            xch_free_pe = state["dw_free_pe"]
        park_tag = ph_park(ch)
        if eff == 1:
            ph_out(ch)
            xch_free_pe = state["out_pe"]

    if eff >= 2:
        ph_attention()
        state["gu_prev_free"] = state["att_done_pe"]
        for ch in range(NCH):
            x_ready = ph_load_x_hmajor(ch, xp2, xch_free_pe)
            if eff >= 3:
                ph_ln(x_ready)
                ph_ffn(state["gu_prev_free"])
                state["gu_prev_free"] = state["gu_free_pe"]
            else:
                def _(vector, xr=x_ready):
                    s, v = xr(0)
                    wt(vector, s, v)
                    inc(vector.tensor_copy(out=sg[0][:, 0:8], in_=xch[:, 0:8]), "c_dve")
                block.vector(_)
                state["ffn_done_dve"] = CNT["c_dve"]
            ph_out(ch)
            xch_free_pe = state["out_pe"]

    def _(sync):
        if stage == 12:
            wt(sync, "d_pk", state["att_done_pk"])
            inc(sync.dma_start(out=dbg_q, in_=qpark[:, :]), "d_out", 16)
            inc(sync.dma_start(out=dbg_kv, in_=kv_recv[:, :]), "d_out", 16)
            inc(sync.dma_start(out=dbg_xp, in_=xp[:, :]), "d_out", 16)
        wt(sync, "d_out", CNT["d_out"])
        wt(sync, "d_pk", CNT["d_pk"])
    block.sync(_)

    if stage == 12:
        def _(vector):
            inc(vector.tensor_copy(out=sg[0][:, 0:16], in_=bexp[:, :]), "c_dve")
        block.vector(_)

        def _(sync):
            wt(sync, "c_dve", CNT["c_dve"])
            inc(sync.dma_start(out=dbg_bexp, in_=sg[0][:, 0:16]), "d_out", 16)
            wt(sync, "d_out", CNT["d_out"])
        block.sync(_)

    blockctx.__exit__(None, None, None)
    es.close()
    return nc



def _byte0(w):
    a = np.ascontiguousarray(w)
    assert a.dtype == np.int32
    return a.view(np.uint8).reshape(a.shape[0], a.shape[1], 4)[:, :, 0]


def kernel(L0_attn_hidden_states, L0_attn_Wq, L0_attn_bq, L0_attn_Wk, L0_attn_bk,
           L0_attn_Wv, L0_attn_bv, L0_ffn_gate_w, L0_ffn_gate_scale, L0_ffn_up_w,
           L0_ffn_up_scale, L0_ffn_down_w, L0_ffn_down_scale, _stage=3):
    bf = ml_dtypes.bfloat16
    x = np.ascontiguousarray(np.asarray(L0_attn_hidden_states, dtype=np.float32)).reshape(16384, D)
    wq = np.ascontiguousarray(np.asarray(L0_attn_Wq, dtype=np.float32))
    wk = np.ascontiguousarray(np.asarray(L0_attn_Wk, dtype=np.float32))
    wv = np.ascontiguousarray(np.asarray(L0_attn_Wv, dtype=np.float32))
    bq = np.ascontiguousarray(np.asarray(L0_attn_bq, dtype=np.float32).reshape(16, 128).T)
    bk = np.ascontiguousarray(np.asarray(L0_attn_bk, dtype=np.float32).reshape(4, 128).T)
    bv = np.ascontiguousarray(np.asarray(L0_attn_bv, dtype=np.float32).reshape(4, 128).T)
    gT = np.ascontiguousarray(_byte0(np.asarray(L0_ffn_gate_w)).T)
    uT = np.ascontiguousarray(_byte0(np.asarray(L0_ffn_up_w)).T)
    dT = np.ascontiguousarray(_byte0(np.asarray(L0_ffn_down_w)).T)
    g4 = np.ascontiguousarray(gT.reshape(16, 128, 32, 256).transpose(2, 1, 0, 3))
    u4 = np.ascontiguousarray(uT.reshape(16, 128, 32, 256).transpose(2, 1, 0, 3))
    d4 = np.ascontiguousarray(dT.reshape(64, 128, 16, 128).transpose(2, 1, 0, 3))
    gsc = np.ascontiguousarray(np.asarray(L0_ffn_gate_scale, np.float32).reshape(64, 128).T)
    usc = np.ascontiguousarray(np.asarray(L0_ffn_up_scale, np.float32).reshape(64, 128).T)
    dsc = np.ascontiguousarray(np.asarray(L0_ffn_down_scale, np.float32).reshape(16, 128).T)
    idf = np.eye(128, dtype=np.float32)
    idb = np.eye(128, dtype=np.float32).astype(bf)
    onef = np.ones((128, 128), np.float32)
    oneb = np.ones((128, 128), np.float32).astype(bf)
    mrow = np.full((1, 128), -128.0, np.float32).astype(bf)

    nc = build(stage=_stage)
    shared = {"wq": wq, "wk": wk, "wv": wv, "bq_in": bq, "bk_in": bk, "bv_in": bv,
              "g4": g4, "u4": u4, "d4": d4, "gsc_in": gsc, "usc_in": usc,
              "dsc_in": dsc, "idf_in": idf, "idb_in": idb, "onef_in": onef,
              "oneb_in": oneb, "mrow_in": mrow}
    in_maps = [dict(shared, x_in=np.ascontiguousarray(x[c * TPC:(c + 1) * TPC]))
               for c in range(8)]
    res = run_bass_kernel_spmd(nc, in_maps, core_ids=list(range(8)))
    full = np.concatenate([np.asarray(r["out"], np.float32) for r in res.results], axis=0)
    return full.reshape(4, 4096, D)


# version 06
# speedup vs baseline: 1.0263x; 1.0262x over previous
import contextlib
import numpy as np
import ml_dtypes
import concourse.bass as bass
import concourse.mybir as mybir
from concourse.bass_utils import run_bass_kernel_spmd

F32 = mybir.dt.float32
BF16 = mybir.dt.bfloat16
U8 = mybir.dt.uint8
AF = mybir.ActivationFunctionType
ALU = mybir.AluOpType

D = 2048
I = 8192
TPC = 2048
CH = 512
NCH = TPC // CH
NHT = D // 128
NIT = I // 128
HQ, HKV, HD = 16, 4, 128
SEQ = 4096
SCALE = 1.0 / float(np.sqrt(128.0))
EXP_OFF = 60.0

OFF_XCH = 0x04000
OFF_XN = 0x0C000
OFF_HB = 0x10000
OFF_KT = 0x10000
OFF_VT = 0x18000
OFF_XSTG = 0x10000
OFF_GW0 = 0x20000
OFF_GW1 = 0x22000
OFF_UW0 = 0x24000
OFF_UW1 = 0x26000
OFF_DW0 = 0x28000
OFF_DW1 = 0x2C000
OFF_PT = 0x20000
OFF_VTMP = 0x28000
OFF_QSQ = 0x28000
OFF_WQ0 = 0x30000
OFF_WQ1 = 0x31000
OFF_OSTG = 0x30000
OFF_IDF = 0x32000
OFF_IDB = 0x32200
OFF_ONEF = 0x32300
OFF_ONEB = 0x32500
OFF_MROW = 0x32600
OFF_BQ = 0x32700
OFF_BQS = 0x32740
OFF_BK = 0x32780
OFF_BV = 0x327A0
OFF_GSC = 0x327C0
OFF_USC = 0x328C0
OFF_DSC = 0x329C0
OFF_DSCH = 0x32A00
OFF_BEXP = 0x32A40
OFF_ROWSB = 0x32A80
OFF_ROWS = 0x33000
OFF_SG0 = 0x34800
OFF_SG1 = 0x35000
OFF_UU0 = 0x35800
OFF_UU1 = 0x36000
OFF_PST0 = 0x36800
OFF_PST1 = 0x36C00
OFF_SQ0 = 0x37000
OFF_SQ1 = 0x37400

SEM_NAMES = [
    "d_x", "d_c", "d_pk", "d_out", "d_kv",
    "d_q0", "d_q1", "d_gu0", "d_gu1", "d_dw0", "d_dw1", "d_wq0", "d_wq1",
    "d_qh0", "d_qh1", "d_x10", "d_x11",
    "s_cc", "c_pe", "c_act", "c_dve",
]


def build(stage=3):
    eff = 2 if stage == 12 else stage
    nc = bass.Bass("TRN2", target_bir_lowering=False)

    x_in = nc.dram_tensor("x_in", [TPC, D], F32, kind="ExternalInput").ap()
    wq = nc.dram_tensor("wq", [D, D], F32, kind="ExternalInput").ap()
    wk = nc.dram_tensor("wk", [D, 512], F32, kind="ExternalInput").ap()
    wv = nc.dram_tensor("wv", [D, 512], F32, kind="ExternalInput").ap()
    bq_in = nc.dram_tensor("bq_in", [128, 16], F32, kind="ExternalInput").ap()
    bk_in = nc.dram_tensor("bk_in", [128, 4], F32, kind="ExternalInput").ap()
    bv_in = nc.dram_tensor("bv_in", [128, 4], F32, kind="ExternalInput").ap()
    g4 = nc.dram_tensor("g4", [32, 128, 16, 256], U8, kind="ExternalInput").ap()
    u4 = nc.dram_tensor("u4", [32, 128, 16, 256], U8, kind="ExternalInput").ap()
    d4 = nc.dram_tensor("d4", [16, 128, 64, 128], U8, kind="ExternalInput").ap()
    gsc_in = nc.dram_tensor("gsc_in", [128, 64], F32, kind="ExternalInput").ap()
    usc_in = nc.dram_tensor("usc_in", [128, 64], F32, kind="ExternalInput").ap()
    dsc_in = nc.dram_tensor("dsc_in", [128, 16], F32, kind="ExternalInput").ap()
    idf_in = nc.dram_tensor("idf_in", [128, 128], F32, kind="ExternalInput").ap()
    idb_in = nc.dram_tensor("idb_in", [128, 128], BF16, kind="ExternalInput").ap()
    onef_in = nc.dram_tensor("onef_in", [128, 128], F32, kind="ExternalInput").ap()
    oneb_in = nc.dram_tensor("oneb_in", [128, 128], BF16, kind="ExternalInput").ap()
    mrow_in = nc.dram_tensor("mrow_in", [1, 128], BF16, kind="ExternalInput").ap()
    out = nc.dram_tensor("out", [TPC, D], F32, kind="ExternalOutput").ap()
    if stage == 11:
        dbg_xn = nc.dram_tensor("dbg_xn", [128, NHT * CH], BF16, kind="ExternalOutput").ap()
        dbg_hb = nc.dram_tensor("dbg_hb", [128, NIT * CH], BF16, kind="ExternalOutput").ap()
        dbg_x = nc.dram_tensor("dbg_x", [128, NHT * CH], F32, kind="ExternalOutput").ap()
    if stage == 12:
        dbg_q = nc.dram_tensor("dbg_q", [D, TPC], BF16, kind="ExternalOutput").ap()
        dbg_kv = nc.dram_tensor("dbg_kv", [2048, TPC], BF16, kind="ExternalOutput").ap()
        dbg_bexp = nc.dram_tensor("dbg_bexp", [128, 16], F32, kind="ExternalOutput").ap()
        dbg_xp = nc.dram_tensor("dbg_xp", [D, TPC], F32, kind="ExternalOutput").ap()
    xp = nc.dram_tensor("xp", [D, TPC], F32, kind="Internal").ap()
    xp2 = nc.dram_tensor("xp2", [D, TPC], F32, kind="Internal").ap()
    qpark = nc.dram_tensor("qpark", [D, TPC], BF16, kind="Internal").ap()
    kv_send = nc.dram_tensor("kv_send", [1024, TPC], BF16, kind="Internal").ap()
    kv_recv = nc.dram_tensor("kv_recv", [2048, TPC], BF16, kind="Internal").ap()

    SH = 0x80
    _MAP_END = 0x37800 + SH
    nc.alloc_sbuf_tensor("mapspacer", [128, _MAP_END - nc.sbuf_base], U8)
    assert nc.sbuf_base >= _MAP_END, hex(nc.sbuf_base)

    def A(name, shape, dtype, offset):
        return nc.alloc_sbuf_tensor_at(name, shape, dtype, offset=offset + SH)

    xch = A("xch", [128, NHT * CH], F32, offset=OFF_XCH)
    xn = A("xn", [128, NHT * CH], BF16, offset=OFF_XN)
    hb = A("hb", [128, NIT * CH], BF16, offset=OFF_HB)
    kT = A("kTs", [128, 4 * SEQ], BF16, offset=OFF_KT)
    v_t = A("v_ts", [128, 4 * SEQ], BF16, offset=OFF_VT)
    xstg = A("xstg", [128, 4 * D], F32, offset=OFF_XSTG)
    gw = [A("gw0", [128, 4096], BF16, offset=OFF_GW0),
          A("gw1", [128, 4096], BF16, offset=OFF_GW1)]
    uw = [A("uw0", [128, 4096], BF16, offset=OFF_UW0),
          A("uw1", [128, 4096], BF16, offset=OFF_UW1)]
    dw = [A("dw0", [128, 8192], BF16, offset=OFF_DW0),
          A("dw1", [128, 8192], BF16, offset=OFF_DW1)]
    pT = A("pTs", [128, 32 * CH], BF16, offset=OFF_PT)
    vtmp = A("vtmp", [128, 4 * SEQ], BF16, offset=OFF_VTMP)
    qsq = A("qsq", [128, SEQ], BF16, offset=OFF_QSQ)
    wqb = [A("wqb0", [128, 2048], BF16, offset=OFF_WQ0),
           A("wqb1", [128, 2048], BF16, offset=OFF_WQ1)]
    qh = wqb
    ostg = A("ostg", [128, D], F32, offset=OFF_OSTG)
    idf = A("idf", [128, 128], F32, offset=OFF_IDF)
    idb = A("idb", [128, 128], BF16, offset=OFF_IDB)
    onef = A("onef", [128, 128], F32, offset=OFF_ONEF)
    oneb = A("oneb", [128, 128], BF16, offset=OFF_ONEB)
    mrow = A("mrow", [1, 128], BF16, offset=OFF_MROW)
    bq_s = A("bq_s", [128, 16], F32, offset=OFF_BQ)
    bqs = A("bqs", [128, 16], F32, offset=OFF_BQS)
    bk_s = A("bk_s", [128, 4], F32, offset=OFF_BK)
    bv_s = A("bv_s", [128, 4], F32, offset=OFF_BV)
    gsc = A("gsc", [128, 64], F32, offset=OFF_GSC)
    usc = A("usc", [128, 64], F32, offset=OFF_USC)
    dsc = A("dsc", [128, 16], F32, offset=OFF_DSC)
    dsch = A("dsch", [128, 16], F32, offset=OFF_DSCH)
    bexp = A("bexp", [128, 16], F32, offset=OFF_BEXP)
    rowsb = A("rowsb", [128, 512], BF16, offset=OFF_ROWSB)
    qmx = A("qmx", [128, 16], F32, offset=OFF_ROWSB + 0x400)
    rows = A("rows", [128, 1536], F32, offset=OFF_ROWS)
    sg = [A("sg0", [128, CH], F32, offset=OFF_SG0),
          A("sg1", [128, CH], F32, offset=OFF_SG1)]
    uu = [A("uu0", [128, CH], F32, offset=OFF_UU0),
          A("uu1", [128, CH], F32, offset=OFF_UU1)]
    pst = [A("pst0", [128, CH], BF16, offset=OFF_PST0),
           A("pst1", [128, CH], BF16, offset=OFF_PST1)]
    sqst = [A("sq0", [128, CH], BF16, offset=OFF_SQ0),
            A("sq1", [128, CH], BF16, offset=OFF_SQ1)]

    es = contextlib.ExitStack()
    psum = [es.enter_context(nc.psum_tensor(f"pb{i}", [128, 512], F32)) for i in range(7)]
    psb = es.enter_context(nc.psum_tensor("pb7", [128, 1024], BF16))
    SEM = {n: es.enter_context(nc.semaphore(n)) for n in SEM_NAMES}
    CNT = {n: 0 for n in SEM_NAMES}
    LASTW = {}
    BANKR = [{"c_act": 0, "c_dve": 0} for _ in range(8)]

    def inc(ins, sem, k=1):
        ins.then_inc(SEM[sem], k)
        CNT[sem] += k
        return CNT[sem]

    def wt(eng, sem, val):
        if val is None or val <= 0:
            return
        key = (id(eng), sem)
        if LASTW.get(key, -1) < val:
            eng.wait_ge(SEM[sem], val)
            LASTW[key] = val

    def bank_guard(tensor, b):
        wt(tensor, "c_act", BANKR[b]["c_act"])
        wt(tensor, "c_dve", BANKR[b]["c_dve"])

    def read_bank(b, sem, tag):
        BANKR[b][sem] = max(BANKR[b][sem], tag)

    def xch_t(ht):
        return xch[:, ht * CH:(ht + 1) * CH]

    def xn_t(ht):
        return xn[:, ht * CH:(ht + 1) * CH]

    def hb_t(it):
        return hb[:, it * CH:(it + 1) * CH]

    def row(i):
        return rows[0:1, i * 512:(i + 1) * 512]

    def rcell(c0, n=1):
        return rows[0:1, c0:c0 + n]

    blockctx = nc.Block()
    block = blockctx.__enter__()
    state = {"xn_ready": 0, "ffn_done_dve": 0, "wq_free_pe": 0,
             "gu_prev_free": 0, "x2_store_tag": 0, "scores_done_pe": 0}

    def ph_consts():
        def _(sync):
            for dst, src in [(idf, idf_in), (idb, idb_in), (onef, onef_in),
                             (oneb, oneb_in), (bq_s, bq_in), (bk_s, bk_in),
                             (bv_s, bv_in), (gsc, gsc_in), (usc, usc_in),
                             (dsc, dsc_in)]:
                inc(sync.dma_start(out=dst[:, :], in_=src), "d_c", 16)
            inc(sync.dma_start(out=mrow[0:1, :], in_=mrow_in), "d_c", 16)
        block.sync(_)
        state["consts"] = CNT["d_c"]

        def _(vector):
            wt(vector, "d_c", state["consts"])
            inc(vector.tensor_scalar_mul(bqs[:, :], bq_s[:, :], SCALE), "c_dve")
            inc(vector.tensor_scalar_mul(dsch[:, :], dsc[:, :], 0.5), "c_dve")
            inc(vector.memset(qmx[:, :], 0.0), "c_dve")
        block.vector(_)
        state["consts_dve"] = CNT["c_dve"]

    def ph_ln(x_ready):
        A0 = CNT["c_act"]
        P0 = CNT["c_pe"]

        def _(scalar):
            for ht in range(NHT):
                s, v = x_ready(ht)
                wt(scalar, s, v)
                if ht >= 2:
                    wt(scalar, "c_pe", P0 + 2 * (ht - 2) + 2)
                inc(scalar.activation(out=xn_t(ht), in_=xch_t(ht), func=AF.Copy), "c_act")
                inc(scalar.activation(out=sqst[ht % 2][:, :], in_=xch_t(ht),
                                      func=AF.Square), "c_act")
        block.scalar(_)

        def _(tensor):
            wt(tensor, "d_c", state["consts"])
            s, v = x_ready(NHT - 1)
            wt(tensor, s, v)
            bank_guard(tensor, 6)
            bank_guard(tensor, 4)
            for ht in range(NHT):
                wt(tensor, "c_act", A0 + 2 * ht + 1)
                inc(tensor.matmul(psum[6][0:1, :], oneb[:, 0:1], xn_t(ht),
                                  start=(ht == 0), stop=(ht == NHT - 1)), "c_pe")
                wt(tensor, "c_act", A0 + 2 * ht + 2)
                inc(tensor.matmul(psum[4][0:1, :], oneb[:, 0:1], sqst[ht % 2][:, :],
                                  start=(ht == 0), stop=(ht == NHT - 1)), "c_pe")
        block.tensor(_)
        pe_stats = CNT["c_pe"]
        assert pe_stats == P0 + 2 * NHT

        def _(vector):
            wt(vector, "c_pe", pe_stats)
            if stage == 11:
                inc(vector.tensor_copy(out=ostg[0:1, 0:512], in_=psum[6][0:1, :]), "c_dve")
                inc(vector.tensor_copy(out=ostg[0:1, 512:1024], in_=psum[4][0:1, :]), "c_dve")
            t1 = inc(vector.tensor_scalar_mul(row(0), psum[6][0:1, :], 1.0 / D), "c_dve")
            t2 = inc(vector.tensor_scalar_mul(row(1), psum[4][0:1, :], 1.0 / D), "c_dve")
            wt(vector, "c_dve", t1)
            t3 = inc(vector.tensor_mul(row(2), row(0), row(0)), "c_dve")
            wt(vector, "c_dve", t3)
            inc(vector.scalar_tensor_tensor(out=row(1), in0=row(1), scalar=1e-5,
                                            in1=row(2), op0=ALU.add, op1=ALU.subtract),
                "c_dve")
        block.vector(_)
        dve_var = CNT["c_dve"]
        read_bank(6, "c_dve", dve_var)
        read_bank(4, "c_dve", dve_var)
        if stage == 11:
            def _(vector):
                wt(vector, "c_dve", dve_var)
                inc(vector.tensor_copy(out=ostg[0:1, 1024:1536], in_=row(1)), "c_dve")
            block.vector(_)
            dve_var = CNT["c_dve"]

        def _(scalar):
            wt(scalar, "c_dve", dve_var)
            a1 = inc(scalar.activation(out=row(2), in_=row(1), func=AF.Ln), "c_act")
            wt(scalar, "c_act", a1)
            inc(scalar.activation(out=row(1), in_=row(2), func=AF.Exp, scale=-0.5), "c_act")
        block.scalar(_)
        if stage == 11:
            def _(vector):
                wt(vector, "c_act", CNT["c_act"])
                inc(vector.tensor_copy(out=sg[1][0:1, :], in_=row(2)), "c_dve")
            block.vector(_)
        act_rstd = CNT["c_act"]
        if stage == 11:
            def _(vector):
                wt(vector, "c_act", act_rstd)
                inc(vector.tensor_copy(out=ostg[0:1, 1536:2048], in_=row(1)), "c_dve")
            block.vector(_)

        def _(vector):
            wt(vector, "c_act", act_rstd)
            inc(vector.scalar_tensor_tensor(out=row(2), in0=row(0), scalar=-1.0,
                                            in1=row(1), op0=ALU.mult, op1=ALU.mult), "c_dve")
        block.vector(_)
        dve_nmr = CNT["c_dve"]

        def _(tensor):
            wt(tensor, "c_dve", dve_nmr)
            bank_guard(tensor, 6)
            bank_guard(tensor, 4)
            inc(tensor.matmul(psum[6][:, :], onef[0:1, :], row(1), start=True, stop=True), "c_pe")
            inc(tensor.matmul(psum[4][:, :], onef[0:1, :], row(2), start=True, stop=True), "c_pe")
        block.tensor(_)
        pe_bc = CNT["c_pe"]

        def _(vector):
            wt(vector, "c_pe", pe_bc)
            prev_add = 0
            tags = []
            for ht in range(NHT):
                wt(vector, "c_dve", prev_add)
                t = inc(vector.tensor_mul(sg[0][:, :], xch_t(ht), psum[6][:, :]), "c_dve")
                wt(vector, "c_dve", t)
                prev_add = inc(vector.tensor_add(xn_t(ht), sg[0][:, :], psum[4][:, :]), "c_dve")
                tags.append(prev_add)
            state["xn_tags"] = tags
        block.vector(_)
        state["xn_ready"] = CNT["c_dve"]
        read_bank(6, "c_dve", state["xn_ready"])
        read_bank(4, "c_dve", state["xn_ready"])

    def ph_ffn(warena_free_pe):
        NB = 32
        P0 = CNT["c_pe"]
        A0 = CNT["c_act"]
        V0 = CNT["c_dve"]
        gu_base = [CNT["d_gu0"], CNT["d_gu1"]]

        def _pe_before(it):
            return 2 * it + max(0, it - 2)

        def tag_g(it):
            return P0 + _pe_before(it) + 1

        def tag_u(it):
            return P0 + _pe_before(it) + 2

        def tag_silu(it):
            return A0 + it + 1

        def tag_uu(it):
            return V0 + 2 * it + 1

        def tag_h(it):
            return V0 + 2 * it + 2

        def _(gpsimd):
            for ib in range(NB):
                if ib >= 2:
                    wt(gpsimd, "c_pe", tag_u(2 * ib - 3))
                else:
                    wt(gpsimd, "c_pe", warena_free_pe)
                sem = "d_gu0" if ib % 2 == 0 else "d_gu1"
                inc(gpsimd.dma_start(
                    out=gw[ib % 2][:].rearrange("p (n c) -> p n c", c=256), in_=g4[ib]), sem, 16)
                inc(gpsimd.dma_start(
                    out=uw[ib % 2][:].rearrange("p (n c) -> p n c", c=256), in_=u4[ib]), sem, 16)
        block.gpsimd(_)

        xtags = state["xn_tags"]

        def _(tensor):
            bank_guard(tensor, 6)
            for it in range(2 * NB):
                ib = it // 2
                sem = "d_gu0" if ib % 2 == 0 else "d_gu1"
                wt(tensor, sem, gu_base[ib % 2] + 32 * (ib // 2 + 1))
                if it >= 2:
                    wt(tensor, "c_act", tag_silu(it - 2))
                    wt(tensor, "c_dve", tag_uu(it - 2))
                else:
                    bank_guard(tensor, it % 2)
                    bank_guard(tensor, 2 + it % 2)
                sl = (it % 2) * 128
                for ht in range(NHT):
                    wt(tensor, "c_dve", xtags[ht])
                    ins = tensor.matmul(
                        psum[it % 2][:, :],
                        gw[ib % 2][:, ht * 256 + sl:ht * 256 + sl + 128],
                        xn_t(ht), start=(ht == 0), stop=(ht == NHT - 1))
                inc(ins, "c_pe")
                for ht in range(NHT):
                    ins = tensor.matmul(
                        psum[2 + it % 2][:, :],
                        uw[ib % 2][:, ht * 256 + sl:ht * 256 + sl + 128],
                        xn_t(ht), start=(ht == 0), stop=(ht == NHT - 1))
                inc(ins, "c_pe")
                if it >= 2:
                    wt(tensor, "c_dve", tag_h(it - 2))
                    inc(tensor.matmul(psum[6][0:1, :], oneb[:, 0:1], hb_t(it - 2),
                                      start=(it == 2), stop=False), "c_pe")
        block.tensor(_)

        def _(tensor):
            for it in range(2 * NB - 2, 2 * NB):
                wt(tensor, "c_dve", tag_h(it))
                inc(tensor.matmul(psum[6][0:1, :], oneb[:, 0:1], hb_t(it),
                                  start=False, stop=(it == 2 * NB - 1)), "c_pe")
        block.tensor(_)
        assert CNT["c_pe"] == P0 + 4 * NB + (2 * NB - 2) + 2, (CNT["c_pe"], P0)

        def _(scalar):
            wt(scalar, "d_c", state["consts"])
            for it in range(2 * NB):
                wt(scalar, "c_pe", tag_g(it))
                if it >= 2:
                    wt(scalar, "c_dve", tag_h(it - 2))
                inc(scalar.activation(out=sg[it % 2][:, :], in_=psum[it % 2][:, :],
                                      func=AF.Silu, scale=gsc[:, it:it + 1]), "c_act")
        block.scalar(_)
        assert CNT["c_act"] == A0 + 2 * NB

        def _(vector):
            wt(vector, "d_c", state["consts"])
            for it in range(2 * NB):
                wt(vector, "c_pe", tag_u(it))
                if it >= 2:
                    wt(vector, "c_dve", tag_h(it - 2))
                t = inc(vector.tensor_scalar_mul(uu[it % 2][:, :], psum[2 + it % 2][:, :],
                                                 usc[:, it:it + 1]), "c_dve")
                wt(vector, "c_act", tag_silu(it))
                wt(vector, "c_dve", t)
                inc(vector.tensor_mul(hb_t(it), sg[it % 2][:, :], uu[it % 2][:, :]), "c_dve")
        block.vector(_)
        assert CNT["c_dve"] == V0 + 4 * NB
        for b in range(4):
            read_bank(b, "c_act", A0 + 2 * NB)
            read_bank(b, "c_dve", V0 + 4 * NB)

        pe_csh = CNT["c_pe"]

        def _(vector):
            wt(vector, "c_pe", pe_csh)
            t1 = inc(vector.tensor_copy(out=row(0), in_=psum[6][0:1, :]), "c_dve")
            wt(vector, "c_dve", t1)
            t2 = inc(vector.tensor_copy(out=rowsb[0:1, :], in_=row(0)), "c_dve")
            wt(vector, "c_dve", t2)
            inc(vector.tensor_sub(sqst[0][0:1, :], row(0), rowsb[0:1, :]), "c_dve")
        block.vector(_)
        csh_ready = CNT["c_dve"]
        read_bank(6, "c_dve", csh_ready)

        Pd0 = CNT["c_pe"]
        Vd0 = CNT["c_dve"]
        dw_base = [CNT["d_dw0"], CNT["d_dw1"]]

        def tag_d(hht):
            return Pd0 + hht + 1

        def tag_stt(hht):
            return Vd0 + hht + 1

        def _(gpsimd):
            for hht in range(NHT):
                if hht >= 2:
                    wt(gpsimd, "c_pe", tag_d(hht - 2))
                sem = "d_dw0" if hht % 2 == 0 else "d_dw1"
                inc(gpsimd.dma_start(
                    out=dw[hht % 2][:].rearrange("p (n c) -> p n c", c=128), in_=d4[hht]), sem, 16)
        block.gpsimd(_)

        def _(tensor):
            wt(tensor, "c_dve", csh_ready)
            bank_guard(tensor, 4)
            bank_guard(tensor, 5)
            for hht in range(NHT):
                sem = "d_dw0" if hht % 2 == 0 else "d_dw1"
                wt(tensor, sem, dw_base[hht % 2] + 16 * (hht // 2 + 1))
                if hht >= 2:
                    wt(tensor, "c_dve", tag_stt(hht - 2))
                tensor.matmul(psum[4 + hht % 2][:, :], mrow[0:1, :], rowsb[0:1, :],
                              start=True, stop=False)
                tensor.matmul(psum[4 + hht % 2][:, :], mrow[0:1, :], sqst[0][0:1, :],
                              start=False, stop=False)
                for it in range(NIT):
                    ins = tensor.matmul(
                        psum[4 + hht % 2][:, :],
                        dw[hht % 2][:, it * 128:(it + 1) * 128],
                        hb_t(it), start=False, stop=(it == NIT - 1))
                inc(ins, "c_pe")
        block.tensor(_)
        assert CNT["c_pe"] == Pd0 + NHT

        def _(vector):
            for hht in range(NHT):
                wt(vector, "c_pe", tag_d(hht))
                inc(vector.scalar_tensor_tensor(
                    out=xch_t(hht), in0=psum[4 + hht % 2][:, :], scalar=dsch[:, hht:hht + 1],
                    in1=xch_t(hht), op0=ALU.mult, op1=ALU.add), "c_dve")
        block.vector(_)
        assert CNT["c_dve"] == Vd0 + NHT
        state["ffn_done_dve"] = CNT["c_dve"]
        read_bank(4, "c_dve", Vd0 + NHT)
        read_bank(5, "c_dve", Vd0 + NHT)
        state["gu_free_pe"] = tag_u(2 * NB - 1)
        state["dw_free_pe"] = CNT["c_pe"]

    def ph_load_x_transposed(ch, xch_free_pe, park_tag):
        def _(sync):
            wt(sync, "c_pe", xch_free_pe)
            wt(sync, "d_pk", park_tag)
            for tt in range(4):
                inc(sync.dma_start(out=xstg[:, tt * D:(tt + 1) * D],
                                   in_=x_in[ch * CH + tt * 128: ch * CH + (tt + 1) * 128, :]),
                    "d_x", 16)
        block.sync(_)
        xload = CNT["d_x"]
        P0 = CNT["c_pe"]
        V0 = CNT["c_dve"]

        def _(tensor):
            wt(tensor, "d_c", state["consts"])
            wt(tensor, "d_x", xload)
            bank_guard(tensor, 4)
            bank_guard(tensor, 5)
            for ht in range(NHT):
                if ht >= 2:
                    wt(tensor, "c_dve", V0 + ht - 1)
                for tt in range(4):
                    ins = tensor.transpose(
                        psum[4 + ht % 2][:, tt * 128:(tt + 1) * 128],
                        xstg[:, tt * D + ht * 128: tt * D + (ht + 1) * 128],
                        idf[:, :])
                inc(ins, "c_pe")
        block.tensor(_)

        def _(vector):
            for ht in range(NHT):
                wt(vector, "c_pe", P0 + ht + 1)
                inc(vector.tensor_copy(out=xch_t(ht), in_=psum[4 + ht % 2][:, :]), "c_dve")
        block.vector(_)
        read_bank(4, "c_dve", V0 + NHT)
        read_bank(5, "c_dve", V0 + NHT)
        return lambda ht: ("c_dve", V0 + ht + 1)

    def ph_load_x_hmajor(ch, src, xch_free_pe):
        def _(sync):
            wt(sync, "c_pe", xch_free_pe)
            wt(sync, "d_pk", state["att_done_pk"])
            inc(sync.dma_start(
                out=xch[:].rearrange("p (n c) -> p n c", c=CH),
                in_=src.rearrange("(n p) t -> p n t", p=128)[:, :, ch * CH:(ch + 1) * CH]),
                "d_x", 16)
        block.sync(_)
        xload = CNT["d_x"]
        return lambda ht: ("d_x", xload)

    def ph_proj(ch):
        NS = 24
        P0 = CNT["c_pe"]
        wq_base = [CNT["d_wq0"], CNT["d_wq1"]]
        q_base = [CNT["d_q0"], CNT["d_q1"]]

        def _pe_before(s):
            return s + min(max(0, s - 2), 16)

        def tag_p(s):
            return P0 + _pe_before(s) + 1

        def tag_nm(n):
            return P0 + _pe_before(n + 2) + 2

        def _(gpsimd):
            for s in range(NS):
                if s >= 2:
                    wt(gpsimd, "c_pe", tag_p(s - 2))
                else:
                    wt(gpsimd, "c_pe", state["wq_free_pe"])
                sem = "d_wq0" if s % 2 == 0 else "d_wq1"
                if s < 16:
                    src = wq.rearrange("(n p) c -> p n c", p=128)[:, :, s * 128:(s + 1) * 128]
                elif s < 20:
                    dt = s - 16
                    src = wk.rearrange("(n p) c -> p n c", p=128)[:, :, dt * 128:(dt + 1) * 128]
                else:
                    dt = s - 20
                    src = wv.rearrange("(n p) c -> p n c", p=128)[:, :, dt * 128:(dt + 1) * 128]
                inc(gpsimd.dma_start(
                    out=wqb[s % 2][:].rearrange("p (n c) -> p n c", c=128), in_=src), sem, 16)
        block.gpsimd(_)

        V0 = CNT["c_dve"]

        def _v_incs(s):
            n = 0
            for t in range(s):
                n += 1
                if t < 16:
                    n += 1
                if 2 <= t < 18:
                    n += 2
            return n

        def tag_e(s):
            return V0 + _v_incs(s) + 1

        def tag_sq(s):
            return tag_e(s) + 1

        def tag_red(n):
            return tag_e(n + 2) + 1 + (1 if (n + 2) < 16 else 0)

        xtags = state["xn_tags"]

        def _(tensor):
            bank_guard(tensor, 6)
            for s in range(NS):
                sem = "d_wq0" if s % 2 == 0 else "d_wq1"
                wt(tensor, sem, wq_base[s % 2] + 16 * (s // 2 + 1))
                if s >= 2:
                    wt(tensor, "c_dve", tag_e(s - 2))
                else:
                    bank_guard(tensor, s % 2)
                for ht in range(NHT):
                    wt(tensor, "c_dve", xtags[ht])
                    ins = tensor.matmul(
                        psum[s % 2][:, :],
                        wqb[s % 2][:, ht * 128:(ht + 1) * 128],
                        xn_t(ht), start=(ht == 0), stop=(ht == NHT - 1))
                inc(ins, "c_pe")
                assert CNT["c_pe"] == tag_p(s), (s, CNT["c_pe"], tag_p(s))
                if s >= 2 and s - 2 < 16:
                    n = s - 2
                    wt(tensor, "c_dve", tag_sq(n))
                    if n >= 1:
                        wt(tensor, "c_dve", tag_red(n - 1))
                    inc(tensor.matmul(psum[6][0:1, :], oneb[:, 0:1], sqst[n % 2][:, :],
                                      start=True, stop=True), "c_pe")
        block.tensor(_)

        def tag_ev(s):
            return tag_e(s)

        def _(vector):
            wt(vector, "c_dve", state["consts_dve"])
            for s in range(NS):
                wt(vector, "c_pe", tag_p(s))
                semd = "d_q0" if s % 2 == 0 else "d_q1"
                wt(vector, semd, q_base[s % 2] + 16 * (s // 2))
                if s < 16:
                    e = inc(vector.tensor_scalar(out=pst[s % 2][:, :], in0=psum[s % 2][:, :],
                                                 scalar1=SCALE, scalar2=bqs[:, s:s + 1],
                                                 op0=ALU.mult, op1=ALU.add), "c_dve")
                elif s < 20:
                    inc(vector.tensor_scalar_add(pst[s % 2][:, :], psum[s % 2][:, :],
                                                 bk_s[:, s - 16:s - 15]), "c_dve")
                else:
                    inc(vector.tensor_scalar_add(pst[s % 2][:, :], psum[s % 2][:, :],
                                                 bv_s[:, s - 20:s - 19]), "c_dve")
                assert CNT["c_dve"] == tag_e(s), (s, CNT["c_dve"], tag_e(s))
                if s < 16:
                    wt(vector, "c_dve", e)
                    wt(vector, "c_pe", tag_nm(s - 2) if s >= 2 else 0)
                    inc(vector.tensor_mul(sqst[s % 2][:, :], pst[s % 2][:, :],
                                          pst[s % 2][:, :]), "c_dve")
                if 2 <= s < 18:
                    n = s - 2
                    wt(vector, "c_pe", tag_nm(n))
                    if n >= 1:
                        wt(vector, "c_dve", tag_red(n - 1) + 1)
                    t = inc(vector.reduce_max(qmx[0:1, 1:2], psum[6][0:1, :],
                                              axis=mybir.AxisListType.X), "c_dve")
                    wt(vector, "c_dve", t)
                    inc(vector.tensor_max(qmx[0:1, 0:1], qmx[0:1, 0:1],
                                          qmx[0:1, 1:2]), "c_dve")
        block.vector(_)
        read_bank(6, "c_dve", CNT["c_dve"])

        def _(scalar):
            for s in range(NS):
                wt(scalar, "c_dve", tag_ev(s))
                sem = "d_q0" if s % 2 == 0 else "d_q1"
                if s < 16:
                    dst = qpark[s * 128:(s + 1) * 128, ch * CH:(ch + 1) * CH]
                elif s < 20:
                    dst = kv_send[(s - 16) * 128:(s - 15) * 128, ch * CH:(ch + 1) * CH]
                else:
                    dst = kv_send[512 + (s - 20) * 128:512 + (s - 19) * 128,
                                  ch * CH:(ch + 1) * CH]
                inc(scalar.dma_start(out=dst, in_=pst[s % 2][:, :]), sem, 16)
        block.scalar(_)
        read_bank(0, "c_dve", tag_ev(NS - 2))
        read_bank(1, "c_dve", tag_ev(NS - 1))
        state["wq_free_pe"] = tag_p(NS - 1)
        state["proj_last_pe"] = tag_p(NS - 1)

    def ph_park(ch):
        def _(sync):
            wt(sync, "c_dve", state["ffn_done_dve"])
            inc(sync.dma_start(
                out=xp.rearrange("(n p) t -> p n t", p=128)[:, :, ch * CH:(ch + 1) * CH],
                in_=xch[:].rearrange("p (n c) -> p n c", c=CH)), "d_pk", 16)
        block.sync(_)
        return CNT["d_pk"]

    def ph_out(ch):
        P0 = CNT["c_pe"]
        V0 = CNT["c_dve"]
        O0 = CNT["d_out"]

        def _(tensor):
            wt(tensor, "d_c", state["consts"])
            wt(tensor, "c_dve", state["ffn_done_dve"])
            bank_guard(tensor, 4)
            bank_guard(tensor, 5)
            gi = 0
            for tt in range(4):
                for grp in range(4):
                    if gi >= 2:
                        wt(tensor, "c_dve", V0 + gi - 1)
                    bank = 4 + gi % 2
                    for j in range(4):
                        ht = grp * 4 + j
                        ins = tensor.transpose(
                            psum[bank][:, j * 128:(j + 1) * 128],
                            xch_t(ht)[:, tt * 128:(tt + 1) * 128],
                            idf[:, :])
                    inc(ins, "c_pe")
                    gi += 1
        block.tensor(_)

        def _(vector):
            gi = 0
            for tt in range(4):
                wt(vector, "d_out", O0 + 16 * tt)
                for grp in range(4):
                    wt(vector, "c_pe", P0 + gi + 1)
                    inc(vector.tensor_copy(out=ostg[:, grp * 512:(grp + 1) * 512],
                                           in_=psum[4 + gi % 2][:, :]), "c_dve")
                    gi += 1
        block.vector(_)

        def _(sync):
            for tt in range(4):
                wt(sync, "c_dve", V0 + 4 * tt + 4)
                inc(sync.dma_start(out=out[ch * CH + tt * 128: ch * CH + (tt + 1) * 128, :],
                                   in_=ostg[:, :]), "d_out", 16)
        block.sync(_)
        read_bank(4, "c_dve", V0 + 16)
        read_bank(5, "c_dve", V0 + 16)
        state["out_pe"] = CNT["c_pe"]
        state["out_last_dve"] = CNT["c_dve"]

    def ph_attention():
        def _(gpsimd):
            wt(gpsimd, "d_q0", CNT["d_q0"])
            wt(gpsimd, "d_q1", CNT["d_q1"])
            inc(gpsimd.collective_compute(
                "AllGather", ALU.bypass,
                replica_groups=[[0, 1], [2, 3], [4, 5], [6, 7]],
                ins=[kv_send[:, :]], outs=[kv_recv[:, :]]), "s_cc", 1)
        block.gpsimd(_)

        qh_base = [CNT["d_qh0"], CNT["d_qh1"]]

        def _(sync):
            wt(sync, "s_cc", CNT["s_cc"])
            inc(sync.dma_start(
                out=kT[:].rearrange("p (g t) -> p g t", t=SEQ)[:, :, 0:TPC],
                in_=kv_recv[0:512, :].rearrange("(g p) t -> p g t", p=128)), "d_kv", 16)
            inc(sync.dma_start(
                out=kT[:].rearrange("p (g t) -> p g t", t=SEQ)[:, :, TPC:SEQ],
                in_=kv_recv[1024:1536, :].rearrange("(g p) t -> p g t", p=128)), "d_kv", 16)
            inc(sync.dma_start(
                out=vtmp[:].rearrange("p (g t) -> p g t", t=SEQ)[:, :, 0:TPC],
                in_=kv_recv[512:1024, :].rearrange("(g p) t -> p g t", p=128)), "d_kv", 16)
            inc(sync.dma_start(
                out=vtmp[:].rearrange("p (g t) -> p g t", t=SEQ)[:, :, TPC:SEQ],
                in_=kv_recv[1536:2048, :].rearrange("(g p) t -> p g t", p=128)), "d_kv", 16)
            inc(sync.dma_start(out=qh[0][:, :], in_=qpark[0:128, :]), "d_qh0", 16)
        block.sync(_)
        kvload = CNT["d_kv"]

        P0 = CNT["c_pe"]
        V0 = CNT["c_dve"]

        def _(tensor):
            wt(tensor, "d_c", state["consts"])
            wt(tensor, "d_kv", kvload)
            for grp in range(16):
                if grp >= 1:
                    wt(tensor, "c_dve", V0 + 8 * grp)
                for j in range(8):
                    tile = grp * 8 + j
                    g, tkt = tile // 32, tile % 32
                    ins = tensor.transpose(
                        psb[:, j * 128:(j + 1) * 128],
                        vtmp[:, g * SEQ + tkt * 128: g * SEQ + (tkt + 1) * 128],
                        idb[:, :])
                inc(ins, "c_pe")
        block.tensor(_)

        def _(vector):
            for grp in range(16):
                wt(vector, "c_pe", P0 + grp + 1)
                for j in range(8):
                    tile = grp * 8 + j
                    g, tkt = tile // 32, tile % 32
                    inc(vector.tensor_copy(
                        out=v_t[:, g * SEQ + tkt * 128:g * SEQ + (tkt + 1) * 128],
                        in_=psb[:, j * 128:(j + 1) * 128]), "c_dve")
        block.vector(_)
        vt_done = CNT["c_dve"]
        read_bank(7, "c_dve", vt_done)

        prev_mm = 0
        for g in range(4):
            Vs = CNT["c_dve"]

            def _(vector, g=g, prev_mm=prev_mm):
                wt(vector, "c_dve", vt_done)
                wt(vector, "c_pe", prev_mm)
                inc(vector.tensor_mul(qsq[:, :], kT[:, g * SEQ:(g + 1) * SEQ],
                                      kT[:, g * SEQ:(g + 1) * SEQ]), "c_dve")
            block.vector(_)
            Pn = CNT["c_pe"]
            Vr0 = CNT["c_dve"]

            def _(tensor, Vs=Vs, Vr0=Vr0):
                wt(tensor, "c_dve", Vs + 1)
                bank_guard(tensor, 6)
                for q4 in range(8):
                    if q4 >= 1:
                        wt(tensor, "c_dve", Vr0 + q4)
                    inc(tensor.matmul(psum[6][0:1, :], oneb[:, 0:1],
                                      qsq[:, q4 * 512:(q4 + 1) * 512],
                                      start=True, stop=True), "c_pe")
            block.tensor(_)
            prev_mm = CNT["c_pe"]

            def _(vector, g=g, Pn=Pn):
                t = 0
                for q4 in range(8):
                    wt(vector, "c_pe", Pn + q4 + 1)
                    t = inc(vector.reduce_max(rcell(1024 + q4), psum[6][0:1, :],
                                              axis=mybir.AxisListType.X), "c_dve")
                wt(vector, "c_dve", t)
                inc(vector.reduce_max(rcell(1045 + g), rows[0:1, 1024:1032],
                                      axis=mybir.AxisListType.X), "c_dve")
            block.vector(_)
            read_bank(6, "c_dve", CNT["c_dve"])
        knorm_done = CNT["c_dve"]

        def _(vector):
            wt(vector, "c_dve", knorm_done)
            t = inc(vector.reduce_max(rcell(1049), rows[0:1, 1045:1049],
                                      axis=mybir.AxisListType.X), "c_dve")
            wt(vector, "c_dve", t)
            t = inc(vector.tensor_mul(rcell(1050), rcell(1049), qmx[0:1, 0:1]), "c_dve")
        block.vector(_)
        Vb = CNT["c_dve"]

        def _(scalar):
            wt(scalar, "c_dve", Vb)
            a = inc(scalar.activation(out=rcell(1051), in_=rcell(1050), func=AF.Ln),
                    "c_act")
            wt(scalar, "c_act", a)
            inc(scalar.activation(out=rcell(1052), in_=rcell(1051), func=AF.Exp,
                                  scale=0.5), "c_act")
        block.scalar(_)
        Ab = CNT["c_act"]

        def _(vector):
            wt(vector, "c_act", Ab)
            inc(vector.tensor_scalar(out=rcell(1053), in0=rcell(1052),
                                     scalar1=-1.001, scalar2=EXP_OFF,
                                     op0=ALU.mult, op1=ALU.add), "c_dve")
        block.vector(_)
        Vb2 = CNT["c_dve"]

        def _(tensor):
            wt(tensor, "c_dve", Vb2)
            bank_guard(tensor, 6)
            inc(tensor.matmul(psum[6][:, 0:1], onef[0:1, :], rcell(1053),
                              start=True, stop=True), "c_pe")
        block.tensor(_)
        Pb = CNT["c_pe"]

        def _(vector):
            wt(vector, "c_pe", Pb)
            inc(vector.tensor_copy(out=bexp[:, 0:1], in_=psum[6][:, 0:1]), "c_dve")
        block.vector(_)
        bexp_done = CNT["c_dve"]
        read_bank(6, "c_dve", bexp_done)

        for hd in range(HQ):
            g = hd % HKV
            par = hd % 2
            semq = "d_qh0" if par == 0 else "d_qh1"
            qtag = qh_base[par] + 16 * (hd // 2 + 1)

            if hd + 1 < HQ:
                def _(sync, hd=hd):
                    wt(sync, "c_pe", state["scores_done_pe"])
                    sem2 = "d_qh0" if (hd + 1) % 2 == 0 else "d_qh1"
                    inc(sync.dma_start(out=qh[(hd + 1) % 2][:, :],
                                       in_=qpark[(hd + 1) * 128:(hd + 2) * 128, :]), sem2, 16)
                block.sync(_)

            for blk in range(NCH):
                bpar = (hd * NCH + blk) % 2
                semx = "d_x10" if bpar == 0 else "d_x11"

                def _(sync, hd=hd, blk=blk, bpar=bpar, semx=semx):
                    wt(sync, "c_dve", state.get(f"x1_free_{bpar}", 0))
                    inc(sync.dma_start(
                        out=sg[bpar][:, :],
                        in_=xp[hd * 128:(hd + 1) * 128, blk * CH:(blk + 1) * CH]), semx, 16)
                block.sync(_)
                x1tag = CNT[semx]

                P0s = CNT["c_pe"]
                A0s = CNT["c_act"]

                def _(tensor, hd=hd, g=g, blk=blk, A0s=A0s, qtag=qtag, semq=semq):
                    wt(tensor, semq, qtag)
                    bank_guard(tensor, 0)
                    bank_guard(tensor, 1)
                    for tkt in range(32):
                        if tkt >= 2:
                            wt(tensor, "c_act", A0s + tkt - 1)
                        inc(tensor.matmul(
                            psum[tkt % 2][:, :],
                            kT[:, g * SEQ + tkt * 128:g * SEQ + (tkt + 1) * 128],
                            qh[hd % 2][:, blk * CH:(blk + 1) * CH],
                            start=True, stop=True), "c_pe")
                block.tensor(_)
                state["scores_done_pe"] = CNT["c_pe"]

                def _(scalar, hd=hd, P0s=P0s, bexp_done=bexp_done):
                    wt(scalar, "c_dve", bexp_done)
                    for tkt in range(32):
                        wt(scalar, "c_pe", P0s + tkt + 1)
                        inc(scalar.activation(
                            out=pT[:, tkt * CH:(tkt + 1) * CH], in_=psum[tkt % 2][:, :],
                            func=AF.Exp, bias=bexp[:, 0:1]), "c_act")
                block.scalar(_)
                exp_base = A0s
                read_bank(0, "c_act", exp_base + 31)
                read_bank(1, "c_act", exp_base + 32)

                def _(tensor, g=g, exp_base=exp_base):
                    bank_guard(tensor, 2)
                    bank_guard(tensor, 3)
                    for tkt in range(32):
                        wt(tensor, "c_act", exp_base + tkt + 1)
                        tensor.matmul(psum[2][:, :],
                                      v_t[:, g * SEQ + tkt * 128:g * SEQ + (tkt + 1) * 128],
                                      pT[:, tkt * CH:(tkt + 1) * CH],
                                      start=(tkt == 0), stop=(tkt == 31))
                        ins = tensor.matmul(psum[3][:, :], oneb[:, :],
                                            pT[:, tkt * CH:(tkt + 1) * CH],
                                            start=(tkt == 0), stop=(tkt == 31))
                    inc(ins, "c_pe")
                block.tensor(_)
                Pctx = CNT["c_pe"]

                def _(vector, bpar=bpar, semx=semx, x1tag=x1tag, Pctx=Pctx):
                    wt(vector, "c_pe", Pctx)
                    wt(vector, "d_pk", state["x2_store_tag"])
                    wt(vector, "c_dve", state.get("epi_prev", 0))
                    t = inc(vector.reciprocal(uu[0][:, :], psum[3][:, :]), "c_dve")
                    wt(vector, "c_dve", t)
                    t = inc(vector.tensor_mul(uu[1][:, :], psum[2][:, :], uu[0][:, :]),
                            "c_dve")
                    wt(vector, semx, x1tag)
                    wt(vector, "c_dve", t)
                    inc(vector.tensor_add(uu[0][:, :], uu[1][:, :], sg[bpar][:, :]), "c_dve")
                block.vector(_)
                state[f"x1_free_{bpar}"] = CNT["c_dve"]
                state["epi_prev"] = CNT["c_dve"]
                x2ready = CNT["c_dve"]
                read_bank(2, "c_dve", x2ready)
                read_bank(3, "c_dve", x2ready)

                def _(sync, hd=hd, blk=blk, x2ready=x2ready):
                    wt(sync, "c_dve", x2ready)
                    inc(sync.dma_start(
                        out=xp2[hd * 128:(hd + 1) * 128, blk * CH:(blk + 1) * CH],
                        in_=uu[0][:, :]), "d_pk", 16)
                block.sync(_)
                state["x2_store_tag"] = CNT["d_pk"]

        state["att_done_pk"] = CNT["d_pk"]
        state["att_done_pe"] = CNT["c_pe"]

    ph_consts()
    if stage == 11:
        x_ready = ph_load_x_transposed(0, 0, 0)
        ph_ln(x_ready)
        ph_ffn(0)
        def _(sync):
            wt(sync, "c_dve", state["ffn_done_dve"])
            inc(sync.dma_start(out=dbg_xn, in_=xn[:, :]), "d_out", 16)
            inc(sync.dma_start(out=dbg_hb, in_=hb[:, :]), "d_out", 16)
            inc(sync.dma_start(out=dbg_x, in_=xch[:, :]), "d_out", 16)
            inc(sync.dma_start(out=out[0:128, :], in_=xstg[:, 0:D]), "d_out", 16)
            inc(sync.dma_start(out=out[200:201, 0:2048], in_=ostg[0:1, :]), "d_out", 16)
            inc(sync.dma_start(out=out[210:211, 0:512], in_=sg[1][0:1, :]), "d_out", 16)
            wt(sync, "d_out", CNT["d_out"])
        block.sync(_)
        blockctx.__exit__(None, None, None)
        es.close()
        return nc
    park_tag = 0
    xch_free_pe = 0
    for ch in range(NCH):
        x_ready = ph_load_x_transposed(ch, xch_free_pe, park_tag)
        ph_ln(x_ready)
        ph_ffn(state["gu_prev_free"])
        state["gu_prev_free"] = state["gu_free_pe"]
        if eff >= 2:
            ph_ln(lambda ht: ("c_dve", state["ffn_done_dve"]))
            ph_proj(ch)
            xch_free_pe = state["proj_last_pe"]
        else:
            xch_free_pe = state["dw_free_pe"]
        park_tag = ph_park(ch)
        if eff == 1:
            ph_out(ch)
            xch_free_pe = state["out_pe"]

    if eff >= 2:
        ph_attention()
        state["gu_prev_free"] = state["att_done_pe"]
        for ch in range(NCH):
            x_ready = ph_load_x_hmajor(ch, xp2, xch_free_pe)
            if eff >= 3:
                ph_ln(x_ready)
                ph_ffn(state["gu_prev_free"])
                state["gu_prev_free"] = state["gu_free_pe"]
            else:
                def _(vector, xr=x_ready):
                    s, v = xr(0)
                    wt(vector, s, v)
                    inc(vector.tensor_copy(out=sg[0][:, 0:8], in_=xch[:, 0:8]), "c_dve")
                block.vector(_)
                state["ffn_done_dve"] = CNT["c_dve"]
            ph_out(ch)
            xch_free_pe = state["out_pe"]

    def _(sync):
        if stage == 12:
            wt(sync, "d_pk", state["att_done_pk"])
            inc(sync.dma_start(out=dbg_q, in_=qpark[:, :]), "d_out", 16)
            inc(sync.dma_start(out=dbg_kv, in_=kv_recv[:, :]), "d_out", 16)
            inc(sync.dma_start(out=dbg_xp, in_=xp[:, :]), "d_out", 16)
        wt(sync, "d_out", CNT["d_out"])
        wt(sync, "d_pk", CNT["d_pk"])
    block.sync(_)

    if stage == 12:
        def _(vector):
            inc(vector.tensor_copy(out=sg[0][:, 0:16], in_=bexp[:, :]), "c_dve")
        block.vector(_)

        def _(sync):
            wt(sync, "c_dve", CNT["c_dve"])
            inc(sync.dma_start(out=dbg_bexp, in_=sg[0][:, 0:16]), "d_out", 16)
            wt(sync, "d_out", CNT["d_out"])
        block.sync(_)

    blockctx.__exit__(None, None, None)
    es.close()
    return nc



def _byte0(w):
    a = np.ascontiguousarray(w)
    assert a.dtype == np.int32
    return a.view(np.uint8).reshape(a.shape[0], a.shape[1], 4)[:, :, 0]


def kernel(L0_attn_hidden_states, L0_attn_Wq, L0_attn_bq, L0_attn_Wk, L0_attn_bk,
           L0_attn_Wv, L0_attn_bv, L0_ffn_gate_w, L0_ffn_gate_scale, L0_ffn_up_w,
           L0_ffn_up_scale, L0_ffn_down_w, L0_ffn_down_scale, _stage=3):
    bf = ml_dtypes.bfloat16
    x = np.ascontiguousarray(np.asarray(L0_attn_hidden_states, dtype=np.float32)).reshape(16384, D)
    wq = np.ascontiguousarray(np.asarray(L0_attn_Wq, dtype=np.float32))
    wk = np.ascontiguousarray(np.asarray(L0_attn_Wk, dtype=np.float32))
    wv = np.ascontiguousarray(np.asarray(L0_attn_Wv, dtype=np.float32))
    bq = np.ascontiguousarray(np.asarray(L0_attn_bq, dtype=np.float32).reshape(16, 128).T)
    bk = np.ascontiguousarray(np.asarray(L0_attn_bk, dtype=np.float32).reshape(4, 128).T)
    bv = np.ascontiguousarray(np.asarray(L0_attn_bv, dtype=np.float32).reshape(4, 128).T)
    gT = np.ascontiguousarray(_byte0(np.asarray(L0_ffn_gate_w)).T)
    uT = np.ascontiguousarray(_byte0(np.asarray(L0_ffn_up_w)).T)
    dT = np.ascontiguousarray(_byte0(np.asarray(L0_ffn_down_w)).T)
    g4 = np.ascontiguousarray(gT.reshape(16, 128, 32, 256).transpose(2, 1, 0, 3))
    u4 = np.ascontiguousarray(uT.reshape(16, 128, 32, 256).transpose(2, 1, 0, 3))
    d4 = np.ascontiguousarray(dT.reshape(64, 128, 16, 128).transpose(2, 1, 0, 3))
    gsc = np.ascontiguousarray(np.asarray(L0_ffn_gate_scale, np.float32).reshape(64, 128).T)
    usc = np.ascontiguousarray(np.asarray(L0_ffn_up_scale, np.float32).reshape(64, 128).T)
    dsc = np.ascontiguousarray(np.asarray(L0_ffn_down_scale, np.float32).reshape(16, 128).T)
    idf = np.eye(128, dtype=np.float32)
    idb = np.eye(128, dtype=np.float32).astype(bf)
    onef = np.ones((128, 128), np.float32)
    oneb = np.ones((128, 128), np.float32).astype(bf)
    mrow = np.full((1, 128), -128.0, np.float32).astype(bf)

    nc = build(stage=_stage)
    shared = {"wq": wq, "wk": wk, "wv": wv, "bq_in": bq, "bk_in": bk, "bv_in": bv,
              "g4": g4, "u4": u4, "d4": d4, "gsc_in": gsc, "usc_in": usc,
              "dsc_in": dsc, "idf_in": idf, "idb_in": idb, "onef_in": onef,
              "oneb_in": oneb, "mrow_in": mrow}
    in_maps = [dict(shared, x_in=np.ascontiguousarray(x[c * TPC:(c + 1) * TPC]))
               for c in range(8)]
    res = run_bass_kernel_spmd(nc, in_maps, core_ids=list(range(8)))
    full = np.concatenate([np.asarray(r["out"], np.float32) for r in res.results], axis=0)
    return full.reshape(4, 4096, D)


# version 07
# speedup vs baseline: 1.0581x; 1.0309x over previous
import contextlib
import numpy as np
import ml_dtypes
import concourse.bass as bass
import concourse.mybir as mybir
from concourse.bass_utils import run_bass_kernel_spmd

F32 = mybir.dt.float32
BF16 = mybir.dt.bfloat16
U8 = mybir.dt.uint8
AF = mybir.ActivationFunctionType
ALU = mybir.AluOpType

D = 2048
I = 8192
TPC = 2048
CH = 512
NCH = TPC // CH
NHT = D // 128
NIT = I // 128
HQ, HKV, HD = 16, 4, 128
SEQ = 4096
SCALE = 1.0 / float(np.sqrt(128.0))
EXP_OFF = 60.0

OFF_XCH = 0x04000
OFF_XN = 0x0C000
OFF_HB = 0x10000
OFF_KT = 0x10000
OFF_VT = 0x18000
OFF_XSTG = 0x10000
OFF_GW0 = 0x20000
OFF_GW1 = 0x22000
OFF_UW0 = 0x24000
OFF_UW1 = 0x26000
OFF_DW0 = 0x28000
OFF_DW1 = 0x2C000
OFF_PT = 0x20000
OFF_VTMP = 0x28000
OFF_QSQ = 0x28000
OFF_WQ0 = 0x30000
OFF_WQ1 = 0x31000
OFF_OSTG = 0x30000
OFF_IDF = 0x32000
OFF_IDB = 0x32200
OFF_ONEF = 0x32300
OFF_ONEB = 0x32500
OFF_MROW = 0x32600
OFF_BQ = 0x32700
OFF_BQS = 0x32740
OFF_BK = 0x32780
OFF_BV = 0x327A0
OFF_GSC = 0x327C0
OFF_USC = 0x328C0
OFF_DSC = 0x329C0
OFF_DSCH = 0x32A00
OFF_BEXP = 0x32A40
OFF_ROWSB = 0x32A80
OFF_ROWS = 0x33000
OFF_SG0 = 0x34800
OFF_SG1 = 0x35000
OFF_UU0 = 0x35800
OFF_UU1 = 0x36000
OFF_PST0 = 0x36800
OFF_PST1 = 0x36C00
OFF_SQ0 = 0x37000
OFF_SQ1 = 0x37400

SEM_NAMES = [
    "d_x", "d_c", "d_pk", "d_out", "d_kv",
    "d_q0", "d_q1", "d_gu0", "d_gu1", "d_dw0", "d_dw1", "d_wq0", "d_wq1",
    "d_qh0", "d_qh1", "d_x10", "d_x11",
    "s_cc", "c_pe", "c_act", "c_dve",
]


def build(stage=3):
    eff = 2 if stage == 12 else stage
    nc = bass.Bass("TRN2", target_bir_lowering=False)

    x_in = nc.dram_tensor("x_in", [TPC, D], F32, kind="ExternalInput").ap()
    wq = nc.dram_tensor("wq", [D, D], F32, kind="ExternalInput").ap()
    wk = nc.dram_tensor("wk", [D, 512], F32, kind="ExternalInput").ap()
    wv = nc.dram_tensor("wv", [D, 512], F32, kind="ExternalInput").ap()
    bq_in = nc.dram_tensor("bq_in", [128, 16], F32, kind="ExternalInput").ap()
    bk_in = nc.dram_tensor("bk_in", [128, 4], F32, kind="ExternalInput").ap()
    bv_in = nc.dram_tensor("bv_in", [128, 4], F32, kind="ExternalInput").ap()
    g4 = nc.dram_tensor("g4", [32, 128, 16, 256], U8, kind="ExternalInput").ap()
    u4 = nc.dram_tensor("u4", [32, 128, 16, 256], U8, kind="ExternalInput").ap()
    d4 = nc.dram_tensor("d4", [16, 128, 64, 128], U8, kind="ExternalInput").ap()
    gsc_in = nc.dram_tensor("gsc_in", [128, 64], F32, kind="ExternalInput").ap()
    usc_in = nc.dram_tensor("usc_in", [128, 64], F32, kind="ExternalInput").ap()
    dsc_in = nc.dram_tensor("dsc_in", [128, 16], F32, kind="ExternalInput").ap()
    idf_in = nc.dram_tensor("idf_in", [128, 128], F32, kind="ExternalInput").ap()
    idb_in = nc.dram_tensor("idb_in", [128, 128], BF16, kind="ExternalInput").ap()
    onef_in = nc.dram_tensor("onef_in", [128, 128], F32, kind="ExternalInput").ap()
    oneb_in = nc.dram_tensor("oneb_in", [128, 128], BF16, kind="ExternalInput").ap()
    mrow_in = nc.dram_tensor("mrow_in", [1, 128], BF16, kind="ExternalInput").ap()
    out = nc.dram_tensor("out", [TPC, D], F32, kind="ExternalOutput").ap()
    if stage == 11:
        dbg_xn = nc.dram_tensor("dbg_xn", [128, NHT * CH], BF16, kind="ExternalOutput").ap()
        dbg_hb = nc.dram_tensor("dbg_hb", [128, NIT * CH], BF16, kind="ExternalOutput").ap()
        dbg_x = nc.dram_tensor("dbg_x", [128, NHT * CH], F32, kind="ExternalOutput").ap()
    if stage == 12:
        dbg_q = nc.dram_tensor("dbg_q", [D, TPC], BF16, kind="ExternalOutput").ap()
        dbg_kv = nc.dram_tensor("dbg_kv", [2048, TPC], BF16, kind="ExternalOutput").ap()
        dbg_bexp = nc.dram_tensor("dbg_bexp", [128, 16], F32, kind="ExternalOutput").ap()
        dbg_xp = nc.dram_tensor("dbg_xp", [D, TPC], F32, kind="ExternalOutput").ap()
    xp = nc.dram_tensor("xp", [D, TPC], F32, kind="Internal").ap()
    xp2 = nc.dram_tensor("xp2", [D, TPC], F32, kind="Internal").ap()
    qpark = nc.dram_tensor("qpark", [D, TPC], BF16, kind="Internal").ap()
    kv_send = nc.dram_tensor("kv_send", [1024, TPC], BF16, kind="Internal").ap()
    kv_recv = nc.dram_tensor("kv_recv", [2048, TPC], BF16, kind="Internal").ap()

    SH = 0x80
    _MAP_END = 0x37800 + SH
    nc.alloc_sbuf_tensor("mapspacer", [128, _MAP_END - nc.sbuf_base], U8)
    assert nc.sbuf_base >= _MAP_END, hex(nc.sbuf_base)

    def A(name, shape, dtype, offset):
        return nc.alloc_sbuf_tensor_at(name, shape, dtype, offset=offset + SH)

    xch = A("xch", [128, NHT * CH], F32, offset=OFF_XCH)
    xn = A("xn", [128, NHT * CH], BF16, offset=OFF_XN)
    hb = A("hb", [128, NIT * CH], BF16, offset=OFF_HB)
    kT = A("kTs", [128, 4 * SEQ], BF16, offset=OFF_KT)
    v_t = A("v_ts", [128, 4 * SEQ], BF16, offset=OFF_VT)
    xstg = A("xstg", [128, 4 * D], F32, offset=OFF_XSTG)
    gw = [A("gw0", [128, 4096], BF16, offset=OFF_GW0),
          A("gw1", [128, 4096], BF16, offset=OFF_GW1)]
    uw = [A("uw0", [128, 4096], BF16, offset=OFF_UW0),
          A("uw1", [128, 4096], BF16, offset=OFF_UW1)]
    dw = [A("dw0", [128, 8192], BF16, offset=OFF_DW0),
          A("dw1", [128, 8192], BF16, offset=OFF_DW1)]
    pT = A("pTs", [128, 32 * CH], BF16, offset=OFF_PT)
    vtmp = A("vtmp", [128, 4 * SEQ], BF16, offset=OFF_VTMP)
    qsq = A("qsq", [128, SEQ], BF16, offset=OFF_QSQ)
    wqb = [A("wqb0", [128, 2048], BF16, offset=OFF_WQ0),
           A("wqb1", [128, 2048], BF16, offset=OFF_WQ1)]
    qh = wqb
    ostg = A("ostg", [128, D], F32, offset=OFF_OSTG)
    idf = A("idf", [128, 128], F32, offset=OFF_IDF)
    idb = A("idb", [128, 128], BF16, offset=OFF_IDB)
    onef = A("onef", [128, 128], F32, offset=OFF_ONEF)
    oneb = A("oneb", [128, 128], BF16, offset=OFF_ONEB)
    mrow = A("mrow", [1, 128], BF16, offset=OFF_MROW)
    bq_s = A("bq_s", [128, 16], F32, offset=OFF_BQ)
    bqs = A("bqs", [128, 16], F32, offset=OFF_BQS)
    bk_s = A("bk_s", [128, 4], F32, offset=OFF_BK)
    bv_s = A("bv_s", [128, 4], F32, offset=OFF_BV)
    gsc = A("gsc", [128, 64], F32, offset=OFF_GSC)
    usc = A("usc", [128, 64], F32, offset=OFF_USC)
    dsc = A("dsc", [128, 16], F32, offset=OFF_DSC)
    dsch = A("dsch", [128, 16], F32, offset=OFF_DSCH)
    bexp = A("bexp", [128, 16], F32, offset=OFF_BEXP)
    rowsb = A("rowsb", [128, 512], BF16, offset=OFF_ROWSB)
    qmx = A("qmx", [128, 16], F32, offset=OFF_ROWSB + 0x400)
    rows = A("rows", [128, 1536], F32, offset=OFF_ROWS)
    sg = [A("sg0", [128, CH], F32, offset=OFF_SG0),
          A("sg1", [128, CH], F32, offset=OFF_SG1)]
    uu = [A("uu0", [128, CH], F32, offset=OFF_UU0),
          A("uu1", [128, CH], F32, offset=OFF_UU1)]
    pst = [A("pst0", [128, CH], BF16, offset=OFF_PST0),
           A("pst1", [128, CH], BF16, offset=OFF_PST1)]
    sqst = [A("sq0", [128, CH], BF16, offset=OFF_SQ0),
            A("sq1", [128, CH], BF16, offset=OFF_SQ1)]

    es = contextlib.ExitStack()
    psum = [es.enter_context(nc.psum_tensor(f"pb{i}", [128, 512], F32)) for i in range(7)]
    psb = es.enter_context(nc.psum_tensor("pb7", [128, 1024], BF16))
    SEM = {n: es.enter_context(nc.semaphore(n)) for n in SEM_NAMES}
    CNT = {n: 0 for n in SEM_NAMES}
    LASTW = {}
    BANKR = [{"c_act": 0, "c_dve": 0} for _ in range(8)]

    def inc(ins, sem, k=1):
        ins.then_inc(SEM[sem], k)
        CNT[sem] += k
        return CNT[sem]

    def wt(eng, sem, val):
        if val is None or val <= 0:
            return
        key = (id(eng), sem)
        if LASTW.get(key, -1) < val:
            eng.wait_ge(SEM[sem], val)
            LASTW[key] = val

    def bank_guard(tensor, b):
        wt(tensor, "c_act", BANKR[b]["c_act"])
        wt(tensor, "c_dve", BANKR[b]["c_dve"])

    def read_bank(b, sem, tag):
        BANKR[b][sem] = max(BANKR[b][sem], tag)

    def xch_t(ht):
        return xch[:, ht * CH:(ht + 1) * CH]

    def xn_t(ht):
        return xn[:, ht * CH:(ht + 1) * CH]

    def hb_t(it):
        return hb[:, it * CH:(it + 1) * CH]

    def row(i):
        return rows[0:1, i * 512:(i + 1) * 512]

    def rcell(c0, n=1):
        return rows[0:1, c0:c0 + n]

    blockctx = nc.Block()
    block = blockctx.__enter__()
    state = {"xn_ready": 0, "ffn_done_dve": 0, "wq_free_pe": 0,
             "gu_prev_free": 0, "x2_store_tag": 0, "scores_done_pe": 0}

    def ph_consts():
        def _(sync):
            for dst, src in [(idf, idf_in), (idb, idb_in), (onef, onef_in),
                             (oneb, oneb_in), (bq_s, bq_in), (bk_s, bk_in),
                             (bv_s, bv_in), (gsc, gsc_in), (usc, usc_in),
                             (dsc, dsc_in)]:
                inc(sync.dma_start(out=dst[:, :], in_=src), "d_c", 16)
            inc(sync.dma_start(out=mrow[0:1, :], in_=mrow_in), "d_c", 16)
        block.sync(_)
        state["consts"] = CNT["d_c"]

        def _(vector):
            wt(vector, "d_c", state["consts"])
            inc(vector.tensor_scalar_mul(bqs[:, :], bq_s[:, :], SCALE), "c_dve")
            inc(vector.tensor_scalar_mul(dsch[:, :], dsc[:, :], 0.5), "c_dve")
            inc(vector.memset(qmx[:, :], 0.0), "c_dve")
        block.vector(_)
        state["consts_dve"] = CNT["c_dve"]

    def ph_ln(x_ready):
        A0 = CNT["c_act"]
        P0 = CNT["c_pe"]

        def _(scalar):
            for ht in range(NHT):
                s, v = x_ready(ht)
                wt(scalar, s, v)
                if ht >= 2:
                    wt(scalar, "c_pe", P0 + 2 * (ht - 2) + 2)
                inc(scalar.activation(out=xn_t(ht), in_=xch_t(ht), func=AF.Copy), "c_act")
                inc(scalar.activation(out=sqst[ht % 2][:, :], in_=xch_t(ht),
                                      func=AF.Square), "c_act")
        block.scalar(_)

        def _(tensor):
            wt(tensor, "d_c", state["consts"])
            s, v = x_ready(NHT - 1)
            wt(tensor, s, v)
            bank_guard(tensor, 6)
            bank_guard(tensor, 4)
            for ht in range(NHT):
                wt(tensor, "c_act", A0 + 2 * ht + 1)
                inc(tensor.matmul(psum[6][0:1, :], oneb[:, 0:1], xn_t(ht),
                                  start=(ht == 0), stop=(ht == NHT - 1)), "c_pe")
                wt(tensor, "c_act", A0 + 2 * ht + 2)
                inc(tensor.matmul(psum[4][0:1, :], oneb[:, 0:1], sqst[ht % 2][:, :],
                                  start=(ht == 0), stop=(ht == NHT - 1)), "c_pe")
        block.tensor(_)
        pe_stats = CNT["c_pe"]
        assert pe_stats == P0 + 2 * NHT

        def _(vector):
            wt(vector, "c_pe", pe_stats)
            if stage == 11:
                inc(vector.tensor_copy(out=ostg[0:1, 0:512], in_=psum[6][0:1, :]), "c_dve")
                inc(vector.tensor_copy(out=ostg[0:1, 512:1024], in_=psum[4][0:1, :]), "c_dve")
            t1 = inc(vector.tensor_scalar_mul(row(0), psum[6][0:1, :], 1.0 / D), "c_dve")
            t2 = inc(vector.tensor_scalar_mul(row(1), psum[4][0:1, :], 1.0 / D), "c_dve")
            wt(vector, "c_dve", t1)
            t3 = inc(vector.tensor_mul(row(2), row(0), row(0)), "c_dve")
            wt(vector, "c_dve", t3)
            inc(vector.scalar_tensor_tensor(out=row(1), in0=row(1), scalar=1e-5,
                                            in1=row(2), op0=ALU.add, op1=ALU.subtract),
                "c_dve")
        block.vector(_)
        dve_var = CNT["c_dve"]
        read_bank(6, "c_dve", dve_var)
        read_bank(4, "c_dve", dve_var)
        if stage == 11:
            def _(vector):
                wt(vector, "c_dve", dve_var)
                inc(vector.tensor_copy(out=ostg[0:1, 1024:1536], in_=row(1)), "c_dve")
            block.vector(_)
            dve_var = CNT["c_dve"]

        def _(scalar):
            wt(scalar, "c_dve", dve_var)
            a1 = inc(scalar.activation(out=row(2), in_=row(1), func=AF.Ln), "c_act")
            wt(scalar, "c_act", a1)
            inc(scalar.activation(out=row(1), in_=row(2), func=AF.Exp, scale=-0.5), "c_act")
        block.scalar(_)
        if stage == 11:
            def _(vector):
                wt(vector, "c_act", CNT["c_act"])
                inc(vector.tensor_copy(out=sg[1][0:1, :], in_=row(2)), "c_dve")
            block.vector(_)
        act_rstd = CNT["c_act"]
        if stage == 11:
            def _(vector):
                wt(vector, "c_act", act_rstd)
                inc(vector.tensor_copy(out=ostg[0:1, 1536:2048], in_=row(1)), "c_dve")
            block.vector(_)

        def _(vector):
            wt(vector, "c_act", act_rstd)
            inc(vector.scalar_tensor_tensor(out=row(2), in0=row(0), scalar=-1.0,
                                            in1=row(1), op0=ALU.mult, op1=ALU.mult), "c_dve")
        block.vector(_)
        dve_nmr = CNT["c_dve"]

        def _(tensor):
            wt(tensor, "c_dve", dve_nmr)
            bank_guard(tensor, 6)
            bank_guard(tensor, 4)
            inc(tensor.matmul(psum[6][:, :], onef[0:1, :], row(1), start=True, stop=True), "c_pe")
            inc(tensor.matmul(psum[4][:, :], onef[0:1, :], row(2), start=True, stop=True), "c_pe")
        block.tensor(_)
        pe_bc = CNT["c_pe"]

        def _(vector):
            wt(vector, "c_pe", pe_bc)
            prev_add = 0
            tags = []
            for ht in range(NHT):
                wt(vector, "c_dve", prev_add)
                t = inc(vector.tensor_mul(sg[0][:, :], xch_t(ht), psum[6][:, :]), "c_dve")
                wt(vector, "c_dve", t)
                prev_add = inc(vector.tensor_add(xn_t(ht), sg[0][:, :], psum[4][:, :]), "c_dve")
                tags.append(prev_add)
            state["xn_tags"] = tags
        block.vector(_)
        state["xn_ready"] = CNT["c_dve"]
        read_bank(6, "c_dve", state["xn_ready"])
        read_bank(4, "c_dve", state["xn_ready"])

    def ph_ffn(warena_free_pe):
        NB = 32
        P0 = CNT["c_pe"]
        A0 = CNT["c_act"]
        V0 = CNT["c_dve"]
        gu_base = [CNT["d_gu0"], CNT["d_gu1"]]

        def _pe_before(it):
            return 2 * it + max(0, it - 2)

        def tag_g(it):
            return P0 + _pe_before(it) + 1

        def tag_u(it):
            return P0 + _pe_before(it) + 2

        def tag_silu(it):
            return A0 + it + 1

        def tag_uu(it):
            return V0 + 2 * it + 1

        def tag_h(it):
            return V0 + 2 * it + 2

        def _(gpsimd):
            for ib in range(NB):
                if ib >= 2:
                    wt(gpsimd, "c_pe", tag_u(2 * ib - 3))
                else:
                    wt(gpsimd, "c_pe", warena_free_pe)
                sem = "d_gu0" if ib % 2 == 0 else "d_gu1"
                inc(gpsimd.dma_start(
                    out=gw[ib % 2][:].rearrange("p (n c) -> p n c", c=256), in_=g4[ib]), sem, 16)
                inc(gpsimd.dma_start(
                    out=uw[ib % 2][:].rearrange("p (n c) -> p n c", c=256), in_=u4[ib]), sem, 16)
        block.gpsimd(_)

        xtags = state["xn_tags"]

        def _(tensor):
            bank_guard(tensor, 6)
            for it in range(2 * NB):
                ib = it // 2
                sem = "d_gu0" if ib % 2 == 0 else "d_gu1"
                wt(tensor, sem, gu_base[ib % 2] + 32 * (ib // 2 + 1))
                if it >= 2:
                    wt(tensor, "c_act", tag_silu(it - 2))
                    wt(tensor, "c_dve", tag_uu(it - 2))
                else:
                    bank_guard(tensor, it % 2)
                    bank_guard(tensor, 2 + it % 2)
                sl = (it % 2) * 128
                for ht in range(NHT):
                    wt(tensor, "c_dve", xtags[ht])
                    ins = tensor.matmul(
                        psum[it % 2][:, :],
                        gw[ib % 2][:, ht * 256 + sl:ht * 256 + sl + 128],
                        xn_t(ht), start=(ht == 0), stop=(ht == NHT - 1))
                inc(ins, "c_pe")
                for ht in range(NHT):
                    ins = tensor.matmul(
                        psum[2 + it % 2][:, :],
                        uw[ib % 2][:, ht * 256 + sl:ht * 256 + sl + 128],
                        xn_t(ht), start=(ht == 0), stop=(ht == NHT - 1))
                inc(ins, "c_pe")
                if it >= 2:
                    wt(tensor, "c_dve", tag_h(it - 2))
                    inc(tensor.matmul(psum[6][0:1, :], oneb[:, 0:1], hb_t(it - 2),
                                      start=(it == 2), stop=False), "c_pe")
        block.tensor(_)

        def _(tensor):
            for it in range(2 * NB - 2, 2 * NB):
                wt(tensor, "c_dve", tag_h(it))
                inc(tensor.matmul(psum[6][0:1, :], oneb[:, 0:1], hb_t(it),
                                  start=False, stop=(it == 2 * NB - 1)), "c_pe")
        block.tensor(_)
        assert CNT["c_pe"] == P0 + 4 * NB + (2 * NB - 2) + 2, (CNT["c_pe"], P0)

        def _(scalar):
            wt(scalar, "d_c", state["consts"])
            for it in range(2 * NB):
                wt(scalar, "c_pe", tag_g(it))
                if it >= 2:
                    wt(scalar, "c_dve", tag_h(it - 2))
                inc(scalar.activation(out=sg[it % 2][:, :], in_=psum[it % 2][:, :],
                                      func=AF.Silu, scale=gsc[:, it:it + 1]), "c_act")
        block.scalar(_)
        assert CNT["c_act"] == A0 + 2 * NB

        def _(vector):
            wt(vector, "d_c", state["consts"])
            for it in range(2 * NB):
                wt(vector, "c_pe", tag_u(it))
                if it >= 2:
                    wt(vector, "c_dve", tag_h(it - 2))
                t = inc(vector.tensor_scalar_mul(uu[it % 2][:, :], psum[2 + it % 2][:, :],
                                                 usc[:, it:it + 1]), "c_dve")
                wt(vector, "c_act", tag_silu(it))
                wt(vector, "c_dve", t)
                inc(vector.tensor_mul(hb_t(it), sg[it % 2][:, :], uu[it % 2][:, :]), "c_dve")
        block.vector(_)
        assert CNT["c_dve"] == V0 + 4 * NB
        for b in range(4):
            read_bank(b, "c_act", A0 + 2 * NB)
            read_bank(b, "c_dve", V0 + 4 * NB)

        pe_csh = CNT["c_pe"]

        def _(vector):
            wt(vector, "c_pe", pe_csh)
            t1 = inc(vector.tensor_copy(out=row(0), in_=psum[6][0:1, :]), "c_dve")
            wt(vector, "c_dve", t1)
            t2 = inc(vector.tensor_copy(out=rowsb[0:1, :], in_=row(0)), "c_dve")
            wt(vector, "c_dve", t2)
            inc(vector.tensor_sub(sqst[0][0:1, :], row(0), rowsb[0:1, :]), "c_dve")
        block.vector(_)
        csh_ready = CNT["c_dve"]
        read_bank(6, "c_dve", csh_ready)

        Pd0 = CNT["c_pe"]
        Vd0 = CNT["c_dve"]
        dw_base = [CNT["d_dw0"], CNT["d_dw1"]]

        def tag_d(hht):
            return Pd0 + hht + 1

        def tag_stt(hht):
            return Vd0 + hht + 1

        def _(gpsimd):
            for hht in range(NHT):
                if hht >= 2:
                    wt(gpsimd, "c_pe", tag_d(hht - 2))
                sem = "d_dw0" if hht % 2 == 0 else "d_dw1"
                inc(gpsimd.dma_start(
                    out=dw[hht % 2][:].rearrange("p (n c) -> p n c", c=128), in_=d4[hht]), sem, 16)
        block.gpsimd(_)

        def _(tensor):
            wt(tensor, "c_dve", csh_ready)
            bank_guard(tensor, 4)
            bank_guard(tensor, 5)
            for hht in range(NHT):
                sem = "d_dw0" if hht % 2 == 0 else "d_dw1"
                wt(tensor, sem, dw_base[hht % 2] + 16 * (hht // 2 + 1))
                if hht >= 2:
                    wt(tensor, "c_dve", tag_stt(hht - 2))
                tensor.matmul(psum[4 + hht % 2][:, :], mrow[0:1, :], rowsb[0:1, :],
                              start=True, stop=False)
                tensor.matmul(psum[4 + hht % 2][:, :], mrow[0:1, :], sqst[0][0:1, :],
                              start=False, stop=False)
                for it in range(NIT):
                    ins = tensor.matmul(
                        psum[4 + hht % 2][:, :],
                        dw[hht % 2][:, it * 128:(it + 1) * 128],
                        hb_t(it), start=False, stop=(it == NIT - 1))
                inc(ins, "c_pe")
        block.tensor(_)
        assert CNT["c_pe"] == Pd0 + NHT

        def _(vector):
            for hht in range(NHT):
                wt(vector, "c_pe", tag_d(hht))
                inc(vector.scalar_tensor_tensor(
                    out=xch_t(hht), in0=psum[4 + hht % 2][:, :], scalar=dsch[:, hht:hht + 1],
                    in1=xch_t(hht), op0=ALU.mult, op1=ALU.add), "c_dve")
        block.vector(_)
        assert CNT["c_dve"] == Vd0 + NHT
        state["ffn_done_dve"] = CNT["c_dve"]
        read_bank(4, "c_dve", Vd0 + NHT)
        read_bank(5, "c_dve", Vd0 + NHT)
        state["gu_free_pe"] = tag_u(2 * NB - 1)
        state["dw_free_pe"] = CNT["c_pe"]

    def ph_load_x_transposed(ch, xch_free_pe, park_tag):
        def _(sync):
            wt(sync, "c_pe", xch_free_pe)
            wt(sync, "d_pk", park_tag)
            for tt in range(4):
                inc(sync.dma_start(out=xstg[:, tt * D:(tt + 1) * D],
                                   in_=x_in[ch * CH + tt * 128: ch * CH + (tt + 1) * 128, :]),
                    "d_x", 16)
        block.sync(_)
        xload = CNT["d_x"]
        P0 = CNT["c_pe"]
        V0 = CNT["c_dve"]

        def _(tensor):
            wt(tensor, "d_c", state["consts"])
            wt(tensor, "d_x", xload)
            bank_guard(tensor, 4)
            bank_guard(tensor, 5)
            for ht in range(NHT):
                if ht >= 2:
                    wt(tensor, "c_dve", V0 + ht - 1)
                for tt in range(4):
                    ins = tensor.transpose(
                        psum[4 + ht % 2][:, tt * 128:(tt + 1) * 128],
                        xstg[:, tt * D + ht * 128: tt * D + (ht + 1) * 128],
                        idf[:, :])
                inc(ins, "c_pe")
        block.tensor(_)

        def _(vector):
            for ht in range(NHT):
                wt(vector, "c_pe", P0 + ht + 1)
                inc(vector.tensor_copy(out=xch_t(ht), in_=psum[4 + ht % 2][:, :]), "c_dve")
        block.vector(_)
        read_bank(4, "c_dve", V0 + NHT)
        read_bank(5, "c_dve", V0 + NHT)
        return lambda ht: ("c_dve", V0 + ht + 1)

    def ph_load_x_hmajor(ch, src, xch_free_pe):
        def _(sync):
            wt(sync, "c_pe", xch_free_pe)
            wt(sync, "d_pk", state["att_done_pk"])
            inc(sync.dma_start(
                out=xch[:].rearrange("p (n c) -> p n c", c=CH),
                in_=src.rearrange("(n p) t -> p n t", p=128)[:, :, ch * CH:(ch + 1) * CH]),
                "d_x", 16)
        block.sync(_)
        xload = CNT["d_x"]
        return lambda ht: ("d_x", xload)

    def ph_proj(ch):
        NS = 24
        P0 = CNT["c_pe"]
        wq_base = [CNT["d_wq0"], CNT["d_wq1"]]
        q_base = [CNT["d_q0"], CNT["d_q1"]]

        def _pe_before(s):
            return s + min(max(0, s - 2), 16)

        def tag_p(s):
            return P0 + _pe_before(s) + 1

        def tag_nm(n):
            return P0 + _pe_before(n + 2) + 2

        def _(gpsimd):
            for s in range(NS):
                if s >= 2:
                    wt(gpsimd, "c_pe", tag_p(s - 2))
                else:
                    wt(gpsimd, "c_pe", state["wq_free_pe"])
                sem = "d_wq0" if s % 2 == 0 else "d_wq1"
                if s < 16:
                    src = wq.rearrange("(n p) c -> p n c", p=128)[:, :, s * 128:(s + 1) * 128]
                elif s < 20:
                    dt = s - 16
                    src = wk.rearrange("(n p) c -> p n c", p=128)[:, :, dt * 128:(dt + 1) * 128]
                else:
                    dt = s - 20
                    src = wv.rearrange("(n p) c -> p n c", p=128)[:, :, dt * 128:(dt + 1) * 128]
                inc(gpsimd.dma_start(
                    out=wqb[s % 2][:].rearrange("p (n c) -> p n c", c=128), in_=src), sem, 16)
        block.gpsimd(_)

        V0 = CNT["c_dve"]

        def _v_incs(s):
            n = 0
            for t in range(s):
                n += 1
                if t < 16:
                    n += 1
                if 2 <= t < 18:
                    n += 2
            return n

        def tag_e(s):
            return V0 + _v_incs(s) + 1

        def tag_sq(s):
            return tag_e(s) + 1

        def tag_red(n):
            return tag_e(n + 2) + 1 + (1 if (n + 2) < 16 else 0)

        xtags = state["xn_tags"]

        def _(tensor):
            bank_guard(tensor, 6)
            for s in range(NS):
                sem = "d_wq0" if s % 2 == 0 else "d_wq1"
                wt(tensor, sem, wq_base[s % 2] + 16 * (s // 2 + 1))
                if s >= 2:
                    wt(tensor, "c_dve", tag_e(s - 2))
                else:
                    bank_guard(tensor, s % 2)
                for ht in range(NHT):
                    wt(tensor, "c_dve", xtags[ht])
                    ins = tensor.matmul(
                        psum[s % 2][:, :],
                        wqb[s % 2][:, ht * 128:(ht + 1) * 128],
                        xn_t(ht), start=(ht == 0), stop=(ht == NHT - 1))
                inc(ins, "c_pe")
                assert CNT["c_pe"] == tag_p(s), (s, CNT["c_pe"], tag_p(s))
                if s >= 2 and s - 2 < 16:
                    n = s - 2
                    wt(tensor, "c_dve", tag_sq(n))
                    if n >= 1:
                        wt(tensor, "c_dve", tag_red(n - 1))
                    inc(tensor.matmul(psum[6][0:1, :], oneb[:, 0:1], sqst[n % 2][:, :],
                                      start=True, stop=True), "c_pe")
        block.tensor(_)

        def tag_ev(s):
            return tag_e(s)

        def _(vector):
            wt(vector, "c_dve", state["consts_dve"])
            for s in range(NS):
                wt(vector, "c_pe", tag_p(s))
                semd = "d_q0" if s % 2 == 0 else "d_q1"
                wt(vector, semd, q_base[s % 2] + 16 * (s // 2))
                if s < 16:
                    e = inc(vector.tensor_scalar(out=pst[s % 2][:, :], in0=psum[s % 2][:, :],
                                                 scalar1=SCALE, scalar2=bqs[:, s:s + 1],
                                                 op0=ALU.mult, op1=ALU.add), "c_dve")
                elif s < 20:
                    inc(vector.tensor_scalar_add(pst[s % 2][:, :], psum[s % 2][:, :],
                                                 bk_s[:, s - 16:s - 15]), "c_dve")
                else:
                    inc(vector.tensor_scalar_add(pst[s % 2][:, :], psum[s % 2][:, :],
                                                 bv_s[:, s - 20:s - 19]), "c_dve")
                assert CNT["c_dve"] == tag_e(s), (s, CNT["c_dve"], tag_e(s))
                if s < 16:
                    wt(vector, "c_dve", e)
                    wt(vector, "c_pe", tag_nm(s - 2) if s >= 2 else 0)
                    inc(vector.tensor_mul(sqst[s % 2][:, :], pst[s % 2][:, :],
                                          pst[s % 2][:, :]), "c_dve")
                if 2 <= s < 18:
                    n = s - 2
                    wt(vector, "c_pe", tag_nm(n))
                    if n >= 1:
                        wt(vector, "c_dve", tag_red(n - 1) + 1)
                    t = inc(vector.reduce_max(qmx[0:1, 1:2], psum[6][0:1, :],
                                              axis=mybir.AxisListType.X), "c_dve")
                    wt(vector, "c_dve", t)
                    inc(vector.tensor_max(qmx[0:1, 0:1], qmx[0:1, 0:1],
                                          qmx[0:1, 1:2]), "c_dve")
        block.vector(_)
        read_bank(6, "c_dve", CNT["c_dve"])

        def _(scalar):
            for s in range(NS):
                wt(scalar, "c_dve", tag_ev(s))
                sem = "d_q0" if s % 2 == 0 else "d_q1"
                if s < 16:
                    dst = qpark[s * 128:(s + 1) * 128, ch * CH:(ch + 1) * CH]
                elif s < 20:
                    dst = kv_send[(s - 16) * 128:(s - 15) * 128, ch * CH:(ch + 1) * CH]
                else:
                    dst = kv_send[512 + (s - 20) * 128:512 + (s - 19) * 128,
                                  ch * CH:(ch + 1) * CH]
                inc(scalar.dma_start(out=dst, in_=pst[s % 2][:, :]), sem, 16)
        block.scalar(_)
        read_bank(0, "c_dve", tag_ev(NS - 2))
        read_bank(1, "c_dve", tag_ev(NS - 1))
        state["wq_free_pe"] = tag_p(NS - 1)
        state["proj_last_pe"] = tag_p(NS - 1)

    def ph_park(ch):
        def _(sync):
            wt(sync, "c_dve", state["ffn_done_dve"])
            inc(sync.dma_start(
                out=xp.rearrange("(n p) t -> p n t", p=128)[:, :, ch * CH:(ch + 1) * CH],
                in_=xch[:].rearrange("p (n c) -> p n c", c=CH)), "d_pk", 16)
        block.sync(_)
        return CNT["d_pk"]

    def ph_out(ch):
        P0 = CNT["c_pe"]
        V0 = CNT["c_dve"]
        O0 = CNT["d_out"]

        def _(tensor):
            wt(tensor, "d_c", state["consts"])
            wt(tensor, "c_dve", state["ffn_done_dve"])
            bank_guard(tensor, 4)
            bank_guard(tensor, 5)
            gi = 0
            for tt in range(4):
                for grp in range(4):
                    if gi >= 2:
                        wt(tensor, "c_dve", V0 + gi - 1)
                    bank = 4 + gi % 2
                    for j in range(4):
                        ht = grp * 4 + j
                        ins = tensor.transpose(
                            psum[bank][:, j * 128:(j + 1) * 128],
                            xch_t(ht)[:, tt * 128:(tt + 1) * 128],
                            idf[:, :])
                    inc(ins, "c_pe")
                    gi += 1
        block.tensor(_)

        def _(vector):
            gi = 0
            for tt in range(4):
                wt(vector, "d_out", O0 + 16 * tt)
                for grp in range(4):
                    wt(vector, "c_pe", P0 + gi + 1)
                    inc(vector.tensor_copy(out=ostg[:, grp * 512:(grp + 1) * 512],
                                           in_=psum[4 + gi % 2][:, :]), "c_dve")
                    gi += 1
        block.vector(_)

        def _(sync):
            for tt in range(4):
                wt(sync, "c_dve", V0 + 4 * tt + 4)
                inc(sync.dma_start(out=out[ch * CH + tt * 128: ch * CH + (tt + 1) * 128, :],
                                   in_=ostg[:, :]), "d_out", 16)
        block.sync(_)
        read_bank(4, "c_dve", V0 + 16)
        read_bank(5, "c_dve", V0 + 16)
        state["out_pe"] = CNT["c_pe"]
        state["out_last_dve"] = CNT["c_dve"]

    def ph_attention():
        def _(gpsimd):
            wt(gpsimd, "d_q0", CNT["d_q0"])
            wt(gpsimd, "d_q1", CNT["d_q1"])
            inc(gpsimd.collective_compute(
                "AllGather", ALU.bypass,
                replica_groups=[[0, 1], [2, 3], [4, 5], [6, 7]],
                ins=[kv_send[:, :]], outs=[kv_recv[:, :]]), "s_cc", 1)
        block.gpsimd(_)

        qh_base = [CNT["d_qh0"], CNT["d_qh1"]]

        def _(sync):
            wt(sync, "s_cc", CNT["s_cc"])
            inc(sync.dma_start(
                out=kT[:].rearrange("p (g t) -> p g t", t=SEQ)[:, :, 0:TPC],
                in_=kv_recv[0:512, :].rearrange("(g p) t -> p g t", p=128)), "d_kv", 16)
            inc(sync.dma_start(
                out=kT[:].rearrange("p (g t) -> p g t", t=SEQ)[:, :, TPC:SEQ],
                in_=kv_recv[1024:1536, :].rearrange("(g p) t -> p g t", p=128)), "d_kv", 16)
            inc(sync.dma_start(
                out=vtmp[:].rearrange("p (g t) -> p g t", t=SEQ)[:, :, 0:TPC],
                in_=kv_recv[512:1024, :].rearrange("(g p) t -> p g t", p=128)), "d_kv", 16)
            inc(sync.dma_start(
                out=vtmp[:].rearrange("p (g t) -> p g t", t=SEQ)[:, :, TPC:SEQ],
                in_=kv_recv[1536:2048, :].rearrange("(g p) t -> p g t", p=128)), "d_kv", 16)
            inc(sync.dma_start(out=qh[0][:, :], in_=qpark[0:128, :]), "d_qh0", 16)
        block.sync(_)
        kvload = CNT["d_kv"]

        P0 = CNT["c_pe"]
        V0 = CNT["c_dve"]

        def _(tensor):
            wt(tensor, "d_c", state["consts"])
            wt(tensor, "d_kv", kvload)
            for grp in range(16):
                if grp >= 1:
                    wt(tensor, "c_dve", V0 + 8 * grp)
                for j in range(8):
                    tile = grp * 8 + j
                    g, tkt = tile // 32, tile % 32
                    ins = tensor.transpose(
                        psb[:, j * 128:(j + 1) * 128],
                        vtmp[:, g * SEQ + tkt * 128: g * SEQ + (tkt + 1) * 128],
                        idb[:, :])
                inc(ins, "c_pe")
        block.tensor(_)

        def _(vector):
            for grp in range(16):
                wt(vector, "c_pe", P0 + grp + 1)
                for j in range(8):
                    tile = grp * 8 + j
                    g, tkt = tile // 32, tile % 32
                    inc(vector.tensor_copy(
                        out=v_t[:, g * SEQ + tkt * 128:g * SEQ + (tkt + 1) * 128],
                        in_=psb[:, j * 128:(j + 1) * 128]), "c_dve")
        block.vector(_)
        vt_done = CNT["c_dve"]
        read_bank(7, "c_dve", vt_done)

        prev_mm = 0
        for g in range(4):
            Vs = CNT["c_dve"]

            def _(vector, g=g, prev_mm=prev_mm):
                wt(vector, "c_dve", vt_done)
                wt(vector, "c_pe", prev_mm)
                inc(vector.tensor_mul(qsq[:, :], kT[:, g * SEQ:(g + 1) * SEQ],
                                      kT[:, g * SEQ:(g + 1) * SEQ]), "c_dve")
            block.vector(_)
            Pn = CNT["c_pe"]
            Vr0 = CNT["c_dve"]

            def _(tensor, Vs=Vs, Vr0=Vr0):
                wt(tensor, "c_dve", Vs + 1)
                bank_guard(tensor, 6)
                for q4 in range(8):
                    if q4 >= 1:
                        wt(tensor, "c_dve", Vr0 + q4)
                    inc(tensor.matmul(psum[6][0:1, :], oneb[:, 0:1],
                                      qsq[:, q4 * 512:(q4 + 1) * 512],
                                      start=True, stop=True), "c_pe")
            block.tensor(_)
            prev_mm = CNT["c_pe"]

            def _(vector, g=g, Pn=Pn):
                t = 0
                for q4 in range(8):
                    wt(vector, "c_pe", Pn + q4 + 1)
                    t = inc(vector.reduce_max(rcell(1024 + q4), psum[6][0:1, :],
                                              axis=mybir.AxisListType.X), "c_dve")
                wt(vector, "c_dve", t)
                inc(vector.reduce_max(rcell(1045 + g), rows[0:1, 1024:1032],
                                      axis=mybir.AxisListType.X), "c_dve")
            block.vector(_)
            read_bank(6, "c_dve", CNT["c_dve"])
        knorm_done = CNT["c_dve"]

        def _(vector):
            wt(vector, "c_dve", knorm_done)
            t = inc(vector.reduce_max(rcell(1049), rows[0:1, 1045:1049],
                                      axis=mybir.AxisListType.X), "c_dve")
            wt(vector, "c_dve", t)
            t = inc(vector.tensor_mul(rcell(1050), rcell(1049), qmx[0:1, 0:1]), "c_dve")
        block.vector(_)
        Vb = CNT["c_dve"]

        def _(scalar):
            wt(scalar, "c_dve", Vb)
            a = inc(scalar.activation(out=rcell(1051), in_=rcell(1050), func=AF.Ln),
                    "c_act")
            wt(scalar, "c_act", a)
            inc(scalar.activation(out=rcell(1052), in_=rcell(1051), func=AF.Exp,
                                  scale=0.5), "c_act")
        block.scalar(_)
        Ab = CNT["c_act"]

        def _(vector):
            wt(vector, "c_act", Ab)
            inc(vector.tensor_scalar(out=rcell(1053), in0=rcell(1052),
                                     scalar1=-1.001, scalar2=EXP_OFF,
                                     op0=ALU.mult, op1=ALU.add), "c_dve")
        block.vector(_)
        Vb2 = CNT["c_dve"]

        def _(tensor):
            wt(tensor, "c_dve", Vb2)
            bank_guard(tensor, 6)
            inc(tensor.matmul(psum[6][:, 0:1], onef[0:1, :], rcell(1053),
                              start=True, stop=True), "c_pe")
        block.tensor(_)
        Pb = CNT["c_pe"]

        def _(vector):
            wt(vector, "c_pe", Pb)
            inc(vector.tensor_copy(out=bexp[:, 0:1], in_=psum[6][:, 0:1]), "c_dve")
        block.vector(_)
        bexp_done = CNT["c_dve"]
        read_bank(6, "c_dve", bexp_done)

        for hd in range(HQ):
            g = hd % HKV
            par = hd % 2
            semq = "d_qh0" if par == 0 else "d_qh1"
            qtag = qh_base[par] + 16 * (hd // 2 + 1)

            if hd + 1 < HQ:
                def _(sync, hd=hd):
                    wt(sync, "c_pe", state["scores_done_pe"])
                    sem2 = "d_qh0" if (hd + 1) % 2 == 0 else "d_qh1"
                    inc(sync.dma_start(out=qh[(hd + 1) % 2][:, :],
                                       in_=qpark[(hd + 1) * 128:(hd + 2) * 128, :]), sem2, 16)
                block.sync(_)

            for blk in range(NCH):
                bpar = (hd * NCH + blk) % 2
                semx = "d_x10" if bpar == 0 else "d_x11"

                def _(sync, hd=hd, blk=blk, bpar=bpar, semx=semx):
                    wt(sync, "c_dve", state.get(f"x1_free_{bpar}", 0))
                    inc(sync.dma_start(
                        out=sg[bpar][:, :],
                        in_=xp[hd * 128:(hd + 1) * 128, blk * CH:(blk + 1) * CH]), semx, 16)
                block.sync(_)
                x1tag = CNT[semx]

                P0s = CNT["c_pe"]
                A0s = CNT["c_act"]

                SB = (0, 1, 4, 5)

                def _(tensor, hd=hd, g=g, blk=blk, A0s=A0s, qtag=qtag, semq=semq):
                    wt(tensor, semq, qtag)
                    for b in SB:
                        bank_guard(tensor, b)
                    for tkt in range(32):
                        if tkt >= 4:
                            wt(tensor, "c_act", A0s + tkt - 3)
                        inc(tensor.matmul(
                            psum[SB[tkt % 4]][:, :],
                            kT[:, g * SEQ + tkt * 128:g * SEQ + (tkt + 1) * 128],
                            qh[hd % 2][:, blk * CH:(blk + 1) * CH],
                            start=True, stop=True), "c_pe")
                block.tensor(_)
                state["scores_done_pe"] = CNT["c_pe"]

                def _(scalar, hd=hd, P0s=P0s, bexp_done=bexp_done):
                    wt(scalar, "c_dve", bexp_done)
                    for tkt in range(32):
                        wt(scalar, "c_pe", P0s + tkt + 1)
                        inc(scalar.activation(
                            out=pT[:, tkt * CH:(tkt + 1) * CH], in_=psum[SB[tkt % 4]][:, :],
                            func=AF.Exp, bias=bexp[:, 0:1]), "c_act")
                block.scalar(_)
                exp_base = A0s
                for j, b in enumerate(SB):
                    read_bank(b, "c_act", exp_base + 29 + j)

                def _(tensor, g=g, exp_base=exp_base):
                    bank_guard(tensor, 2)
                    bank_guard(tensor, 3)
                    for tkt in range(32):
                        wt(tensor, "c_act", exp_base + tkt + 1)
                        ins = tensor.matmul(psum[2][:, :],
                                            v_t[:, g * SEQ + tkt * 128:g * SEQ + (tkt + 1) * 128],
                                            pT[:, tkt * CH:(tkt + 1) * CH],
                                            start=(tkt == 0), stop=(tkt == 31))
                    inc(ins, "c_pe")
                    for tkt in range(32):
                        ins = tensor.matmul(psum[3][:, :], oneb[:, :],
                                            pT[:, tkt * CH:(tkt + 1) * CH],
                                            start=(tkt == 0), stop=(tkt == 31))
                    inc(ins, "c_pe")
                block.tensor(_)
                Pctx = CNT["c_pe"]

                def _(vector, bpar=bpar, semx=semx, x1tag=x1tag, Pctx=Pctx):
                    wt(vector, "c_pe", Pctx)
                    wt(vector, "d_pk", state["x2_store_tag"])
                    wt(vector, "c_dve", state.get("epi_prev", 0))
                    t = inc(vector.reciprocal(uu[0][:, :], psum[3][:, :]), "c_dve")
                    wt(vector, "c_dve", t)
                    t = inc(vector.tensor_mul(uu[1][:, :], psum[2][:, :], uu[0][:, :]),
                            "c_dve")
                    wt(vector, semx, x1tag)
                    wt(vector, "c_dve", t)
                    inc(vector.tensor_add(uu[0][:, :], uu[1][:, :], sg[bpar][:, :]), "c_dve")
                block.vector(_)
                state[f"x1_free_{bpar}"] = CNT["c_dve"]
                state["epi_prev"] = CNT["c_dve"]
                x2ready = CNT["c_dve"]
                read_bank(2, "c_dve", x2ready)
                read_bank(3, "c_dve", x2ready)

                def _(sync, hd=hd, blk=blk, x2ready=x2ready):
                    wt(sync, "c_dve", x2ready)
                    inc(sync.dma_start(
                        out=xp2[hd * 128:(hd + 1) * 128, blk * CH:(blk + 1) * CH],
                        in_=uu[0][:, :]), "d_pk", 16)
                block.sync(_)
                state["x2_store_tag"] = CNT["d_pk"]

        state["att_done_pk"] = CNT["d_pk"]
        state["att_done_pe"] = CNT["c_pe"]

    ph_consts()
    if stage == 11:
        x_ready = ph_load_x_transposed(0, 0, 0)
        ph_ln(x_ready)
        ph_ffn(0)
        def _(sync):
            wt(sync, "c_dve", state["ffn_done_dve"])
            inc(sync.dma_start(out=dbg_xn, in_=xn[:, :]), "d_out", 16)
            inc(sync.dma_start(out=dbg_hb, in_=hb[:, :]), "d_out", 16)
            inc(sync.dma_start(out=dbg_x, in_=xch[:, :]), "d_out", 16)
            inc(sync.dma_start(out=out[0:128, :], in_=xstg[:, 0:D]), "d_out", 16)
            inc(sync.dma_start(out=out[200:201, 0:2048], in_=ostg[0:1, :]), "d_out", 16)
            inc(sync.dma_start(out=out[210:211, 0:512], in_=sg[1][0:1, :]), "d_out", 16)
            wt(sync, "d_out", CNT["d_out"])
        block.sync(_)
        blockctx.__exit__(None, None, None)
        es.close()
        return nc
    park_tag = 0
    xch_free_pe = 0
    for ch in range(NCH):
        x_ready = ph_load_x_transposed(ch, xch_free_pe, park_tag)
        ph_ln(x_ready)
        ph_ffn(state["gu_prev_free"])
        state["gu_prev_free"] = state["gu_free_pe"]
        if eff >= 2:
            ph_ln(lambda ht: ("c_dve", state["ffn_done_dve"]))
            ph_proj(ch)
            xch_free_pe = state["proj_last_pe"]
        else:
            xch_free_pe = state["dw_free_pe"]
        park_tag = ph_park(ch)
        if eff == 1:
            ph_out(ch)
            xch_free_pe = state["out_pe"]

    if eff >= 2:
        ph_attention()
        state["gu_prev_free"] = state["att_done_pe"]
        for ch in range(NCH):
            x_ready = ph_load_x_hmajor(ch, xp2, xch_free_pe)
            if eff >= 3:
                ph_ln(x_ready)
                ph_ffn(state["gu_prev_free"])
                state["gu_prev_free"] = state["gu_free_pe"]
            else:
                def _(vector, xr=x_ready):
                    s, v = xr(0)
                    wt(vector, s, v)
                    inc(vector.tensor_copy(out=sg[0][:, 0:8], in_=xch[:, 0:8]), "c_dve")
                block.vector(_)
                state["ffn_done_dve"] = CNT["c_dve"]
            ph_out(ch)
            xch_free_pe = state["out_pe"]

    def _(sync):
        if stage == 12:
            wt(sync, "d_pk", state["att_done_pk"])
            inc(sync.dma_start(out=dbg_q, in_=qpark[:, :]), "d_out", 16)
            inc(sync.dma_start(out=dbg_kv, in_=kv_recv[:, :]), "d_out", 16)
            inc(sync.dma_start(out=dbg_xp, in_=xp[:, :]), "d_out", 16)
        wt(sync, "d_out", CNT["d_out"])
        wt(sync, "d_pk", CNT["d_pk"])
    block.sync(_)

    if stage == 12:
        def _(vector):
            inc(vector.tensor_copy(out=sg[0][:, 0:16], in_=bexp[:, :]), "c_dve")
        block.vector(_)

        def _(sync):
            wt(sync, "c_dve", CNT["c_dve"])
            inc(sync.dma_start(out=dbg_bexp, in_=sg[0][:, 0:16]), "d_out", 16)
            wt(sync, "d_out", CNT["d_out"])
        block.sync(_)

    blockctx.__exit__(None, None, None)
    es.close()
    return nc



def _byte0(w):
    a = np.ascontiguousarray(w)
    assert a.dtype == np.int32
    return a.view(np.uint8).reshape(a.shape[0], a.shape[1], 4)[:, :, 0]


def kernel(L0_attn_hidden_states, L0_attn_Wq, L0_attn_bq, L0_attn_Wk, L0_attn_bk,
           L0_attn_Wv, L0_attn_bv, L0_ffn_gate_w, L0_ffn_gate_scale, L0_ffn_up_w,
           L0_ffn_up_scale, L0_ffn_down_w, L0_ffn_down_scale, _stage=3):
    bf = ml_dtypes.bfloat16
    x = np.ascontiguousarray(np.asarray(L0_attn_hidden_states, dtype=np.float32)).reshape(16384, D)
    wq = np.ascontiguousarray(np.asarray(L0_attn_Wq, dtype=np.float32))
    wk = np.ascontiguousarray(np.asarray(L0_attn_Wk, dtype=np.float32))
    wv = np.ascontiguousarray(np.asarray(L0_attn_Wv, dtype=np.float32))
    bq = np.ascontiguousarray(np.asarray(L0_attn_bq, dtype=np.float32).reshape(16, 128).T)
    bk = np.ascontiguousarray(np.asarray(L0_attn_bk, dtype=np.float32).reshape(4, 128).T)
    bv = np.ascontiguousarray(np.asarray(L0_attn_bv, dtype=np.float32).reshape(4, 128).T)
    gT = np.ascontiguousarray(_byte0(np.asarray(L0_ffn_gate_w)).T)
    uT = np.ascontiguousarray(_byte0(np.asarray(L0_ffn_up_w)).T)
    dT = np.ascontiguousarray(_byte0(np.asarray(L0_ffn_down_w)).T)
    g4 = np.ascontiguousarray(gT.reshape(16, 128, 32, 256).transpose(2, 1, 0, 3))
    u4 = np.ascontiguousarray(uT.reshape(16, 128, 32, 256).transpose(2, 1, 0, 3))
    d4 = np.ascontiguousarray(dT.reshape(64, 128, 16, 128).transpose(2, 1, 0, 3))
    gsc = np.ascontiguousarray(np.asarray(L0_ffn_gate_scale, np.float32).reshape(64, 128).T)
    usc = np.ascontiguousarray(np.asarray(L0_ffn_up_scale, np.float32).reshape(64, 128).T)
    dsc = np.ascontiguousarray(np.asarray(L0_ffn_down_scale, np.float32).reshape(16, 128).T)
    idf = np.eye(128, dtype=np.float32)
    idb = np.eye(128, dtype=np.float32).astype(bf)
    onef = np.ones((128, 128), np.float32)
    oneb = np.ones((128, 128), np.float32).astype(bf)
    mrow = np.full((1, 128), -128.0, np.float32).astype(bf)

    nc = build(stage=_stage)
    shared = {"wq": wq, "wk": wk, "wv": wv, "bq_in": bq, "bk_in": bk, "bv_in": bv,
              "g4": g4, "u4": u4, "d4": d4, "gsc_in": gsc, "usc_in": usc,
              "dsc_in": dsc, "idf_in": idf, "idb_in": idb, "onef_in": onef,
              "oneb_in": oneb, "mrow_in": mrow}
    in_maps = [dict(shared, x_in=np.ascontiguousarray(x[c * TPC:(c + 1) * TPC]))
               for c in range(8)]
    res = run_bass_kernel_spmd(nc, in_maps, core_ids=list(range(8)))
    full = np.concatenate([np.asarray(r["out"], np.float32) for r in res.results], axis=0)
    return full.reshape(4, 4096, D)


# version 08
# speedup vs baseline: 1.0642x; 1.0058x over previous
import contextlib
import numpy as np
import ml_dtypes
import concourse.bass as bass
import concourse.mybir as mybir
from concourse.bass_utils import run_bass_kernel_spmd

F32 = mybir.dt.float32
BF16 = mybir.dt.bfloat16
U8 = mybir.dt.uint8
AF = mybir.ActivationFunctionType
ALU = mybir.AluOpType

D = 2048
I = 8192
TPC = 2048
CH = 512
NCH = TPC // CH
NHT = D // 128
NIT = I // 128
HQ, HKV, HD = 16, 4, 128
SEQ = 4096
SCALE = 1.0 / float(np.sqrt(128.0))
EXP_OFF = 60.0

OFF_XCH = 0x04000
OFF_XN = 0x0C000
OFF_HB = 0x10000
OFF_KT = 0x10000
OFF_VT = 0x18000
OFF_XSTG = 0x10000
OFF_GW0 = 0x20000
OFF_GW1 = 0x22000
OFF_UW0 = 0x24000
OFF_UW1 = 0x26000
OFF_DW0 = 0x28000
OFF_DW1 = 0x2C000
OFF_PT = 0x20000
OFF_VTMP = 0x28000
OFF_QSQ = 0x28000
OFF_WQ0 = 0x30000
OFF_WQ1 = 0x31000
OFF_OSTG = 0x30000
OFF_IDF = 0x32000
OFF_IDB = 0x32200
OFF_ONEF = 0x32300
OFF_ONEB = 0x32500
OFF_MROW = 0x32600
OFF_BQ = 0x32700
OFF_BQS = 0x32740
OFF_BK = 0x32780
OFF_BV = 0x327A0
OFF_GSC = 0x327C0
OFF_USC = 0x328C0
OFF_DSC = 0x329C0
OFF_DSCH = 0x32A00
OFF_BEXP = 0x32A40
OFF_ROWSB = 0x32A80
OFF_ROWS = 0x33000
OFF_SG0 = 0x34800
OFF_SG1 = 0x35000
OFF_UU0 = 0x35800
OFF_UU1 = 0x36000
OFF_PST0 = 0x36800
OFF_PST1 = 0x36C00
OFF_SQ0 = 0x37000
OFF_SQ1 = 0x37400

SEM_NAMES = [
    "d_x", "d_c", "d_pk", "d_out", "d_kv",
    "d_q0", "d_q1", "d_gu0", "d_gu1", "d_dw0", "d_dw1", "d_wq0", "d_wq1",
    "d_qh0", "d_qh1", "d_x10", "d_x11", "d_x2a", "d_x2b0", "d_x2b1",
    "s_cc", "c_pe", "c_act", "c_dve",
]


def build(stage=3):
    eff = 2 if stage == 12 else stage
    nc = bass.Bass("TRN2", target_bir_lowering=False)

    x_in = nc.dram_tensor("x_in", [TPC, D], F32, kind="ExternalInput").ap()
    wq = nc.dram_tensor("wq", [D, D], F32, kind="ExternalInput").ap()
    wk = nc.dram_tensor("wk", [D, 512], F32, kind="ExternalInput").ap()
    wv = nc.dram_tensor("wv", [D, 512], F32, kind="ExternalInput").ap()
    bq_in = nc.dram_tensor("bq_in", [128, 16], F32, kind="ExternalInput").ap()
    bk_in = nc.dram_tensor("bk_in", [128, 4], F32, kind="ExternalInput").ap()
    bv_in = nc.dram_tensor("bv_in", [128, 4], F32, kind="ExternalInput").ap()
    g4 = nc.dram_tensor("g4", [32, 128, 16, 256], U8, kind="ExternalInput").ap()
    u4 = nc.dram_tensor("u4", [32, 128, 16, 256], U8, kind="ExternalInput").ap()
    d4 = nc.dram_tensor("d4", [16, 128, 64, 128], U8, kind="ExternalInput").ap()
    gsc_in = nc.dram_tensor("gsc_in", [128, 64], F32, kind="ExternalInput").ap()
    usc_in = nc.dram_tensor("usc_in", [128, 64], F32, kind="ExternalInput").ap()
    dsc_in = nc.dram_tensor("dsc_in", [128, 16], F32, kind="ExternalInput").ap()
    idf_in = nc.dram_tensor("idf_in", [128, 128], F32, kind="ExternalInput").ap()
    idb_in = nc.dram_tensor("idb_in", [128, 128], BF16, kind="ExternalInput").ap()
    onef_in = nc.dram_tensor("onef_in", [128, 128], F32, kind="ExternalInput").ap()
    oneb_in = nc.dram_tensor("oneb_in", [128, 128], BF16, kind="ExternalInput").ap()
    mrow_in = nc.dram_tensor("mrow_in", [1, 128], BF16, kind="ExternalInput").ap()
    out = nc.dram_tensor("out", [TPC, D], F32, kind="ExternalOutput").ap()
    if stage == 11:
        dbg_xn = nc.dram_tensor("dbg_xn", [128, NHT * CH], BF16, kind="ExternalOutput").ap()
        dbg_hb = nc.dram_tensor("dbg_hb", [128, NIT * CH], BF16, kind="ExternalOutput").ap()
        dbg_x = nc.dram_tensor("dbg_x", [128, NHT * CH], F32, kind="ExternalOutput").ap()
    if stage == 12:
        dbg_q = nc.dram_tensor("dbg_q", [D, TPC], BF16, kind="ExternalOutput").ap()
        dbg_kv = nc.dram_tensor("dbg_kv", [2048, TPC], BF16, kind="ExternalOutput").ap()
        dbg_bexp = nc.dram_tensor("dbg_bexp", [128, 16], F32, kind="ExternalOutput").ap()
        dbg_xp = nc.dram_tensor("dbg_xp", [D, TPC], F32, kind="ExternalOutput").ap()
    xp = nc.dram_tensor("xp", [D, TPC], F32, kind="Internal").ap()
    xp2 = nc.dram_tensor("xp2", [D, TPC], F32, kind="Internal").ap()
    qpark = nc.dram_tensor("qpark", [D, TPC], BF16, kind="Internal").ap()
    kv_send = nc.dram_tensor("kv_send", [1024, TPC], BF16, kind="Internal").ap()
    kv_recv = nc.dram_tensor("kv_recv", [2048, TPC], BF16, kind="Internal").ap()

    SH = 0x80
    _MAP_END = 0x37800 + SH
    nc.alloc_sbuf_tensor("mapspacer", [128, _MAP_END - nc.sbuf_base], U8)
    assert nc.sbuf_base >= _MAP_END, hex(nc.sbuf_base)

    def A(name, shape, dtype, offset):
        return nc.alloc_sbuf_tensor_at(name, shape, dtype, offset=offset + SH)

    xch = A("xch", [128, NHT * CH], F32, offset=OFF_XCH)
    xn = A("xn", [128, NHT * CH], BF16, offset=OFF_XN)
    hb = A("hb", [128, NIT * CH], BF16, offset=OFF_HB)
    kT = A("kTs", [128, 4 * SEQ], BF16, offset=OFF_KT)
    v_t = A("v_ts", [128, 4 * SEQ], BF16, offset=OFF_VT)
    xstg = A("xstg", [128, 4 * D], F32, offset=OFF_XSTG)
    gw = [A("gw0", [128, 4096], BF16, offset=OFF_GW0),
          A("gw1", [128, 4096], BF16, offset=OFF_GW1)]
    uw = [A("uw0", [128, 4096], BF16, offset=OFF_UW0),
          A("uw1", [128, 4096], BF16, offset=OFF_UW1)]
    dw = [A("dw0", [128, 8192], BF16, offset=OFF_DW0),
          A("dw1", [128, 8192], BF16, offset=OFF_DW1)]
    pT = A("pTs", [128, 32 * CH], BF16, offset=OFF_PT)
    pTP = A("pTP", [128, 64 * CH], BF16, offset=OFF_PT)
    vtmp = A("vtmp", [128, 4 * SEQ], BF16, offset=OFF_VTMP)
    qsq = A("qsq", [128, SEQ], BF16, offset=OFF_QSQ)
    wqb = [A("wqb0", [128, 2048], BF16, offset=OFF_WQ0),
           A("wqb1", [128, 2048], BF16, offset=OFF_WQ1)]
    qh = wqb
    ostg = A("ostg", [128, D], F32, offset=OFF_OSTG)
    idf = A("idf", [128, 128], F32, offset=OFF_IDF)
    idb = A("idb", [128, 128], BF16, offset=OFF_IDB)
    onef = A("onef", [128, 128], F32, offset=OFF_ONEF)
    oneb = A("oneb", [128, 128], BF16, offset=OFF_ONEB)
    mrow = A("mrow", [1, 128], BF16, offset=OFF_MROW)
    bq_s = A("bq_s", [128, 16], F32, offset=OFF_BQ)
    bqs = A("bqs", [128, 16], F32, offset=OFF_BQS)
    bk_s = A("bk_s", [128, 4], F32, offset=OFF_BK)
    bv_s = A("bv_s", [128, 4], F32, offset=OFF_BV)
    gsc = A("gsc", [128, 64], F32, offset=OFF_GSC)
    usc = A("usc", [128, 64], F32, offset=OFF_USC)
    dsc = A("dsc", [128, 16], F32, offset=OFF_DSC)
    dsch = A("dsch", [128, 16], F32, offset=OFF_DSCH)
    bexp = A("bexp", [128, 16], F32, offset=OFF_BEXP)
    rowsb = A("rowsb", [128, 512], BF16, offset=OFF_ROWSB)
    qmx = A("qmx", [128, 16], F32, offset=OFF_ROWSB + 0x400)
    rows = A("rows", [128, 1536], F32, offset=OFF_ROWS)
    sg = [A("sg0", [128, CH], F32, offset=OFF_SG0),
          A("sg1", [128, CH], F32, offset=OFF_SG1)]
    uu = [A("uu0", [128, CH], F32, offset=OFF_UU0),
          A("uu1", [128, CH], F32, offset=OFF_UU1)]
    pst = [A("pst0", [128, CH], BF16, offset=OFF_PST0),
           A("pst1", [128, CH], BF16, offset=OFF_PST1)]
    sqst = [A("sq0", [128, CH], BF16, offset=OFF_SQ0),
            A("sq1", [128, CH], BF16, offset=OFF_SQ1)]

    es = contextlib.ExitStack()
    pb01 = es.enter_context(nc.psum_tensor("pb01", [128, 1024], F32))
    pb23 = es.enter_context(nc.psum_tensor("pb23", [128, 1024], F32))
    pb45 = es.enter_context(nc.psum_tensor("pb45", [128, 1024], F32))
    pb6 = es.enter_context(nc.psum_tensor("pb6", [128, 512], F32))
    psb = es.enter_context(nc.psum_tensor("pb7", [128, 1024], BF16))
    psum = [pb01[:, 0:512], pb01[:, 512:1024], pb23[:, 0:512], pb23[:, 512:1024],
            pb45[:, 0:512], pb45[:, 512:1024], pb6[:, :]]
    pbS = [pb01, pb45]
    SEM = {n: es.enter_context(nc.semaphore(n)) for n in SEM_NAMES}
    CNT = {n: 0 for n in SEM_NAMES}
    LASTW = {}
    BANKR = [{"c_act": 0, "c_dve": 0} for _ in range(8)]

    def inc(ins, sem, k=1):
        ins.then_inc(SEM[sem], k)
        CNT[sem] += k
        return CNT[sem]

    def wt(eng, sem, val):
        if val is None or val <= 0:
            return
        key = (id(eng), sem)
        if LASTW.get(key, -1) < val:
            eng.wait_ge(SEM[sem], val)
            LASTW[key] = val

    def bank_guard(tensor, b):
        wt(tensor, "c_act", BANKR[b]["c_act"])
        wt(tensor, "c_dve", BANKR[b]["c_dve"])

    def read_bank(b, sem, tag):
        BANKR[b][sem] = max(BANKR[b][sem], tag)

    def xch_t(ht):
        return xch[:, ht * CH:(ht + 1) * CH]

    def xn_t(ht):
        return xn[:, ht * CH:(ht + 1) * CH]

    def hb_t(it):
        return hb[:, it * CH:(it + 1) * CH]

    def row(i):
        return rows[0:1, i * 512:(i + 1) * 512]

    def rcell(c0, n=1):
        return rows[0:1, c0:c0 + n]

    blockctx = nc.Block()
    block = blockctx.__enter__()
    state = {"xn_ready": 0, "ffn_done_dve": 0, "wq_free_pe": 0,
             "gu_prev_free": 0, "x2_store_tag": 0, "scores_done_pe": 0}

    def ph_consts():
        def _(sync):
            for dst, src in [(idf, idf_in), (idb, idb_in), (onef, onef_in),
                             (oneb, oneb_in), (bq_s, bq_in), (bk_s, bk_in),
                             (bv_s, bv_in), (gsc, gsc_in), (usc, usc_in),
                             (dsc, dsc_in)]:
                inc(sync.dma_start(out=dst[:, :], in_=src), "d_c", 16)
            inc(sync.dma_start(out=mrow[0:1, :], in_=mrow_in), "d_c", 16)
        block.sync(_)
        state["consts"] = CNT["d_c"]

        def _(vector):
            wt(vector, "d_c", state["consts"])
            inc(vector.tensor_scalar_mul(bqs[:, :], bq_s[:, :], SCALE), "c_dve")
            inc(vector.tensor_scalar_mul(dsch[:, :], dsc[:, :], 0.5), "c_dve")
            inc(vector.memset(qmx[:, :], 0.0), "c_dve")
        block.vector(_)
        state["consts_dve"] = CNT["c_dve"]

    def ph_ln(x_ready):
        A0 = CNT["c_act"]
        P0 = CNT["c_pe"]

        def _(scalar):
            for ht in range(NHT):
                s, v = x_ready(ht)
                wt(scalar, s, v)
                if ht >= 2:
                    wt(scalar, "c_pe", P0 + 2 * (ht - 2) + 2)
                inc(scalar.activation(out=xn_t(ht), in_=xch_t(ht), func=AF.Copy), "c_act")
                inc(scalar.activation(out=sqst[ht % 2][:, :], in_=xch_t(ht),
                                      func=AF.Square), "c_act")
        block.scalar(_)

        def _(tensor):
            wt(tensor, "d_c", state["consts"])
            s, v = x_ready(NHT - 1)
            wt(tensor, s, v)
            bank_guard(tensor, 6)
            bank_guard(tensor, 4)
            for ht in range(NHT):
                wt(tensor, "c_act", A0 + 2 * ht + 1)
                inc(tensor.matmul(psum[6][0:1, :], oneb[:, 0:1], xn_t(ht),
                                  start=(ht == 0), stop=(ht == NHT - 1)), "c_pe")
                wt(tensor, "c_act", A0 + 2 * ht + 2)
                inc(tensor.matmul(psum[4][0:1, :], oneb[:, 0:1], sqst[ht % 2][:, :],
                                  start=(ht == 0), stop=(ht == NHT - 1)), "c_pe")
        block.tensor(_)
        pe_stats = CNT["c_pe"]
        assert pe_stats == P0 + 2 * NHT

        def _(vector):
            wt(vector, "c_pe", pe_stats)
            if stage == 11:
                inc(vector.tensor_copy(out=ostg[0:1, 0:512], in_=psum[6][0:1, :]), "c_dve")
                inc(vector.tensor_copy(out=ostg[0:1, 512:1024], in_=psum[4][0:1, :]), "c_dve")
            t1 = inc(vector.tensor_scalar_mul(row(0), psum[6][0:1, :], 1.0 / D), "c_dve")
            t2 = inc(vector.tensor_scalar_mul(row(1), psum[4][0:1, :], 1.0 / D), "c_dve")
            wt(vector, "c_dve", t1)
            t3 = inc(vector.tensor_mul(row(2), row(0), row(0)), "c_dve")
            wt(vector, "c_dve", t3)
            inc(vector.scalar_tensor_tensor(out=row(1), in0=row(1), scalar=1e-5,
                                            in1=row(2), op0=ALU.add, op1=ALU.subtract),
                "c_dve")
        block.vector(_)
        dve_var = CNT["c_dve"]
        read_bank(6, "c_dve", dve_var)
        read_bank(4, "c_dve", dve_var)
        if stage == 11:
            def _(vector):
                wt(vector, "c_dve", dve_var)
                inc(vector.tensor_copy(out=ostg[0:1, 1024:1536], in_=row(1)), "c_dve")
            block.vector(_)
            dve_var = CNT["c_dve"]

        def _(scalar):
            wt(scalar, "c_dve", dve_var)
            a1 = inc(scalar.activation(out=row(2), in_=row(1), func=AF.Ln), "c_act")
            wt(scalar, "c_act", a1)
            inc(scalar.activation(out=row(1), in_=row(2), func=AF.Exp, scale=-0.5), "c_act")
        block.scalar(_)
        if stage == 11:
            def _(vector):
                wt(vector, "c_act", CNT["c_act"])
                inc(vector.tensor_copy(out=sg[1][0:1, :], in_=row(2)), "c_dve")
            block.vector(_)
        act_rstd = CNT["c_act"]
        if stage == 11:
            def _(vector):
                wt(vector, "c_act", act_rstd)
                inc(vector.tensor_copy(out=ostg[0:1, 1536:2048], in_=row(1)), "c_dve")
            block.vector(_)

        def _(vector):
            wt(vector, "c_act", act_rstd)
            inc(vector.scalar_tensor_tensor(out=row(2), in0=row(0), scalar=-1.0,
                                            in1=row(1), op0=ALU.mult, op1=ALU.mult), "c_dve")
        block.vector(_)
        dve_nmr = CNT["c_dve"]

        def _(tensor):
            wt(tensor, "c_dve", dve_nmr)
            bank_guard(tensor, 6)
            bank_guard(tensor, 4)
            inc(tensor.matmul(psum[6][:, :], onef[0:1, :], row(1), start=True, stop=True), "c_pe")
            inc(tensor.matmul(psum[4][:, :], onef[0:1, :], row(2), start=True, stop=True), "c_pe")
        block.tensor(_)
        pe_bc = CNT["c_pe"]

        def _(vector):
            wt(vector, "c_pe", pe_bc)
            prev_add = 0
            tags = []
            for ht in range(NHT):
                wt(vector, "c_dve", prev_add)
                t = inc(vector.tensor_mul(sg[0][:, :], xch_t(ht), psum[6][:, :]), "c_dve")
                wt(vector, "c_dve", t)
                prev_add = inc(vector.tensor_add(xn_t(ht), sg[0][:, :], psum[4][:, :]), "c_dve")
                tags.append(prev_add)
            state["xn_tags"] = tags
        block.vector(_)
        state["xn_ready"] = CNT["c_dve"]
        read_bank(6, "c_dve", state["xn_ready"])
        read_bank(4, "c_dve", state["xn_ready"])

    def ph_ffn(warena_free_pe):
        NB = 32
        P0 = CNT["c_pe"]
        A0 = CNT["c_act"]
        V0 = CNT["c_dve"]
        gu_base = [CNT["d_gu0"], CNT["d_gu1"]]

        def _pe_before(it):
            return 2 * it + max(0, it - 2)

        def tag_g(it):
            return P0 + _pe_before(it) + 1

        def tag_u(it):
            return P0 + _pe_before(it) + 2

        def tag_silu(it):
            return A0 + it + 1

        def tag_uu(it):
            return V0 + 2 * it + 1

        def tag_h(it):
            return V0 + 2 * it + 2

        def _(gpsimd):
            for ib in range(NB):
                if ib >= 2:
                    wt(gpsimd, "c_pe", tag_u(2 * ib - 3))
                else:
                    wt(gpsimd, "c_pe", warena_free_pe)
                sem = "d_gu0" if ib % 2 == 0 else "d_gu1"
                inc(gpsimd.dma_start(
                    out=gw[ib % 2][:].rearrange("p (n c) -> p n c", c=256), in_=g4[ib]), sem, 16)
                inc(gpsimd.dma_start(
                    out=uw[ib % 2][:].rearrange("p (n c) -> p n c", c=256), in_=u4[ib]), sem, 16)
        block.gpsimd(_)

        xtags = state["xn_tags"]

        def _(tensor):
            bank_guard(tensor, 6)
            for it in range(2 * NB):
                ib = it // 2
                sem = "d_gu0" if ib % 2 == 0 else "d_gu1"
                wt(tensor, sem, gu_base[ib % 2] + 32 * (ib // 2 + 1))
                if it >= 2:
                    wt(tensor, "c_act", tag_silu(it - 2))
                    wt(tensor, "c_dve", tag_uu(it - 2))
                else:
                    bank_guard(tensor, it % 2)
                    bank_guard(tensor, 2 + it % 2)
                sl = (it % 2) * 128
                for ht in range(NHT):
                    wt(tensor, "c_dve", xtags[ht])
                    ins = tensor.matmul(
                        psum[it % 2][:, :],
                        gw[ib % 2][:, ht * 256 + sl:ht * 256 + sl + 128],
                        xn_t(ht), start=(ht == 0), stop=(ht == NHT - 1))
                inc(ins, "c_pe")
                for ht in range(NHT):
                    ins = tensor.matmul(
                        psum[2 + it % 2][:, :],
                        uw[ib % 2][:, ht * 256 + sl:ht * 256 + sl + 128],
                        xn_t(ht), start=(ht == 0), stop=(ht == NHT - 1))
                inc(ins, "c_pe")
                if it >= 2:
                    wt(tensor, "c_dve", tag_h(it - 2))
                    inc(tensor.matmul(psum[6][0:1, :], oneb[:, 0:1], hb_t(it - 2),
                                      start=(it == 2), stop=False), "c_pe")
        block.tensor(_)

        def _(tensor):
            for it in range(2 * NB - 2, 2 * NB):
                wt(tensor, "c_dve", tag_h(it))
                inc(tensor.matmul(psum[6][0:1, :], oneb[:, 0:1], hb_t(it),
                                  start=False, stop=(it == 2 * NB - 1)), "c_pe")
        block.tensor(_)
        assert CNT["c_pe"] == P0 + 4 * NB + (2 * NB - 2) + 2, (CNT["c_pe"], P0)

        def _(scalar):
            wt(scalar, "d_c", state["consts"])
            for it in range(2 * NB):
                wt(scalar, "c_pe", tag_g(it))
                if it >= 2:
                    wt(scalar, "c_dve", tag_h(it - 2))
                inc(scalar.activation(out=sg[it % 2][:, :], in_=psum[it % 2][:, :],
                                      func=AF.Silu, scale=gsc[:, it:it + 1]), "c_act")
        block.scalar(_)
        assert CNT["c_act"] == A0 + 2 * NB

        def _(vector):
            wt(vector, "d_c", state["consts"])
            for it in range(2 * NB):
                wt(vector, "c_pe", tag_u(it))
                if it >= 2:
                    wt(vector, "c_dve", tag_h(it - 2))
                t = inc(vector.tensor_scalar_mul(uu[it % 2][:, :], psum[2 + it % 2][:, :],
                                                 usc[:, it:it + 1]), "c_dve")
                wt(vector, "c_act", tag_silu(it))
                wt(vector, "c_dve", t)
                inc(vector.tensor_mul(hb_t(it), sg[it % 2][:, :], uu[it % 2][:, :]), "c_dve")
        block.vector(_)
        assert CNT["c_dve"] == V0 + 4 * NB
        for b in range(4):
            read_bank(b, "c_act", A0 + 2 * NB)
            read_bank(b, "c_dve", V0 + 4 * NB)

        pe_csh = CNT["c_pe"]

        def _(vector):
            wt(vector, "c_pe", pe_csh)
            t1 = inc(vector.tensor_copy(out=row(0), in_=psum[6][0:1, :]), "c_dve")
            wt(vector, "c_dve", t1)
            t2 = inc(vector.tensor_copy(out=rowsb[0:1, :], in_=row(0)), "c_dve")
            wt(vector, "c_dve", t2)
            inc(vector.tensor_sub(sqst[0][0:1, :], row(0), rowsb[0:1, :]), "c_dve")
        block.vector(_)
        csh_ready = CNT["c_dve"]
        read_bank(6, "c_dve", csh_ready)

        Pd0 = CNT["c_pe"]
        Vd0 = CNT["c_dve"]
        dw_base = [CNT["d_dw0"], CNT["d_dw1"]]

        def tag_d(hht):
            return Pd0 + hht + 1

        def tag_stt(hht):
            return Vd0 + hht + 1

        def _(gpsimd):
            for hht in range(NHT):
                if hht >= 2:
                    wt(gpsimd, "c_pe", tag_d(hht - 2))
                sem = "d_dw0" if hht % 2 == 0 else "d_dw1"
                inc(gpsimd.dma_start(
                    out=dw[hht % 2][:].rearrange("p (n c) -> p n c", c=128), in_=d4[hht]), sem, 16)
        block.gpsimd(_)

        def _(tensor):
            wt(tensor, "c_dve", csh_ready)
            bank_guard(tensor, 4)
            bank_guard(tensor, 5)
            for hht in range(NHT):
                sem = "d_dw0" if hht % 2 == 0 else "d_dw1"
                wt(tensor, sem, dw_base[hht % 2] + 16 * (hht // 2 + 1))
                if hht >= 2:
                    wt(tensor, "c_dve", tag_stt(hht - 2))
                tensor.matmul(psum[4 + hht % 2][:, :], mrow[0:1, :], rowsb[0:1, :],
                              start=True, stop=False)
                tensor.matmul(psum[4 + hht % 2][:, :], mrow[0:1, :], sqst[0][0:1, :],
                              start=False, stop=False)
                for it in range(NIT):
                    ins = tensor.matmul(
                        psum[4 + hht % 2][:, :],
                        dw[hht % 2][:, it * 128:(it + 1) * 128],
                        hb_t(it), start=False, stop=(it == NIT - 1))
                inc(ins, "c_pe")
        block.tensor(_)
        assert CNT["c_pe"] == Pd0 + NHT

        def _(vector):
            for hht in range(NHT):
                wt(vector, "c_pe", tag_d(hht))
                inc(vector.scalar_tensor_tensor(
                    out=xch_t(hht), in0=psum[4 + hht % 2][:, :], scalar=dsch[:, hht:hht + 1],
                    in1=xch_t(hht), op0=ALU.mult, op1=ALU.add), "c_dve")
        block.vector(_)
        assert CNT["c_dve"] == Vd0 + NHT
        state["ffn_done_dve"] = CNT["c_dve"]
        read_bank(4, "c_dve", Vd0 + NHT)
        read_bank(5, "c_dve", Vd0 + NHT)
        state["gu_free_pe"] = tag_u(2 * NB - 1)
        state["dw_free_pe"] = CNT["c_pe"]

    def ph_load_x_transposed(ch, xch_free_pe, park_tag):
        def _(sync):
            wt(sync, "c_pe", xch_free_pe)
            wt(sync, "d_pk", park_tag)
            for tt in range(4):
                inc(sync.dma_start(out=xstg[:, tt * D:(tt + 1) * D],
                                   in_=x_in[ch * CH + tt * 128: ch * CH + (tt + 1) * 128, :]),
                    "d_x", 16)
        block.sync(_)
        xload = CNT["d_x"]
        P0 = CNT["c_pe"]
        V0 = CNT["c_dve"]

        def _(tensor):
            wt(tensor, "d_c", state["consts"])
            wt(tensor, "d_x", xload)
            bank_guard(tensor, 4)
            bank_guard(tensor, 5)
            for ht in range(NHT):
                if ht >= 2:
                    wt(tensor, "c_dve", V0 + ht - 1)
                for tt in range(4):
                    ins = tensor.transpose(
                        psum[4 + ht % 2][:, tt * 128:(tt + 1) * 128],
                        xstg[:, tt * D + ht * 128: tt * D + (ht + 1) * 128],
                        idf[:, :])
                inc(ins, "c_pe")
        block.tensor(_)

        def _(vector):
            for ht in range(NHT):
                wt(vector, "c_pe", P0 + ht + 1)
                inc(vector.tensor_copy(out=xch_t(ht), in_=psum[4 + ht % 2][:, :]), "c_dve")
        block.vector(_)
        read_bank(4, "c_dve", V0 + NHT)
        read_bank(5, "c_dve", V0 + NHT)
        return lambda ht: ("c_dve", V0 + ht + 1)

    def ph_load_x_hmajor(ch, src, xch_free_pe):
        def _(sync):
            wt(sync, "c_pe", xch_free_pe)
            for s, v in state["att_done"].items():
                wt(sync, s, v)
            inc(sync.dma_start(
                out=xch[:].rearrange("p (n c) -> p n c", c=CH),
                in_=src.rearrange("(n p) t -> p n t", p=128)[:, :, ch * CH:(ch + 1) * CH]),
                "d_x", 16)
        block.sync(_)
        xload = CNT["d_x"]
        return lambda ht: ("d_x", xload)

    def ph_proj(ch):
        NS = 24
        P0 = CNT["c_pe"]
        wq_base = [CNT["d_wq0"], CNT["d_wq1"]]
        q_base = [CNT["d_q0"], CNT["d_q1"]]

        def _pe_before(s):
            return s + min(max(0, s - 2), 16)

        def tag_p(s):
            return P0 + _pe_before(s) + 1

        def tag_nm(n):
            return P0 + _pe_before(n + 2) + 2

        def _(gpsimd):
            for s in range(NS):
                if s >= 2:
                    wt(gpsimd, "c_pe", tag_p(s - 2))
                else:
                    wt(gpsimd, "c_pe", state["wq_free_pe"])
                sem = "d_wq0" if s % 2 == 0 else "d_wq1"
                if s < 16:
                    src = wq.rearrange("(n p) c -> p n c", p=128)[:, :, s * 128:(s + 1) * 128]
                elif s < 20:
                    dt = s - 16
                    src = wk.rearrange("(n p) c -> p n c", p=128)[:, :, dt * 128:(dt + 1) * 128]
                else:
                    dt = s - 20
                    src = wv.rearrange("(n p) c -> p n c", p=128)[:, :, dt * 128:(dt + 1) * 128]
                inc(gpsimd.dma_start(
                    out=wqb[s % 2][:].rearrange("p (n c) -> p n c", c=128), in_=src), sem, 16)
        block.gpsimd(_)

        V0 = CNT["c_dve"]

        def _v_incs(s):
            n = 0
            for t in range(s):
                n += 1
                if t < 16:
                    n += 1
                if 2 <= t < 18:
                    n += 2
            return n

        def tag_e(s):
            return V0 + _v_incs(s) + 1

        def tag_sq(s):
            return tag_e(s) + 1

        def tag_red(n):
            return tag_e(n + 2) + 1 + (1 if (n + 2) < 16 else 0)

        xtags = state["xn_tags"]

        def _(tensor):
            bank_guard(tensor, 6)
            for s in range(NS):
                sem = "d_wq0" if s % 2 == 0 else "d_wq1"
                wt(tensor, sem, wq_base[s % 2] + 16 * (s // 2 + 1))
                if s >= 2:
                    wt(tensor, "c_dve", tag_e(s - 2))
                else:
                    bank_guard(tensor, s % 2)
                for ht in range(NHT):
                    wt(tensor, "c_dve", xtags[ht])
                    ins = tensor.matmul(
                        psum[s % 2][:, :],
                        wqb[s % 2][:, ht * 128:(ht + 1) * 128],
                        xn_t(ht), start=(ht == 0), stop=(ht == NHT - 1))
                inc(ins, "c_pe")
                assert CNT["c_pe"] == tag_p(s), (s, CNT["c_pe"], tag_p(s))
                if s >= 2 and s - 2 < 16:
                    n = s - 2
                    wt(tensor, "c_dve", tag_sq(n))
                    if n >= 1:
                        wt(tensor, "c_dve", tag_red(n - 1))
                    inc(tensor.matmul(psum[6][0:1, :], oneb[:, 0:1], sqst[n % 2][:, :],
                                      start=True, stop=True), "c_pe")
        block.tensor(_)

        def tag_ev(s):
            return tag_e(s)

        def _(vector):
            wt(vector, "c_dve", state["consts_dve"])
            for s in range(NS):
                wt(vector, "c_pe", tag_p(s))
                semd = "d_q0" if s % 2 == 0 else "d_q1"
                wt(vector, semd, q_base[s % 2] + 16 * (s // 2))
                if s < 16:
                    e = inc(vector.tensor_scalar(out=pst[s % 2][:, :], in0=psum[s % 2][:, :],
                                                 scalar1=SCALE, scalar2=bqs[:, s:s + 1],
                                                 op0=ALU.mult, op1=ALU.add), "c_dve")
                elif s < 20:
                    inc(vector.tensor_scalar_add(pst[s % 2][:, :], psum[s % 2][:, :],
                                                 bk_s[:, s - 16:s - 15]), "c_dve")
                else:
                    inc(vector.tensor_scalar_add(pst[s % 2][:, :], psum[s % 2][:, :],
                                                 bv_s[:, s - 20:s - 19]), "c_dve")
                assert CNT["c_dve"] == tag_e(s), (s, CNT["c_dve"], tag_e(s))
                if s < 16:
                    wt(vector, "c_dve", e)
                    wt(vector, "c_pe", tag_nm(s - 2) if s >= 2 else 0)
                    inc(vector.tensor_mul(sqst[s % 2][:, :], pst[s % 2][:, :],
                                          pst[s % 2][:, :]), "c_dve")
                if 2 <= s < 18:
                    n = s - 2
                    wt(vector, "c_pe", tag_nm(n))
                    if n >= 1:
                        wt(vector, "c_dve", tag_red(n - 1) + 1)
                    t = inc(vector.reduce_max(qmx[0:1, 1:2], psum[6][0:1, :],
                                              axis=mybir.AxisListType.X), "c_dve")
                    wt(vector, "c_dve", t)
                    inc(vector.tensor_max(qmx[0:1, 0:1], qmx[0:1, 0:1],
                                          qmx[0:1, 1:2]), "c_dve")
        block.vector(_)
        read_bank(6, "c_dve", CNT["c_dve"])

        def _(scalar):
            for s in range(NS):
                wt(scalar, "c_dve", tag_ev(s))
                sem = "d_q0" if s % 2 == 0 else "d_q1"
                if s < 16:
                    dst = qpark[s * 128:(s + 1) * 128, ch * CH:(ch + 1) * CH]
                elif s < 20:
                    dst = kv_send[(s - 16) * 128:(s - 15) * 128, ch * CH:(ch + 1) * CH]
                else:
                    dst = kv_send[512 + (s - 20) * 128:512 + (s - 19) * 128,
                                  ch * CH:(ch + 1) * CH]
                inc(scalar.dma_start(out=dst, in_=pst[s % 2][:, :]), sem, 16)
        block.scalar(_)
        read_bank(0, "c_dve", tag_ev(NS - 2))
        read_bank(1, "c_dve", tag_ev(NS - 1))
        state["wq_free_pe"] = tag_p(NS - 1)
        state["proj_last_pe"] = tag_p(NS - 1)

    def ph_park(ch):
        def _(sync):
            wt(sync, "c_dve", state["ffn_done_dve"])
            inc(sync.dma_start(
                out=xp.rearrange("(n p) t -> p n t", p=128)[:, :, ch * CH:(ch + 1) * CH],
                in_=xch[:].rearrange("p (n c) -> p n c", c=CH)), "d_pk", 16)
        block.sync(_)
        return CNT["d_pk"]

    def ph_out(ch):
        P0 = CNT["c_pe"]
        V0 = CNT["c_dve"]
        O0 = CNT["d_out"]

        def _(tensor):
            wt(tensor, "d_c", state["consts"])
            wt(tensor, "c_dve", state["ffn_done_dve"])
            bank_guard(tensor, 4)
            bank_guard(tensor, 5)
            gi = 0
            for tt in range(4):
                for grp in range(4):
                    if gi >= 2:
                        wt(tensor, "c_dve", V0 + gi - 1)
                    bank = 4 + gi % 2
                    for j in range(4):
                        ht = grp * 4 + j
                        ins = tensor.transpose(
                            psum[bank][:, j * 128:(j + 1) * 128],
                            xch_t(ht)[:, tt * 128:(tt + 1) * 128],
                            idf[:, :])
                    inc(ins, "c_pe")
                    gi += 1
        block.tensor(_)

        def _(vector):
            gi = 0
            for tt in range(4):
                wt(vector, "d_out", O0 + 16 * tt)
                for grp in range(4):
                    wt(vector, "c_pe", P0 + gi + 1)
                    inc(vector.tensor_copy(out=ostg[:, grp * 512:(grp + 1) * 512],
                                           in_=psum[4 + gi % 2][:, :]), "c_dve")
                    gi += 1
        block.vector(_)

        def _(sync):
            for tt in range(4):
                wt(sync, "c_dve", V0 + 4 * tt + 4)
                inc(sync.dma_start(out=out[ch * CH + tt * 128: ch * CH + (tt + 1) * 128, :],
                                   in_=ostg[:, :]), "d_out", 16)
        block.sync(_)
        read_bank(4, "c_dve", V0 + 16)
        read_bank(5, "c_dve", V0 + 16)
        state["out_pe"] = CNT["c_pe"]
        state["out_last_dve"] = CNT["c_dve"]

    def ph_attention():
        def _(gpsimd):
            wt(gpsimd, "d_q0", CNT["d_q0"])
            wt(gpsimd, "d_q1", CNT["d_q1"])
            inc(gpsimd.collective_compute(
                "AllGather", ALU.bypass,
                replica_groups=[[0, 1], [2, 3], [4, 5], [6, 7]],
                ins=[kv_send[:, :]], outs=[kv_recv[:, :]]), "s_cc", 1)
        block.gpsimd(_)

        qh_base = [CNT["d_qh0"], CNT["d_qh1"]]

        def _(sync):
            wt(sync, "s_cc", CNT["s_cc"])
            inc(sync.dma_start(
                out=kT[:].rearrange("p (g t) -> p g t", t=SEQ)[:, :, 0:TPC],
                in_=kv_recv[0:512, :].rearrange("(g p) t -> p g t", p=128)), "d_kv", 16)
            inc(sync.dma_start(
                out=kT[:].rearrange("p (g t) -> p g t", t=SEQ)[:, :, TPC:SEQ],
                in_=kv_recv[1024:1536, :].rearrange("(g p) t -> p g t", p=128)), "d_kv", 16)
            inc(sync.dma_start(
                out=vtmp[:].rearrange("p (g t) -> p g t", t=SEQ)[:, :, 0:TPC],
                in_=kv_recv[512:1024, :].rearrange("(g p) t -> p g t", p=128)), "d_kv", 16)
            inc(sync.dma_start(
                out=vtmp[:].rearrange("p (g t) -> p g t", t=SEQ)[:, :, TPC:SEQ],
                in_=kv_recv[1536:2048, :].rearrange("(g p) t -> p g t", p=128)), "d_kv", 16)
        block.sync(_)
        kvload = CNT["d_kv"]

        P0 = CNT["c_pe"]
        V0 = CNT["c_dve"]

        def _(tensor):
            wt(tensor, "d_c", state["consts"])
            wt(tensor, "d_kv", kvload)
            for grp in range(16):
                if grp >= 1:
                    wt(tensor, "c_dve", V0 + 8 * grp)
                for j in range(8):
                    tile = grp * 8 + j
                    g, tkt = tile // 32, tile % 32
                    ins = tensor.transpose(
                        psb[:, j * 128:(j + 1) * 128],
                        vtmp[:, g * SEQ + tkt * 128: g * SEQ + (tkt + 1) * 128],
                        idb[:, :])
                inc(ins, "c_pe")
        block.tensor(_)
        state["vt_pe"] = CNT["c_pe"]

        def _(vector):
            for grp in range(16):
                wt(vector, "c_pe", P0 + grp + 1)
                for j in range(8):
                    tile = grp * 8 + j
                    g, tkt = tile // 32, tile % 32
                    inc(vector.tensor_copy(
                        out=v_t[:, g * SEQ + tkt * 128:g * SEQ + (tkt + 1) * 128],
                        in_=psb[:, j * 128:(j + 1) * 128]), "c_dve")
        block.vector(_)
        vt_done = CNT["c_dve"]
        read_bank(7, "c_dve", vt_done)

        prev_mm = 0
        for g in range(4):
            Vs = CNT["c_dve"]

            def _(vector, g=g, prev_mm=prev_mm):
                wt(vector, "c_dve", vt_done)
                wt(vector, "c_pe", prev_mm)
                inc(vector.tensor_mul(qsq[:, :], kT[:, g * SEQ:(g + 1) * SEQ],
                                      kT[:, g * SEQ:(g + 1) * SEQ]), "c_dve")
            block.vector(_)
            Pn = CNT["c_pe"]
            Vr0 = CNT["c_dve"]

            def _(tensor, Vs=Vs, Vr0=Vr0):
                wt(tensor, "c_dve", Vs + 1)
                bank_guard(tensor, 6)
                for q4 in range(8):
                    if q4 >= 1:
                        wt(tensor, "c_dve", Vr0 + q4)
                    inc(tensor.matmul(psum[6][0:1, :], oneb[:, 0:1],
                                      qsq[:, q4 * 512:(q4 + 1) * 512],
                                      start=True, stop=True), "c_pe")
            block.tensor(_)
            prev_mm = CNT["c_pe"]

            def _(vector, g=g, Pn=Pn):
                t = 0
                for q4 in range(8):
                    wt(vector, "c_pe", Pn + q4 + 1)
                    t = inc(vector.reduce_max(rcell(1024 + q4), psum[6][0:1, :],
                                              axis=mybir.AxisListType.X), "c_dve")
                wt(vector, "c_dve", t)
                inc(vector.reduce_max(rcell(1045 + g), rows[0:1, 1024:1032],
                                      axis=mybir.AxisListType.X), "c_dve")
            block.vector(_)
            read_bank(6, "c_dve", CNT["c_dve"])
        knorm_done = CNT["c_dve"]

        def _(vector):
            wt(vector, "c_dve", knorm_done)
            t = inc(vector.reduce_max(rcell(1049), rows[0:1, 1045:1049],
                                      axis=mybir.AxisListType.X), "c_dve")
            wt(vector, "c_dve", t)
            t = inc(vector.tensor_mul(rcell(1050), rcell(1049), qmx[0:1, 0:1]), "c_dve")
        block.vector(_)
        Vb = CNT["c_dve"]

        def _(scalar):
            wt(scalar, "c_dve", Vb)
            a = inc(scalar.activation(out=rcell(1051), in_=rcell(1050), func=AF.Ln),
                    "c_act")
            wt(scalar, "c_act", a)
            inc(scalar.activation(out=rcell(1052), in_=rcell(1051), func=AF.Exp,
                                  scale=0.5), "c_act")
        block.scalar(_)
        Ab = CNT["c_act"]

        def _(vector):
            wt(vector, "c_act", Ab)
            inc(vector.tensor_scalar(out=rcell(1053), in0=rcell(1052),
                                     scalar1=-1.001, scalar2=EXP_OFF,
                                     op0=ALU.mult, op1=ALU.add), "c_dve")
        block.vector(_)
        Vb2 = CNT["c_dve"]

        def _(tensor):
            wt(tensor, "c_dve", Vb2)
            bank_guard(tensor, 6)
            inc(tensor.matmul(psum[6][:, 0:1], onef[0:1, :], rcell(1053),
                              start=True, stop=True), "c_pe")
        block.tensor(_)
        Pb = CNT["c_pe"]

        def _(vector):
            wt(vector, "c_pe", Pb)
            inc(vector.tensor_copy(out=bexp[:, 0:1], in_=psum[6][:, 0:1]), "c_dve")
        block.vector(_)
        bexp_done = CNT["c_dve"]
        read_bank(6, "c_dve", bexp_done)

        vt_pe = state["vt_pe"]
        pk_all = CNT["d_pk"]
        for pi in range(8):
            g = pi % 4
            hd0 = g + 8 * (pi // 4)
            hd1 = hd0 + 4
            def _(sync, hd0=hd0, hd1=hd1):
                wt(sync, "c_pe", state.get("pair_scores_done", 0))
                inc(sync.dma_start(out=qh[0][:, :],
                                   in_=qpark[hd0 * 128:(hd0 + 1) * 128, :]), "d_qh0", 16)
                inc(sync.dma_start(out=qh[1][:, :],
                                   in_=qpark[hd1 * 128:(hd1 + 1) * 128, :]), "d_qh1", 16)
            block.sync(_)
            qt0 = CNT["d_qh0"]
            qt1 = CNT["d_qh1"]

            for blk in range(NCH):
                def _(sync, hd0=hd0, hd1=hd1, blk=blk):
                    wt(sync, "c_dve", state.get("x1_free_0", 0))
                    inc(sync.dma_start(
                        out=sg[0][:, :],
                        in_=xp[hd0 * 128:(hd0 + 1) * 128, blk * CH:(blk + 1) * CH]),
                        "d_x10", 16)
                    wt(sync, "c_dve", state.get("x1_free_1", 0))
                    inc(sync.dma_start(
                        out=sg[1][:, :],
                        in_=xp[hd1 * 128:(hd1 + 1) * 128, blk * CH:(blk + 1) * CH]),
                        "d_x11", 16)
                block.sync(_)
                x1t0 = CNT["d_x10"]
                x1t1 = CNT["d_x11"]

                P0s = CNT["c_pe"]
                A0s = CNT["c_act"]

                def scr_tag(tkt, h):
                    base = 2 * tkt + 2 * max(0, tkt - 2)
                    return P0s + base + h + 1

                def ctx_tag(t2):
                    return P0s + 2 * (t2 + 2) + 2 * t2 + 2 + 2

                def _(tensor, g=g, blk=blk, A0s=A0s, qt0=qt0, qt1=qt1,
                      bexp_done=bexp_done):
                    wt(tensor, "d_qh0", qt0)
                    wt(tensor, "d_qh1", qt1)
                    for b in (0, 1, 2, 3, 4, 5):
                        bank_guard(tensor, b)
                    for tkt in range(32):
                        if tkt >= 2:
                            wt(tensor, "c_act", A0s + tkt - 1)
                        lt = kT[:, g * SEQ + tkt * 128:g * SEQ + (tkt + 1) * 128]
                        inc(tensor.matmul(pbS[tkt % 2][:, 0:512], lt,
                                          qh[0][:, blk * CH:(blk + 1) * CH],
                                          start=True, stop=True), "c_pe")
                        inc(tensor.matmul(pbS[tkt % 2][:, 512:1024], lt,
                                          qh[1][:, blk * CH:(blk + 1) * CH],
                                          start=True, stop=True), "c_pe")
                        if tkt >= 2:
                            t2 = tkt - 2
                            wt(tensor, "c_act", A0s + t2 + 1)
                            vt = v_t[:, g * SEQ + t2 * 128:g * SEQ + (t2 + 1) * 128]
                            inc(tensor.matmul(psum[2][:, :], vt,
                                              pTP[:, t2 * 1024:t2 * 1024 + 512],
                                              start=(t2 == 0), stop=(t2 == 31)), "c_pe")
                            inc(tensor.matmul(psum[3][:, :], vt,
                                              pTP[:, t2 * 1024 + 512:(t2 + 1) * 1024],
                                              start=(t2 == 0), stop=(t2 == 31)), "c_pe")
                    for t2 in (30, 31):
                        wt(tensor, "c_act", A0s + t2 + 1)
                        vt = v_t[:, g * SEQ + t2 * 128:g * SEQ + (t2 + 1) * 128]
                        inc(tensor.matmul(psum[2][:, :], vt,
                                          pTP[:, t2 * 1024:t2 * 1024 + 512],
                                          start=(t2 == 0), stop=(t2 == 31)), "c_pe")
                        inc(tensor.matmul(psum[3][:, :], vt,
                                          pTP[:, t2 * 1024 + 512:(t2 + 1) * 1024],
                                          start=(t2 == 0), stop=(t2 == 31)), "c_pe")
                block.tensor(_)
                state["pair_scores_done"] = CNT["c_pe"]
                Pctx = CNT["c_pe"]

                def _(scalar, A0s=A0s, bexp_done=bexp_done, vt_pe=vt_pe):
                    wt(scalar, "c_dve", bexp_done)
                    wt(scalar, "c_pe", vt_pe)
                    for tkt in range(32):
                        wt(scalar, "c_pe", scr_tag(tkt, 1))
                        inc(scalar.activation(
                            out=pTP[:, tkt * 1024:(tkt + 1) * 1024],
                            in_=pbS[tkt % 2][:, 0:1024],
                            func=AF.Exp, bias=bexp[:, 0:1]), "c_act")
                block.scalar(_)
                assert CNT["c_act"] == A0s + 32
                for j, b in enumerate((0, 1, 4, 5)):
                    read_bank(b, "c_act", A0s + 31 + (j // 2))

                def _(tensor):
                    bank_guard(tensor, 6)
                    for tkt in range(32):
                        ins = tensor.matmul(psum[6][:, :], oneb[:, :],
                                            pTP[:, tkt * 1024:tkt * 1024 + 512],
                                            start=(tkt == 0), stop=(tkt == 31))
                    inc(ins, "c_pe")
                block.tensor(_)
                PdA = CNT["c_pe"]

                def _(vector, PdA=PdA, Pctx=Pctx, x1t0=x1t0):
                    wt(vector, "c_pe", PdA)
                    wt(vector, "d_x2a", CNT["d_x2a"])
                    wt(vector, "c_dve", state.get("epi_prev", 0))
                    t = inc(vector.reciprocal(uu[0][:, :], psum[6][:, :]), "c_dve")
                    wt(vector, "c_dve", t)
                    t = inc(vector.tensor_mul(uu[1][:, :], psum[2][:, :], uu[0][:, :]),
                            "c_dve")
                    wt(vector, "d_x10", x1t0)
                    wt(vector, "c_dve", t)
                    inc(vector.tensor_add(uu[0][:, :], uu[1][:, :], sg[0][:, :]), "c_dve")
                block.vector(_)
                recA = CNT["c_dve"] - 2
                state["x1_free_0"] = CNT["c_dve"]
                x2A = CNT["c_dve"]
                read_bank(2, "c_dve", x2A - 1)

                def _(sync, hd0=hd0, blk=blk, x2A=x2A):
                    wt(sync, "c_dve", x2A)
                    inc(sync.dma_start(
                        out=xp2[hd0 * 128:(hd0 + 1) * 128, blk * CH:(blk + 1) * CH],
                        in_=uu[0][:, :]), "d_x2a", 16)
                block.sync(_)

                def _(tensor, recA=recA):
                    wt(tensor, "c_dve", recA)
                    for tkt in range(32):
                        ins = tensor.matmul(psum[6][:, :], oneb[:, :],
                                            pTP[:, tkt * 1024 + 512:(tkt + 1) * 1024],
                                            start=(tkt == 0), stop=(tkt == 31))
                    inc(ins, "c_pe")
                block.tensor(_)
                PdB = CNT["c_pe"]
                bh = "B0" if blk % 2 == 0 else "B1"
                ohalf = (blk % 2) * 1024

                def _(vector, PdB=PdB, x1t1=x1t1, bh=bh, ohalf=ohalf):
                    wt(vector, "c_pe", PdB)
                    wt(vector, "d_pk", pk_all)
                    wt(vector, "d_x2b0" if bh == "B0" else "d_x2b1",
                       CNT["d_x2b0" if bh == "B0" else "d_x2b1"])
                    t = inc(vector.reciprocal(xch[:, ohalf:ohalf + 512], psum[6][:, :]),
                            "c_dve")
                    wt(vector, "c_dve", t)
                    t = inc(vector.tensor_mul(xch[:, ohalf + 512:ohalf + 1024],
                                              psum[3][:, :], xch[:, ohalf:ohalf + 512]),
                            "c_dve")
                    wt(vector, "d_x11", x1t1)
                    wt(vector, "c_dve", t)
                    inc(vector.tensor_add(xch[:, ohalf:ohalf + 512],
                                          xch[:, ohalf + 512:ohalf + 1024],
                                          sg[1][:, :]), "c_dve")
                block.vector(_)
                state["x1_free_1"] = CNT["c_dve"]
                state["epi_prev"] = CNT["c_dve"]
                x2B = CNT["c_dve"]
                read_bank(3, "c_dve", x2B - 1)
                read_bank(6, "c_dve", x2B - 2)

                def _(sync, hd1=hd1, blk=blk, x2B=x2B, ohalf=ohalf):
                    wt(sync, "c_dve", x2B)
                    inc(sync.dma_start(
                        out=xp2[hd1 * 128:(hd1 + 1) * 128, blk * CH:(blk + 1) * CH],
                        in_=xch[:, ohalf:ohalf + 512]),
                        "d_x2b0" if bh == "B0" else "d_x2b1", 16)
                block.sync(_)

        state["att_done"] = {s: CNT[s] for s in ("d_x2a", "d_x2b0", "d_x2b1")}
        state["att_done_pe"] = CNT["c_pe"]

    ph_consts()
    if stage == 11:
        x_ready = ph_load_x_transposed(0, 0, 0)
        ph_ln(x_ready)
        ph_ffn(0)
        def _(sync):
            wt(sync, "c_dve", state["ffn_done_dve"])
            inc(sync.dma_start(out=dbg_xn, in_=xn[:, :]), "d_out", 16)
            inc(sync.dma_start(out=dbg_hb, in_=hb[:, :]), "d_out", 16)
            inc(sync.dma_start(out=dbg_x, in_=xch[:, :]), "d_out", 16)
            inc(sync.dma_start(out=out[0:128, :], in_=xstg[:, 0:D]), "d_out", 16)
            inc(sync.dma_start(out=out[200:201, 0:2048], in_=ostg[0:1, :]), "d_out", 16)
            inc(sync.dma_start(out=out[210:211, 0:512], in_=sg[1][0:1, :]), "d_out", 16)
            wt(sync, "d_out", CNT["d_out"])
        block.sync(_)
        blockctx.__exit__(None, None, None)
        es.close()
        return nc
    park_tag = 0
    xch_free_pe = 0
    for ch in range(NCH):
        x_ready = ph_load_x_transposed(ch, xch_free_pe, park_tag)
        ph_ln(x_ready)
        ph_ffn(state["gu_prev_free"])
        state["gu_prev_free"] = state["gu_free_pe"]
        if eff >= 2:
            ph_ln(lambda ht: ("c_dve", state["ffn_done_dve"]))
            ph_proj(ch)
            xch_free_pe = state["proj_last_pe"]
        else:
            xch_free_pe = state["dw_free_pe"]
        park_tag = ph_park(ch)
        if eff == 1:
            ph_out(ch)
            xch_free_pe = state["out_pe"]

    if eff >= 2:
        ph_attention()
        state["gu_prev_free"] = state["att_done_pe"]
        for ch in range(NCH):
            x_ready = ph_load_x_hmajor(ch, xp2, xch_free_pe)
            if eff >= 3:
                ph_ln(x_ready)
                ph_ffn(state["gu_prev_free"])
                state["gu_prev_free"] = state["gu_free_pe"]
            else:
                def _(vector, xr=x_ready):
                    s, v = xr(0)
                    wt(vector, s, v)
                    inc(vector.tensor_copy(out=sg[0][:, 0:8], in_=xch[:, 0:8]), "c_dve")
                block.vector(_)
                state["ffn_done_dve"] = CNT["c_dve"]
            ph_out(ch)
            xch_free_pe = state["out_pe"]

    def _(sync):
        if stage == 12:
            for s, v in state["att_done"].items():
                wt(sync, s, v)
            inc(sync.dma_start(out=dbg_q, in_=qpark[:, :]), "d_out", 16)
            inc(sync.dma_start(out=dbg_kv, in_=kv_recv[:, :]), "d_out", 16)
            inc(sync.dma_start(out=dbg_xp, in_=xp[:, :]), "d_out", 16)
        wt(sync, "d_out", CNT["d_out"])
        wt(sync, "d_pk", CNT["d_pk"])
    block.sync(_)

    if stage == 12:
        def _(vector):
            inc(vector.tensor_copy(out=sg[0][:, 0:16], in_=bexp[:, :]), "c_dve")
        block.vector(_)

        def _(sync):
            wt(sync, "c_dve", CNT["c_dve"])
            inc(sync.dma_start(out=dbg_bexp, in_=sg[0][:, 0:16]), "d_out", 16)
            wt(sync, "d_out", CNT["d_out"])
        block.sync(_)

    blockctx.__exit__(None, None, None)
    es.close()
    return nc



def _byte0(w):
    a = np.ascontiguousarray(w)
    assert a.dtype == np.int32
    return a.view(np.uint8).reshape(a.shape[0], a.shape[1], 4)[:, :, 0]


def kernel(L0_attn_hidden_states, L0_attn_Wq, L0_attn_bq, L0_attn_Wk, L0_attn_bk,
           L0_attn_Wv, L0_attn_bv, L0_ffn_gate_w, L0_ffn_gate_scale, L0_ffn_up_w,
           L0_ffn_up_scale, L0_ffn_down_w, L0_ffn_down_scale, _stage=3):
    bf = ml_dtypes.bfloat16
    x = np.ascontiguousarray(np.asarray(L0_attn_hidden_states, dtype=np.float32)).reshape(16384, D)
    wq = np.ascontiguousarray(np.asarray(L0_attn_Wq, dtype=np.float32))
    wk = np.ascontiguousarray(np.asarray(L0_attn_Wk, dtype=np.float32))
    wv = np.ascontiguousarray(np.asarray(L0_attn_Wv, dtype=np.float32))
    bq = np.ascontiguousarray(np.asarray(L0_attn_bq, dtype=np.float32).reshape(16, 128).T)
    bk = np.ascontiguousarray(np.asarray(L0_attn_bk, dtype=np.float32).reshape(4, 128).T)
    bv = np.ascontiguousarray(np.asarray(L0_attn_bv, dtype=np.float32).reshape(4, 128).T)
    gT = np.ascontiguousarray(_byte0(np.asarray(L0_ffn_gate_w)).T)
    uT = np.ascontiguousarray(_byte0(np.asarray(L0_ffn_up_w)).T)
    dT = np.ascontiguousarray(_byte0(np.asarray(L0_ffn_down_w)).T)
    g4 = np.ascontiguousarray(gT.reshape(16, 128, 32, 256).transpose(2, 1, 0, 3))
    u4 = np.ascontiguousarray(uT.reshape(16, 128, 32, 256).transpose(2, 1, 0, 3))
    d4 = np.ascontiguousarray(dT.reshape(64, 128, 16, 128).transpose(2, 1, 0, 3))
    gsc = np.ascontiguousarray(np.asarray(L0_ffn_gate_scale, np.float32).reshape(64, 128).T)
    usc = np.ascontiguousarray(np.asarray(L0_ffn_up_scale, np.float32).reshape(64, 128).T)
    dsc = np.ascontiguousarray(np.asarray(L0_ffn_down_scale, np.float32).reshape(16, 128).T)
    idf = np.eye(128, dtype=np.float32)
    idb = np.eye(128, dtype=np.float32).astype(bf)
    onef = np.ones((128, 128), np.float32)
    oneb = np.ones((128, 128), np.float32).astype(bf)
    mrow = np.full((1, 128), -128.0, np.float32).astype(bf)

    nc = build(stage=_stage)
    shared = {"wq": wq, "wk": wk, "wv": wv, "bq_in": bq, "bk_in": bk, "bv_in": bv,
              "g4": g4, "u4": u4, "d4": d4, "gsc_in": gsc, "usc_in": usc,
              "dsc_in": dsc, "idf_in": idf, "idb_in": idb, "onef_in": onef,
              "oneb_in": oneb, "mrow_in": mrow}
    in_maps = [dict(shared, x_in=np.ascontiguousarray(x[c * TPC:(c + 1) * TPC]))
               for c in range(8)]
    res = run_bass_kernel_spmd(nc, in_maps, core_ids=list(range(8)))
    full = np.concatenate([np.asarray(r["out"], np.float32) for r in res.results], axis=0)
    return full.reshape(4, 4096, D)


# version 09
# speedup vs baseline: 1.1023x; 1.0359x over previous
import contextlib
import numpy as np
import ml_dtypes
import concourse.bass as bass
import concourse.mybir as mybir
from concourse.bass_utils import run_bass_kernel_spmd

F32 = mybir.dt.float32
BF16 = mybir.dt.bfloat16
U8 = mybir.dt.uint8
AF = mybir.ActivationFunctionType
ALU = mybir.AluOpType

D = 2048
I = 8192
TPC = 2048
CH = 512
NCH = TPC // CH
NHT = D // 128
NIT = I // 128
HQ, HKV, HD = 16, 4, 128
SEQ = 4096
SCALE = 1.0 / float(np.sqrt(128.0))
EXP_OFF = 60.0

OFF_XCH = 0x04000
OFF_XN = 0x0C000
OFF_HB = 0x10000
OFF_KT = 0x10000
OFF_VT = 0x18000
OFF_XSTG = 0x10000
OFF_GW0 = 0x20000
OFF_GW1 = 0x22000
OFF_UW0 = 0x24000
OFF_UW1 = 0x26000
OFF_DW0 = 0x28000
OFF_DW1 = 0x2C000
OFF_PT = 0x20000
OFF_VTMP = 0x28000
OFF_QSQ = 0x28000
OFF_WQ0 = 0x30000
OFF_WQ1 = 0x31000
OFF_OSTG = 0x30000
OFF_IDF = 0x32000
OFF_IDB = 0x32200
OFF_ONEF = 0x32300
OFF_ONEB = 0x32500
OFF_MROW = 0x32600
OFF_BQ = 0x32700
OFF_BQS = 0x32740
OFF_BK = 0x32780
OFF_BV = 0x327A0
OFF_GSC = 0x327C0
OFF_USC = 0x328C0
OFF_DSC = 0x329C0
OFF_DSCH = 0x32A00
OFF_BEXP = 0x32A40
OFF_ROWSB = 0x32A80
OFF_ROWS = 0x33000
OFF_SG0 = 0x34800
OFF_SG1 = 0x35000
OFF_UU0 = 0x35800
OFF_UU1 = 0x36000
OFF_PST0 = 0x36800
OFF_PST1 = 0x36C00
OFF_SQ0 = 0x37000
OFF_SQ1 = 0x37400

SEM_NAMES = [
    "d_x", "d_c", "d_pk", "d_out", "d_kv",
    "d_q0", "d_q1", "d_gu0", "d_gu1", "d_dw0", "d_dw1", "d_wq0", "d_wq1",
    "d_qh0", "d_qh1", "d_x10", "d_x11", "d_x2a", "d_x2b0", "d_x2b1",
    "s_cc", "c_pe", "c_act", "c_dve",
]


def build(stage=3):
    eff = 2 if stage == 12 else stage
    nc = bass.Bass("TRN2", target_bir_lowering=False)

    x_in = nc.dram_tensor("x_in", [TPC, D], F32, kind="ExternalInput").ap()
    wq = nc.dram_tensor("wq", [D, D], F32, kind="ExternalInput").ap()
    wk = nc.dram_tensor("wk", [D, 512], F32, kind="ExternalInput").ap()
    wv = nc.dram_tensor("wv", [D, 512], F32, kind="ExternalInput").ap()
    bq_in = nc.dram_tensor("bq_in", [128, 16], F32, kind="ExternalInput").ap()
    bk_in = nc.dram_tensor("bk_in", [128, 4], F32, kind="ExternalInput").ap()
    bv_in = nc.dram_tensor("bv_in", [128, 4], F32, kind="ExternalInput").ap()
    g4 = nc.dram_tensor("g4", [32, 128, 16, 256], U8, kind="ExternalInput").ap()
    u4 = nc.dram_tensor("u4", [32, 128, 16, 256], U8, kind="ExternalInput").ap()
    d4 = nc.dram_tensor("d4", [16, 128, 64, 128], U8, kind="ExternalInput").ap()
    gsc_in = nc.dram_tensor("gsc_in", [128, 64], F32, kind="ExternalInput").ap()
    usc_in = nc.dram_tensor("usc_in", [128, 64], F32, kind="ExternalInput").ap()
    dsc_in = nc.dram_tensor("dsc_in", [128, 16], F32, kind="ExternalInput").ap()
    idf_in = nc.dram_tensor("idf_in", [128, 128], F32, kind="ExternalInput").ap()
    idb_in = nc.dram_tensor("idb_in", [128, 128], BF16, kind="ExternalInput").ap()
    onef_in = nc.dram_tensor("onef_in", [128, 128], F32, kind="ExternalInput").ap()
    oneb_in = nc.dram_tensor("oneb_in", [128, 128], BF16, kind="ExternalInput").ap()
    mrow_in = nc.dram_tensor("mrow_in", [1, 128], BF16, kind="ExternalInput").ap()
    out = nc.dram_tensor("out", [TPC, D], F32, kind="ExternalOutput").ap()
    if stage == 11:
        dbg_xn = nc.dram_tensor("dbg_xn", [128, NHT * CH], BF16, kind="ExternalOutput").ap()
        dbg_hb = nc.dram_tensor("dbg_hb", [128, NIT * CH], BF16, kind="ExternalOutput").ap()
        dbg_x = nc.dram_tensor("dbg_x", [128, NHT * CH], F32, kind="ExternalOutput").ap()
    if stage == 12:
        dbg_q = nc.dram_tensor("dbg_q", [D, TPC], BF16, kind="ExternalOutput").ap()
        dbg_kv = nc.dram_tensor("dbg_kv", [2048, TPC], BF16, kind="ExternalOutput").ap()
        dbg_bexp = nc.dram_tensor("dbg_bexp", [128, 16], F32, kind="ExternalOutput").ap()
        dbg_xp = nc.dram_tensor("dbg_xp", [D, TPC], F32, kind="ExternalOutput").ap()
    xp = nc.dram_tensor("xp", [D, TPC], F32, kind="Internal").ap()
    xp2 = nc.dram_tensor("xp2", [D, TPC], F32, kind="Internal").ap()
    qpark = nc.dram_tensor("qpark", [D, TPC], BF16, kind="Internal").ap()
    kv_send = nc.dram_tensor("kv_send", [1024, TPC], BF16, kind="Internal").ap()
    kv_recv = nc.dram_tensor("kv_recv", [2048, TPC], BF16, kind="Internal").ap()

    SH = 0x80
    _MAP_END = 0x37800 + SH
    nc.alloc_sbuf_tensor("mapspacer", [128, _MAP_END - nc.sbuf_base], U8)
    assert nc.sbuf_base >= _MAP_END, hex(nc.sbuf_base)

    def A(name, shape, dtype, offset):
        return nc.alloc_sbuf_tensor_at(name, shape, dtype, offset=offset + SH)

    xch = A("xch", [128, NHT * CH], F32, offset=OFF_XCH)
    xn = A("xn", [128, NHT * CH], BF16, offset=OFF_XN)
    hb = A("hb", [128, NIT * CH], BF16, offset=OFF_HB)
    kT = A("kTs", [128, 4 * SEQ], BF16, offset=OFF_KT)
    v_t = A("v_ts", [128, 4 * SEQ], BF16, offset=OFF_VT)
    xstg = A("xstg", [128, 4 * D], F32, offset=OFF_XSTG)
    gw = [A("gw0", [128, 4096], BF16, offset=OFF_GW0),
          A("gw1", [128, 4096], BF16, offset=OFF_GW1)]
    uw = [A("uw0", [128, 4096], BF16, offset=OFF_UW0),
          A("uw1", [128, 4096], BF16, offset=OFF_UW1)]
    dw = [A("dw0", [128, 8192], BF16, offset=OFF_DW0),
          A("dw1", [128, 8192], BF16, offset=OFF_DW1)]
    pT = A("pTs", [128, 32 * CH], BF16, offset=OFF_PT)
    pTP = A("pTP", [128, 64 * CH], BF16, offset=OFF_PT)
    vtmp = A("vtmp", [128, 4 * SEQ], BF16, offset=OFF_VTMP)
    qsq = A("qsq", [128, SEQ], BF16, offset=OFF_QSQ)
    wqb = [A("wqb0", [128, 2048], BF16, offset=OFF_WQ0),
           A("wqb1", [128, 2048], BF16, offset=OFF_WQ1)]
    qh = wqb
    ostg = A("ostg", [128, D], F32, offset=OFF_OSTG)
    idf = A("idf", [128, 128], F32, offset=OFF_IDF)
    idb = A("idb", [128, 128], BF16, offset=OFF_IDB)
    onef = A("onef", [128, 128], F32, offset=OFF_ONEF)
    oneb = A("oneb", [128, 128], BF16, offset=OFF_ONEB)
    mrow = A("mrow", [1, 128], BF16, offset=OFF_MROW)
    bq_s = A("bq_s", [128, 16], F32, offset=OFF_BQ)
    bqs = A("bqs", [128, 16], F32, offset=OFF_BQS)
    bk_s = A("bk_s", [128, 4], F32, offset=OFF_BK)
    bv_s = A("bv_s", [128, 4], F32, offset=OFF_BV)
    gsc = A("gsc", [128, 64], F32, offset=OFF_GSC)
    usc = A("usc", [128, 64], F32, offset=OFF_USC)
    dsc = A("dsc", [128, 16], F32, offset=OFF_DSC)
    dsch = A("dsch", [128, 16], F32, offset=OFF_DSCH)
    bexp = A("bexp", [128, 16], F32, offset=OFF_BEXP)
    rowsb = A("rowsb", [128, 512], BF16, offset=OFF_ROWSB)
    qmx = A("qmx", [128, 16], F32, offset=OFF_ROWSB + 0x400)
    rows = A("rows", [128, 1536], F32, offset=OFF_ROWS)
    lob = A("lob", [1, 512], BF16, offset=OFF_ROWS + 0x1000)
    sg = [A("sg0", [128, CH], F32, offset=OFF_SG0),
          A("sg1", [128, CH], F32, offset=OFF_SG1)]
    uu = [A("uu0", [128, CH], F32, offset=OFF_UU0),
          A("uu1", [128, CH], F32, offset=OFF_UU1)]
    pst = [A("pst0", [128, CH], BF16, offset=OFF_PST0),
           A("pst1", [128, CH], BF16, offset=OFF_PST1)]
    sqst = [A("sq0", [128, CH], BF16, offset=OFF_SQ0),
            A("sq1", [128, CH], BF16, offset=OFF_SQ1)]

    es = contextlib.ExitStack()
    pb01 = es.enter_context(nc.psum_tensor("pb01", [128, 1024], F32))
    pb23 = es.enter_context(nc.psum_tensor("pb23", [128, 1024], F32))
    pb45 = es.enter_context(nc.psum_tensor("pb45", [128, 1024], F32))
    pb6 = es.enter_context(nc.psum_tensor("pb6", [128, 512], F32))
    psb = es.enter_context(nc.psum_tensor("pb7", [128, 1024], BF16))
    psum = [pb01[:, 0:512], pb01[:, 512:1024], pb23[:, 0:512], pb23[:, 512:1024],
            pb45[:, 0:512], pb45[:, 512:1024], pb6[:, :]]
    pbS = [pb01, pb45]
    SEM = {n: es.enter_context(nc.semaphore(n)) for n in SEM_NAMES}
    CNT = {n: 0 for n in SEM_NAMES}
    LASTW = {}
    BANKR = [{"c_act": 0, "c_dve": 0} for _ in range(8)]

    def inc(ins, sem, k=1):
        ins.then_inc(SEM[sem], k)
        CNT[sem] += k
        return CNT[sem]

    def wt(eng, sem, val):
        if val is None or val <= 0:
            return
        key = (id(eng), sem)
        if LASTW.get(key, -1) < val:
            eng.wait_ge(SEM[sem], val)
            LASTW[key] = val

    def bank_guard(tensor, b):
        wt(tensor, "c_act", BANKR[b]["c_act"])
        wt(tensor, "c_dve", BANKR[b]["c_dve"])

    def read_bank(b, sem, tag):
        BANKR[b][sem] = max(BANKR[b][sem], tag)

    def xch_t(ht):
        return xch[:, ht * CH:(ht + 1) * CH]

    def xn_t(ht):
        return xn[:, ht * CH:(ht + 1) * CH]

    def hb_t(it):
        return hb[:, it * CH:(it + 1) * CH]

    def row(i):
        return rows[0:1, i * 512:(i + 1) * 512]

    def rcell(c0, n=1):
        return rows[0:1, c0:c0 + n]

    blockctx = nc.Block()
    block = blockctx.__enter__()
    state = {"xn_ready": 0, "ffn_done_dve": 0, "wq_free_pe": 0,
             "gu_prev_free": 0, "x2_store_tag": 0, "scores_done_pe": 0}

    def ph_consts():
        def _(sync):
            for dst, src in [(idf, idf_in), (idb, idb_in), (onef, onef_in),
                             (oneb, oneb_in), (bq_s, bq_in), (bk_s, bk_in),
                             (bv_s, bv_in), (gsc, gsc_in), (usc, usc_in),
                             (dsc, dsc_in)]:
                inc(sync.dma_start(out=dst[:, :], in_=src), "d_c", 16)
            inc(sync.dma_start(out=mrow[0:1, :], in_=mrow_in), "d_c", 16)
        block.sync(_)
        state["consts"] = CNT["d_c"]

        def _(vector):
            wt(vector, "d_c", state["consts"])
            inc(vector.tensor_scalar_mul(bqs[:, :], bq_s[:, :], SCALE), "c_dve")
            inc(vector.tensor_scalar_mul(dsch[:, :], dsc[:, :], 0.5), "c_dve")
            inc(vector.memset(qmx[:, :], 0.0), "c_dve")
        block.vector(_)
        state["consts_dve"] = CNT["c_dve"]

    def ph_ln(x_ready):
        A0 = CNT["c_act"]
        P0 = CNT["c_pe"]

        def _(scalar):
            for ht in range(NHT):
                s, v = x_ready(ht)
                wt(scalar, s, v)
                if ht >= 2:
                    wt(scalar, "c_pe", P0 + 2 * (ht - 2) + 2)
                inc(scalar.activation(out=xn_t(ht), in_=xch_t(ht), func=AF.Copy), "c_act")
                inc(scalar.activation(out=sqst[ht % 2][:, :], in_=xch_t(ht),
                                      func=AF.Square), "c_act")
        block.scalar(_)

        def _(tensor):
            wt(tensor, "d_c", state["consts"])
            bank_guard(tensor, 2)
            bank_guard(tensor, 3)
            for ht in range(NHT):
                wt(tensor, "c_act", A0 + 2 * ht + 1)
                inc(tensor.matmul(psum[2][0:1, :], oneb[:, 0:1], xn_t(ht),
                                  start=(ht == 0), stop=(ht == NHT - 1)), "c_pe")
                wt(tensor, "c_act", A0 + 2 * ht + 2)
                inc(tensor.matmul(psum[3][0:1, :], oneb[:, 0:1], sqst[ht % 2][:, :],
                                  start=(ht == 0), stop=(ht == NHT - 1)), "c_pe")
        block.tensor(_)
        pe_stats = CNT["c_pe"]
        assert pe_stats == P0 + 2 * NHT

        def _(vector):
            wt(vector, "c_pe", pe_stats)
            if stage == 11:
                inc(vector.tensor_copy(out=ostg[0:1, 0:512], in_=psum[2][0:1, :]), "c_dve")
                inc(vector.tensor_copy(out=ostg[0:1, 512:1024], in_=psum[3][0:1, :]), "c_dve")
            t1 = inc(vector.tensor_scalar_mul(row(0), psum[2][0:1, :], 1.0 / D), "c_dve")
            t2 = inc(vector.tensor_scalar_mul(row(1), psum[3][0:1, :], 1.0 / D), "c_dve")
            wt(vector, "c_dve", t1)
            t3 = inc(vector.tensor_mul(row(2), row(0), row(0)), "c_dve")
            wt(vector, "c_dve", t3)
            inc(vector.scalar_tensor_tensor(out=row(1), in0=row(1), scalar=1e-5,
                                            in1=row(2), op0=ALU.add, op1=ALU.subtract),
                "c_dve")
        block.vector(_)
        dve_var = CNT["c_dve"]
        read_bank(2, "c_dve", dve_var)
        read_bank(3, "c_dve", dve_var)
        if stage == 11:
            def _(vector):
                wt(vector, "c_dve", dve_var)
                inc(vector.tensor_copy(out=ostg[0:1, 1024:1536], in_=row(1)), "c_dve")
            block.vector(_)
            dve_var = CNT["c_dve"]

        def _(scalar):
            wt(scalar, "c_dve", dve_var)
            a1 = inc(scalar.activation(out=row(2), in_=row(1), func=AF.Ln), "c_act")
            wt(scalar, "c_act", a1)
            inc(scalar.activation(out=row(1), in_=row(2), func=AF.Exp, scale=-0.5), "c_act")
        block.scalar(_)
        if stage == 11:
            def _(vector):
                wt(vector, "c_act", CNT["c_act"])
                inc(vector.tensor_copy(out=sg[1][0:1, :], in_=row(2)), "c_dve")
            block.vector(_)
        act_rstd = CNT["c_act"]
        if stage == 11:
            def _(vector):
                wt(vector, "c_act", act_rstd)
                inc(vector.tensor_copy(out=ostg[0:1, 1536:2048], in_=row(1)), "c_dve")
            block.vector(_)

        def _(vector):
            wt(vector, "c_act", act_rstd)
            inc(vector.scalar_tensor_tensor(out=row(2), in0=row(0), scalar=-1.0,
                                            in1=row(1), op0=ALU.mult, op1=ALU.mult), "c_dve")
        block.vector(_)
        dve_nmr = CNT["c_dve"]

        def _(tensor):
            wt(tensor, "c_dve", dve_nmr)
            bank_guard(tensor, 2)
            bank_guard(tensor, 3)
            inc(tensor.matmul(psum[2][:, :], onef[0:1, :], row(1), start=True, stop=True), "c_pe")
            inc(tensor.matmul(psum[3][:, :], onef[0:1, :], row(2), start=True, stop=True), "c_pe")
        block.tensor(_)
        pe_bc = CNT["c_pe"]

        def _(vector):
            wt(vector, "c_pe", pe_bc)
            prev_add = 0
            tags = []
            for ht in range(NHT):
                wt(vector, "c_dve", prev_add)
                t = inc(vector.tensor_mul(sg[0][:, :], xch_t(ht), psum[2][:, :]), "c_dve")
                wt(vector, "c_dve", t)
                prev_add = inc(vector.tensor_add(xn_t(ht), sg[0][:, :], psum[3][:, :]), "c_dve")
                tags.append(prev_add)
            state["xn_tags"] = tags
        block.vector(_)
        state["xn_ready"] = CNT["c_dve"]
        read_bank(2, "c_dve", state["xn_ready"])
        read_bank(3, "c_dve", state["xn_ready"])

    def ph_ffn(warena_free_pe):
        NB = 32
        P0 = CNT["c_pe"]
        A0 = CNT["c_act"]
        V0 = CNT["c_dve"]
        gu_base = [CNT["d_gu0"], CNT["d_gu1"]]

        def _pe_before(it):
            return 2 * it + max(0, it - 2)

        def tag_g(it):
            return P0 + _pe_before(it) + 1

        def tag_u(it):
            return P0 + _pe_before(it) + 2

        def tag_silu(it):
            return A0 + it + 1

        def tag_uu(it):
            return V0 + 2 * it + 1

        def tag_h(it):
            return V0 + 2 * it + 2

        def _(gpsimd):
            for ib in range(NB):
                if ib >= 2:
                    wt(gpsimd, "c_pe", tag_u(2 * ib - 3))
                else:
                    wt(gpsimd, "c_pe", warena_free_pe)
                sem = "d_gu0" if ib % 2 == 0 else "d_gu1"
                inc(gpsimd.dma_start(
                    out=gw[ib % 2][:].rearrange("p (n c) -> p n c", c=256), in_=g4[ib]), sem, 16)
                inc(gpsimd.dma_start(
                    out=uw[ib % 2][:].rearrange("p (n c) -> p n c", c=256), in_=u4[ib]), sem, 16)
        block.gpsimd(_)

        xtags = state["xn_tags"]

        def _(tensor):
            bank_guard(tensor, 6)
            for it in range(2 * NB):
                ib = it // 2
                sem = "d_gu0" if ib % 2 == 0 else "d_gu1"
                wt(tensor, sem, gu_base[ib % 2] + 32 * (ib // 2 + 1))
                if it >= 2:
                    wt(tensor, "c_act", tag_silu(it - 2))
                    wt(tensor, "c_dve", tag_uu(it - 2))
                else:
                    bank_guard(tensor, it % 2)
                    bank_guard(tensor, 2 + it % 2)
                sl = (it % 2) * 128
                for ht in range(NHT):
                    wt(tensor, "c_dve", xtags[ht])
                    ins = tensor.matmul(
                        psum[it % 2][:, :],
                        gw[ib % 2][:, ht * 256 + sl:ht * 256 + sl + 128],
                        xn_t(ht), start=(ht == 0), stop=(ht == NHT - 1))
                inc(ins, "c_pe")
                for ht in range(NHT):
                    ins = tensor.matmul(
                        psum[2 + it % 2][:, :],
                        uw[ib % 2][:, ht * 256 + sl:ht * 256 + sl + 128],
                        xn_t(ht), start=(ht == 0), stop=(ht == NHT - 1))
                inc(ins, "c_pe")
                if it >= 2:
                    wt(tensor, "c_dve", tag_h(it - 2))
                    inc(tensor.matmul(psum[6][0:1, :], oneb[:, 0:1], hb_t(it - 2),
                                      start=(it == 2), stop=False), "c_pe")
        block.tensor(_)

        def _(tensor):
            for it in range(2 * NB - 2, 2 * NB):
                wt(tensor, "c_dve", tag_h(it))
                inc(tensor.matmul(psum[6][0:1, :], oneb[:, 0:1], hb_t(it),
                                  start=False, stop=(it == 2 * NB - 1)), "c_pe")
        block.tensor(_)
        assert CNT["c_pe"] == P0 + 4 * NB + (2 * NB - 2) + 2, (CNT["c_pe"], P0)

        def _(scalar):
            wt(scalar, "d_c", state["consts"])
            for it in range(2 * NB):
                wt(scalar, "c_pe", tag_g(it))
                if it >= 2:
                    wt(scalar, "c_dve", tag_h(it - 2))
                inc(scalar.activation(out=sg[it % 2][:, :], in_=psum[it % 2][:, :],
                                      func=AF.Silu, scale=gsc[:, it:it + 1]), "c_act")
        block.scalar(_)
        assert CNT["c_act"] == A0 + 2 * NB

        def _(vector):
            wt(vector, "d_c", state["consts"])
            for it in range(2 * NB):
                wt(vector, "c_pe", tag_u(it))
                if it >= 2:
                    wt(vector, "c_dve", tag_h(it - 2))
                t = inc(vector.tensor_scalar_mul(uu[it % 2][:, :], psum[2 + it % 2][:, :],
                                                 usc[:, it:it + 1]), "c_dve")
                wt(vector, "c_act", tag_silu(it))
                wt(vector, "c_dve", t)
                inc(vector.tensor_mul(hb_t(it), sg[it % 2][:, :], uu[it % 2][:, :]), "c_dve")
        block.vector(_)
        assert CNT["c_dve"] == V0 + 4 * NB
        for b in range(4):
            read_bank(b, "c_act", A0 + 2 * NB)
            read_bank(b, "c_dve", V0 + 4 * NB)

        pe_csh = CNT["c_pe"]

        def _(vector):
            wt(vector, "c_pe", pe_csh)
            t1 = inc(vector.tensor_copy(out=row(0), in_=psum[6][0:1, :]), "c_dve")
            wt(vector, "c_dve", t1)
            t2 = inc(vector.tensor_copy(out=rowsb[0:1, :], in_=row(0)), "c_dve")
            wt(vector, "c_dve", t2)
            inc(vector.tensor_sub(lob[0:1, :], row(0), rowsb[0:1, :]), "c_dve")
        block.vector(_)
        csh_ready = CNT["c_dve"]
        read_bank(6, "c_dve", csh_ready)

        Pd0 = CNT["c_pe"]
        Vd0 = CNT["c_dve"]
        dw_base = [CNT["d_dw0"], CNT["d_dw1"]]

        def tag_d(hht):
            return Pd0 + hht + 1

        def tag_stt(hht):
            return Vd0 + hht + 1

        def _(gpsimd):
            for hht in range(NHT):
                if hht >= 2:
                    wt(gpsimd, "c_pe", tag_d(hht - 2))
                sem = "d_dw0" if hht % 2 == 0 else "d_dw1"
                inc(gpsimd.dma_start(
                    out=dw[hht % 2][:].rearrange("p (n c) -> p n c", c=128), in_=d4[hht]), sem, 16)
        block.gpsimd(_)

        def _(tensor):
            wt(tensor, "c_dve", csh_ready)
            bank_guard(tensor, 4)
            bank_guard(tensor, 5)
            for hht in range(NHT):
                sem = "d_dw0" if hht % 2 == 0 else "d_dw1"
                wt(tensor, sem, dw_base[hht % 2] + 16 * (hht // 2 + 1))
                if hht >= 2:
                    wt(tensor, "c_dve", tag_stt(hht - 2))
                tensor.matmul(psum[4 + hht % 2][:, :], mrow[0:1, :], rowsb[0:1, :],
                              start=True, stop=False)
                tensor.matmul(psum[4 + hht % 2][:, :], mrow[0:1, :], lob[0:1, :],
                              start=False, stop=False)
                for it in range(NIT):
                    ins = tensor.matmul(
                        psum[4 + hht % 2][:, :],
                        dw[hht % 2][:, it * 128:(it + 1) * 128],
                        hb_t(it), start=False, stop=(it == NIT - 1))
                inc(ins, "c_pe")
        block.tensor(_)
        assert CNT["c_pe"] == Pd0 + NHT

        def _(vector):
            for hht in range(NHT):
                wt(vector, "c_pe", tag_d(hht))
                inc(vector.scalar_tensor_tensor(
                    out=xch_t(hht), in0=psum[4 + hht % 2][:, :], scalar=dsch[:, hht:hht + 1],
                    in1=xch_t(hht), op0=ALU.mult, op1=ALU.add), "c_dve")
        block.vector(_)
        assert CNT["c_dve"] == Vd0 + NHT
        state["stt_tags"] = [tag_stt(h) for h in range(NHT)]
        state["ffn_done_dve"] = CNT["c_dve"]
        read_bank(4, "c_dve", Vd0 + NHT)
        read_bank(5, "c_dve", Vd0 + NHT)
        state["gu_free_pe"] = tag_u(2 * NB - 1)
        state["dw_free_pe"] = CNT["c_pe"]

    def ph_load_x_transposed(ch, xch_free_pe, park_tag):
        def _(sync):
            wt(sync, "c_pe", xch_free_pe)
            wt(sync, "d_pk", park_tag)
            for tt in range(4):
                inc(sync.dma_start(out=xstg[:, tt * D:(tt + 1) * D],
                                   in_=x_in[ch * CH + tt * 128: ch * CH + (tt + 1) * 128, :]),
                    "d_x", 16)
        block.sync(_)
        xload = CNT["d_x"]
        P0 = CNT["c_pe"]
        V0 = CNT["c_dve"]

        def _(tensor):
            wt(tensor, "d_c", state["consts"])
            wt(tensor, "d_x", xload)
            bank_guard(tensor, 4)
            bank_guard(tensor, 5)
            for ht in range(NHT):
                if ht >= 2:
                    wt(tensor, "c_dve", V0 + ht - 1)
                for tt in range(4):
                    ins = tensor.transpose(
                        psum[4 + ht % 2][:, tt * 128:(tt + 1) * 128],
                        xstg[:, tt * D + ht * 128: tt * D + (ht + 1) * 128],
                        idf[:, :])
                inc(ins, "c_pe")
        block.tensor(_)

        def _(vector):
            for ht in range(NHT):
                wt(vector, "c_pe", P0 + ht + 1)
                inc(vector.tensor_copy(out=xch_t(ht), in_=psum[4 + ht % 2][:, :]), "c_dve")
        block.vector(_)
        read_bank(4, "c_dve", V0 + NHT)
        read_bank(5, "c_dve", V0 + NHT)
        return lambda ht: ("c_dve", V0 + ht + 1)

    def ph_load_x_hmajor(ch, src, xch_free_pe):
        def _(sync):
            wt(sync, "c_pe", xch_free_pe)
            for s, v in state["att_done"].items():
                wt(sync, s, v)
            inc(sync.dma_start(
                out=xch[:].rearrange("p (n c) -> p n c", c=CH),
                in_=src.rearrange("(n p) t -> p n t", p=128)[:, :, ch * CH:(ch + 1) * CH]),
                "d_x", 16)
        block.sync(_)
        xload = CNT["d_x"]
        return lambda ht: ("d_x", xload)

    def ph_proj(ch):
        NS = 24
        P0 = CNT["c_pe"]
        wq_base = [CNT["d_wq0"], CNT["d_wq1"]]
        q_base = [CNT["d_q0"], CNT["d_q1"]]

        def _pe_before(s):
            return s + min(max(0, s - 2), 16)

        def tag_p(s):
            return P0 + _pe_before(s) + 1

        def tag_nm(n):
            return P0 + _pe_before(n + 2) + 2

        def _(gpsimd):
            for s in range(NS):
                if s >= 2:
                    wt(gpsimd, "c_pe", tag_p(s - 2))
                else:
                    wt(gpsimd, "c_pe", state["wq_free_pe"])
                sem = "d_wq0" if s % 2 == 0 else "d_wq1"
                if s < 16:
                    src = wq.rearrange("(n p) c -> p n c", p=128)[:, :, s * 128:(s + 1) * 128]
                elif s < 20:
                    dt = s - 16
                    src = wk.rearrange("(n p) c -> p n c", p=128)[:, :, dt * 128:(dt + 1) * 128]
                else:
                    dt = s - 20
                    src = wv.rearrange("(n p) c -> p n c", p=128)[:, :, dt * 128:(dt + 1) * 128]
                inc(gpsimd.dma_start(
                    out=wqb[s % 2][:].rearrange("p (n c) -> p n c", c=128), in_=src), sem, 16)
        block.gpsimd(_)

        V0 = CNT["c_dve"]

        def _v_incs(s):
            n = 0
            for t in range(s):
                n += 1
                if t < 16:
                    n += 1
                if 2 <= t < 18:
                    n += 2
            return n

        def tag_e(s):
            return V0 + _v_incs(s) + 1

        def tag_sq(s):
            return tag_e(s) + 1

        def tag_red(n):
            return tag_e(n + 2) + 1 + (1 if (n + 2) < 16 else 0)

        xtags = state["xn_tags"]

        def _(tensor):
            bank_guard(tensor, 6)
            for s in range(NS):
                sem = "d_wq0" if s % 2 == 0 else "d_wq1"
                wt(tensor, sem, wq_base[s % 2] + 16 * (s // 2 + 1))
                if s >= 2:
                    wt(tensor, "c_dve", tag_e(s - 2))
                else:
                    bank_guard(tensor, s % 2)
                for ht in range(NHT):
                    wt(tensor, "c_dve", xtags[ht])
                    ins = tensor.matmul(
                        psum[s % 2][:, :],
                        wqb[s % 2][:, ht * 128:(ht + 1) * 128],
                        xn_t(ht), start=(ht == 0), stop=(ht == NHT - 1))
                inc(ins, "c_pe")
                assert CNT["c_pe"] == tag_p(s), (s, CNT["c_pe"], tag_p(s))
                if s >= 2 and s - 2 < 16:
                    n = s - 2
                    wt(tensor, "c_dve", tag_sq(n))
                    if n >= 1:
                        wt(tensor, "c_dve", tag_red(n - 1))
                    inc(tensor.matmul(psum[6][0:1, :], oneb[:, 0:1], sqst[n % 2][:, :],
                                      start=True, stop=True), "c_pe")
        block.tensor(_)

        def tag_ev(s):
            return tag_e(s)

        def _(vector):
            wt(vector, "c_dve", state["consts_dve"])
            for s in range(NS):
                wt(vector, "c_pe", tag_p(s))
                semd = "d_q0" if s % 2 == 0 else "d_q1"
                wt(vector, semd, q_base[s % 2] + 16 * (s // 2))
                if s < 16:
                    e = inc(vector.tensor_scalar(out=pst[s % 2][:, :], in0=psum[s % 2][:, :],
                                                 scalar1=SCALE, scalar2=bqs[:, s:s + 1],
                                                 op0=ALU.mult, op1=ALU.add), "c_dve")
                elif s < 20:
                    inc(vector.tensor_scalar_add(pst[s % 2][:, :], psum[s % 2][:, :],
                                                 bk_s[:, s - 16:s - 15]), "c_dve")
                else:
                    inc(vector.tensor_scalar_add(pst[s % 2][:, :], psum[s % 2][:, :],
                                                 bv_s[:, s - 20:s - 19]), "c_dve")
                assert CNT["c_dve"] == tag_e(s), (s, CNT["c_dve"], tag_e(s))
                if s < 16:
                    wt(vector, "c_dve", e)
                    wt(vector, "c_pe", tag_nm(s - 2) if s >= 2 else 0)
                    inc(vector.tensor_mul(sqst[s % 2][:, :], pst[s % 2][:, :],
                                          pst[s % 2][:, :]), "c_dve")
                if 2 <= s < 18:
                    n = s - 2
                    wt(vector, "c_pe", tag_nm(n))
                    if n >= 1:
                        wt(vector, "c_dve", tag_red(n - 1) + 1)
                    t = inc(vector.reduce_max(qmx[0:1, 1:2], psum[6][0:1, :],
                                              axis=mybir.AxisListType.X), "c_dve")
                    wt(vector, "c_dve", t)
                    inc(vector.tensor_max(qmx[0:1, 0:1], qmx[0:1, 0:1],
                                          qmx[0:1, 1:2]), "c_dve")
        block.vector(_)
        read_bank(6, "c_dve", CNT["c_dve"])

        def _(scalar):
            for s in range(NS):
                wt(scalar, "c_dve", tag_ev(s))
                sem = "d_q0" if s % 2 == 0 else "d_q1"
                if s < 16:
                    dst = qpark[s * 128:(s + 1) * 128, ch * CH:(ch + 1) * CH]
                elif s < 20:
                    dst = kv_send[(s - 16) * 128:(s - 15) * 128, ch * CH:(ch + 1) * CH]
                else:
                    dst = kv_send[512 + (s - 20) * 128:512 + (s - 19) * 128,
                                  ch * CH:(ch + 1) * CH]
                inc(scalar.dma_start(out=dst, in_=pst[s % 2][:, :]), sem, 16)
        block.scalar(_)
        read_bank(0, "c_dve", tag_ev(NS - 2))
        read_bank(1, "c_dve", tag_ev(NS - 1))
        state["wq_free_pe"] = tag_p(NS - 1)
        state["proj_last_pe"] = tag_p(NS - 1)

    def ph_park(ch):
        def _(sync):
            for half in range(2):
                wt(sync, "c_dve", state["stt_tags"][8 * half + 7])
                inc(sync.dma_start(
                    out=xp.rearrange("(n p) t -> p n t", p=128)[:, 8 * half:8 * half + 8,
                                                               ch * CH:(ch + 1) * CH],
                    in_=xch[:].rearrange("p (n c) -> p n c", c=CH)[:, 8 * half:8 * half + 8, :]),
                    "d_pk", 16)
        block.sync(_)
        return CNT["d_pk"]

    def ph_out(ch):
        P0 = CNT["c_pe"]
        V0 = CNT["c_dve"]
        O0 = CNT["d_out"]

        stt = state["stt_tags"]

        def _(tensor):
            wt(tensor, "d_c", state["consts"])
            bank_guard(tensor, 4)
            bank_guard(tensor, 5)
            gi = 0
            for tt in range(4):
                for grp in range(4):
                    wt(tensor, "c_dve", stt[grp * 4 + 3])
                    if gi >= 2:
                        wt(tensor, "c_dve", V0 + gi - 1)
                    bank = 4 + gi % 2
                    for j in range(4):
                        ht = grp * 4 + j
                        ins = tensor.transpose(
                            psum[bank][:, j * 128:(j + 1) * 128],
                            xch_t(ht)[:, tt * 128:(tt + 1) * 128],
                            idf[:, :])
                    inc(ins, "c_pe")
                    gi += 1
        block.tensor(_)

        def _(vector):
            gi = 0
            for tt in range(4):
                wt(vector, "d_out", O0 + 16 * tt)
                for grp in range(4):
                    wt(vector, "c_pe", P0 + gi + 1)
                    inc(vector.tensor_copy(out=ostg[:, grp * 512:(grp + 1) * 512],
                                           in_=psum[4 + gi % 2][:, :]), "c_dve")
                    gi += 1
        block.vector(_)

        def _(sync):
            for tt in range(4):
                wt(sync, "c_dve", V0 + 4 * tt + 4)
                inc(sync.dma_start(out=out[ch * CH + tt * 128: ch * CH + (tt + 1) * 128, :],
                                   in_=ostg[:, :]), "d_out", 16)
        block.sync(_)
        read_bank(4, "c_dve", V0 + 16)
        read_bank(5, "c_dve", V0 + 16)
        state["out_pe"] = CNT["c_pe"]
        state["out_last_dve"] = CNT["c_dve"]

    def ph_attention():
        def _(gpsimd):
            wt(gpsimd, "d_q0", CNT["d_q0"])
            wt(gpsimd, "d_q1", CNT["d_q1"])
            inc(gpsimd.collective_compute(
                "AllGather", ALU.bypass,
                replica_groups=[[0, 1], [2, 3], [4, 5], [6, 7]],
                ins=[kv_send[:, :]], outs=[kv_recv[:, :]]), "s_cc", 1)
        block.gpsimd(_)

        qh_base = [CNT["d_qh0"], CNT["d_qh1"]]

        def _(sync):
            wt(sync, "s_cc", CNT["s_cc"])
            inc(sync.dma_start(
                out=kT[:].rearrange("p (g t) -> p g t", t=SEQ)[:, :, 0:TPC],
                in_=kv_recv[0:512, :].rearrange("(g p) t -> p g t", p=128)), "d_kv", 16)
            inc(sync.dma_start(
                out=kT[:].rearrange("p (g t) -> p g t", t=SEQ)[:, :, TPC:SEQ],
                in_=kv_recv[1024:1536, :].rearrange("(g p) t -> p g t", p=128)), "d_kv", 16)
            inc(sync.dma_start(
                out=vtmp[:].rearrange("p (g t) -> p g t", t=SEQ)[:, :, 0:TPC],
                in_=kv_recv[512:1024, :].rearrange("(g p) t -> p g t", p=128)), "d_kv", 16)
            inc(sync.dma_start(
                out=vtmp[:].rearrange("p (g t) -> p g t", t=SEQ)[:, :, TPC:SEQ],
                in_=kv_recv[1536:2048, :].rearrange("(g p) t -> p g t", p=128)), "d_kv", 16)
        block.sync(_)
        kvload = CNT["d_kv"]

        P0 = CNT["c_pe"]
        V0 = CNT["c_dve"]

        def _(tensor):
            wt(tensor, "d_c", state["consts"])
            wt(tensor, "d_kv", kvload)
            for grp in range(16):
                if grp >= 1:
                    wt(tensor, "c_dve", V0 + 8 * grp)
                for j in range(8):
                    tile = grp * 8 + j
                    g, tkt = tile // 32, tile % 32
                    ins = tensor.transpose(
                        psb[:, j * 128:(j + 1) * 128],
                        vtmp[:, g * SEQ + tkt * 128: g * SEQ + (tkt + 1) * 128],
                        idb[:, :])
                inc(ins, "c_pe")
        block.tensor(_)
        state["vt_pe"] = CNT["c_pe"]

        def _(vector):
            for grp in range(16):
                wt(vector, "c_pe", P0 + grp + 1)
                for j in range(8):
                    tile = grp * 8 + j
                    g, tkt = tile // 32, tile % 32
                    inc(vector.tensor_copy(
                        out=v_t[:, g * SEQ + tkt * 128:g * SEQ + (tkt + 1) * 128],
                        in_=psb[:, j * 128:(j + 1) * 128]), "c_dve")
        block.vector(_)
        vt_done = CNT["c_dve"]
        read_bank(7, "c_dve", vt_done)

        prev_mm = 0
        for g in range(4):
            Vs = CNT["c_dve"]

            def _(vector, g=g, prev_mm=prev_mm):
                wt(vector, "c_dve", vt_done)
                wt(vector, "c_pe", prev_mm)
                inc(vector.tensor_mul(qsq[:, :], kT[:, g * SEQ:(g + 1) * SEQ],
                                      kT[:, g * SEQ:(g + 1) * SEQ]), "c_dve")
            block.vector(_)
            Pn = CNT["c_pe"]
            Vr0 = CNT["c_dve"]

            def _(tensor, Vs=Vs, Vr0=Vr0):
                wt(tensor, "c_dve", Vs + 1)
                bank_guard(tensor, 6)
                for q4 in range(8):
                    if q4 >= 1:
                        wt(tensor, "c_dve", Vr0 + q4)
                    inc(tensor.matmul(psum[6][0:1, :], oneb[:, 0:1],
                                      qsq[:, q4 * 512:(q4 + 1) * 512],
                                      start=True, stop=True), "c_pe")
            block.tensor(_)
            prev_mm = CNT["c_pe"]

            def _(vector, g=g, Pn=Pn):
                t = 0
                for q4 in range(8):
                    wt(vector, "c_pe", Pn + q4 + 1)
                    t = inc(vector.reduce_max(rcell(1024 + q4), psum[6][0:1, :],
                                              axis=mybir.AxisListType.X), "c_dve")
                wt(vector, "c_dve", t)
                inc(vector.reduce_max(rcell(1045 + g), rows[0:1, 1024:1032],
                                      axis=mybir.AxisListType.X), "c_dve")
            block.vector(_)
            read_bank(6, "c_dve", CNT["c_dve"])
        knorm_done = CNT["c_dve"]

        def _(vector):
            wt(vector, "c_dve", knorm_done)
            t = inc(vector.reduce_max(rcell(1049), rows[0:1, 1045:1049],
                                      axis=mybir.AxisListType.X), "c_dve")
            wt(vector, "c_dve", t)
            t = inc(vector.tensor_mul(rcell(1050), rcell(1049), qmx[0:1, 0:1]), "c_dve")
        block.vector(_)
        Vb = CNT["c_dve"]

        def _(scalar):
            wt(scalar, "c_dve", Vb)
            a = inc(scalar.activation(out=rcell(1051), in_=rcell(1050), func=AF.Ln),
                    "c_act")
            wt(scalar, "c_act", a)
            inc(scalar.activation(out=rcell(1052), in_=rcell(1051), func=AF.Exp,
                                  scale=0.5), "c_act")
        block.scalar(_)
        Ab = CNT["c_act"]

        def _(vector):
            wt(vector, "c_act", Ab)
            inc(vector.tensor_scalar(out=rcell(1053), in0=rcell(1052),
                                     scalar1=-1.001, scalar2=EXP_OFF,
                                     op0=ALU.mult, op1=ALU.add), "c_dve")
        block.vector(_)
        Vb2 = CNT["c_dve"]

        def _(tensor):
            wt(tensor, "c_dve", Vb2)
            bank_guard(tensor, 6)
            inc(tensor.matmul(psum[6][:, 0:1], onef[0:1, :], rcell(1053),
                              start=True, stop=True), "c_pe")
        block.tensor(_)
        Pb = CNT["c_pe"]

        def _(vector):
            wt(vector, "c_pe", Pb)
            inc(vector.tensor_copy(out=bexp[:, 0:1], in_=psum[6][:, 0:1]), "c_dve")
        block.vector(_)
        bexp_done = CNT["c_dve"]
        read_bank(6, "c_dve", bexp_done)

        vt_pe = state["vt_pe"]
        pk_all = CNT["d_pk"]
        for pi in range(8):
            g = pi % 4
            hd0 = g + 8 * (pi // 4)
            hd1 = hd0 + 4
            def _(sync, hd0=hd0, hd1=hd1):
                wt(sync, "c_pe", state.get("pair_scores_done", 0))
                inc(sync.dma_start(out=qh[0][:, :],
                                   in_=qpark[hd0 * 128:(hd0 + 1) * 128, :]), "d_qh0", 16)
                inc(sync.dma_start(out=qh[1][:, :],
                                   in_=qpark[hd1 * 128:(hd1 + 1) * 128, :]), "d_qh1", 16)
            block.sync(_)
            qt0 = CNT["d_qh0"]
            qt1 = CNT["d_qh1"]

            for blk in range(NCH):
                def _(sync, hd0=hd0, hd1=hd1, blk=blk):
                    wt(sync, "c_dve", state.get("x1_free_0", 0))
                    inc(sync.dma_start(
                        out=sg[0][:, :],
                        in_=xp[hd0 * 128:(hd0 + 1) * 128, blk * CH:(blk + 1) * CH]),
                        "d_x10", 16)
                    wt(sync, "c_dve", state.get("x1_free_1", 0))
                    inc(sync.dma_start(
                        out=sg[1][:, :],
                        in_=xp[hd1 * 128:(hd1 + 1) * 128, blk * CH:(blk + 1) * CH]),
                        "d_x11", 16)
                block.sync(_)
                x1t0 = CNT["d_x10"]
                x1t1 = CNT["d_x11"]

                P0s = CNT["c_pe"]
                A0s = CNT["c_act"]

                def scr_tag(tkt, h):
                    base = 2 * tkt + 2 * max(0, tkt - 2)
                    return P0s + base + h + 1

                def ctx_tag(t2):
                    return P0s + 2 * (t2 + 2) + 2 * t2 + 2 + 2

                def _(tensor, g=g, blk=blk, A0s=A0s, qt0=qt0, qt1=qt1,
                      bexp_done=bexp_done):
                    wt(tensor, "d_qh0", qt0)
                    wt(tensor, "d_qh1", qt1)
                    for b in (0, 1, 2, 3, 4, 5):
                        bank_guard(tensor, b)
                    for tkt in range(32):
                        if tkt >= 2:
                            wt(tensor, "c_act", A0s + tkt - 1)
                        lt = kT[:, g * SEQ + tkt * 128:g * SEQ + (tkt + 1) * 128]
                        inc(tensor.matmul(pbS[tkt % 2][:, 0:512], lt,
                                          qh[0][:, blk * CH:(blk + 1) * CH],
                                          start=True, stop=True), "c_pe")
                        inc(tensor.matmul(pbS[tkt % 2][:, 512:1024], lt,
                                          qh[1][:, blk * CH:(blk + 1) * CH],
                                          start=True, stop=True), "c_pe")
                        if tkt >= 2:
                            t2 = tkt - 2
                            wt(tensor, "c_act", A0s + t2 + 1)
                            vt = v_t[:, g * SEQ + t2 * 128:g * SEQ + (t2 + 1) * 128]
                            inc(tensor.matmul(psum[2][:, :], vt,
                                              pTP[:, t2 * 1024:t2 * 1024 + 512],
                                              start=(t2 == 0), stop=(t2 == 31)), "c_pe")
                            inc(tensor.matmul(psum[3][:, :], vt,
                                              pTP[:, t2 * 1024 + 512:(t2 + 1) * 1024],
                                              start=(t2 == 0), stop=(t2 == 31)), "c_pe")
                    for t2 in (30, 31):
                        wt(tensor, "c_act", A0s + t2 + 1)
                        vt = v_t[:, g * SEQ + t2 * 128:g * SEQ + (t2 + 1) * 128]
                        inc(tensor.matmul(psum[2][:, :], vt,
                                          pTP[:, t2 * 1024:t2 * 1024 + 512],
                                          start=(t2 == 0), stop=(t2 == 31)), "c_pe")
                        inc(tensor.matmul(psum[3][:, :], vt,
                                          pTP[:, t2 * 1024 + 512:(t2 + 1) * 1024],
                                          start=(t2 == 0), stop=(t2 == 31)), "c_pe")
                block.tensor(_)
                state["pair_scores_done"] = CNT["c_pe"]
                Pctx = CNT["c_pe"]

                def _(scalar, A0s=A0s, bexp_done=bexp_done, vt_pe=vt_pe):
                    wt(scalar, "c_dve", bexp_done)
                    wt(scalar, "c_pe", vt_pe)
                    for tkt in range(32):
                        wt(scalar, "c_pe", scr_tag(tkt, 1))
                        inc(scalar.activation(
                            out=pTP[:, tkt * 1024:(tkt + 1) * 1024],
                            in_=pbS[tkt % 2][:, 0:1024],
                            func=AF.Exp, bias=bexp[:, 0:1]), "c_act")
                block.scalar(_)
                assert CNT["c_act"] == A0s + 32
                for j, b in enumerate((0, 1, 4, 5)):
                    read_bank(b, "c_act", A0s + 31 + (j // 2))

                def _(tensor):
                    bank_guard(tensor, 6)
                    for tkt in range(32):
                        ins = tensor.matmul(psum[6][:, :], oneb[:, :],
                                            pTP[:, tkt * 1024:tkt * 1024 + 512],
                                            start=(tkt == 0), stop=(tkt == 31))
                    inc(ins, "c_pe")
                block.tensor(_)
                PdA = CNT["c_pe"]

                def _(vector, PdA=PdA, Pctx=Pctx, x1t0=x1t0):
                    wt(vector, "c_pe", PdA)
                    wt(vector, "d_x2a", CNT["d_x2a"])
                    wt(vector, "c_dve", state.get("epi_prev", 0))
                    t = inc(vector.reciprocal(uu[0][:, :], psum[6][:, :]), "c_dve")
                    wt(vector, "c_dve", t)
                    t = inc(vector.tensor_mul(uu[1][:, :], psum[2][:, :], uu[0][:, :]),
                            "c_dve")
                    wt(vector, "d_x10", x1t0)
                    wt(vector, "c_dve", t)
                    inc(vector.tensor_add(uu[0][:, :], uu[1][:, :], sg[0][:, :]), "c_dve")
                block.vector(_)
                recA = CNT["c_dve"] - 2
                state["x1_free_0"] = CNT["c_dve"]
                x2A = CNT["c_dve"]
                read_bank(2, "c_dve", x2A - 1)

                def _(sync, hd0=hd0, blk=blk, x2A=x2A):
                    wt(sync, "c_dve", x2A)
                    inc(sync.dma_start(
                        out=xp2[hd0 * 128:(hd0 + 1) * 128, blk * CH:(blk + 1) * CH],
                        in_=uu[0][:, :]), "d_x2a", 16)
                block.sync(_)

                def _(tensor, recA=recA):
                    wt(tensor, "c_dve", recA)
                    for tkt in range(32):
                        ins = tensor.matmul(psum[6][:, :], oneb[:, :],
                                            pTP[:, tkt * 1024 + 512:(tkt + 1) * 1024],
                                            start=(tkt == 0), stop=(tkt == 31))
                    inc(ins, "c_pe")
                block.tensor(_)
                PdB = CNT["c_pe"]
                bh = "B0" if blk % 2 == 0 else "B1"
                ohalf = (blk % 2) * 1024

                def _(vector, PdB=PdB, x1t1=x1t1, bh=bh, ohalf=ohalf):
                    wt(vector, "c_pe", PdB)
                    wt(vector, "d_pk", pk_all)
                    wt(vector, "d_x2b0" if bh == "B0" else "d_x2b1",
                       CNT["d_x2b0" if bh == "B0" else "d_x2b1"])
                    t = inc(vector.reciprocal(xch[:, ohalf:ohalf + 512], psum[6][:, :]),
                            "c_dve")
                    wt(vector, "c_dve", t)
                    t = inc(vector.tensor_mul(xch[:, ohalf + 512:ohalf + 1024],
                                              psum[3][:, :], xch[:, ohalf:ohalf + 512]),
                            "c_dve")
                    wt(vector, "d_x11", x1t1)
                    wt(vector, "c_dve", t)
                    inc(vector.tensor_add(xch[:, ohalf:ohalf + 512],
                                          xch[:, ohalf + 512:ohalf + 1024],
                                          sg[1][:, :]), "c_dve")
                block.vector(_)
                state["x1_free_1"] = CNT["c_dve"]
                state["epi_prev"] = CNT["c_dve"]
                x2B = CNT["c_dve"]
                read_bank(3, "c_dve", x2B - 1)
                read_bank(6, "c_dve", x2B - 2)

                def _(sync, hd1=hd1, blk=blk, x2B=x2B, ohalf=ohalf):
                    wt(sync, "c_dve", x2B)
                    inc(sync.dma_start(
                        out=xp2[hd1 * 128:(hd1 + 1) * 128, blk * CH:(blk + 1) * CH],
                        in_=xch[:, ohalf:ohalf + 512]),
                        "d_x2b0" if bh == "B0" else "d_x2b1", 16)
                block.sync(_)

        state["att_done"] = {s: CNT[s] for s in ("d_x2a", "d_x2b0", "d_x2b1")}
        state["att_done_pe"] = CNT["c_pe"]

    ph_consts()
    if stage == 11:
        x_ready = ph_load_x_transposed(0, 0, 0)
        ph_ln(x_ready)
        ph_ffn(0)
        def _(sync):
            wt(sync, "c_dve", state["ffn_done_dve"])
            inc(sync.dma_start(out=dbg_xn, in_=xn[:, :]), "d_out", 16)
            inc(sync.dma_start(out=dbg_hb, in_=hb[:, :]), "d_out", 16)
            inc(sync.dma_start(out=dbg_x, in_=xch[:, :]), "d_out", 16)
            inc(sync.dma_start(out=out[0:128, :], in_=xstg[:, 0:D]), "d_out", 16)
            inc(sync.dma_start(out=out[200:201, 0:2048], in_=ostg[0:1, :]), "d_out", 16)
            inc(sync.dma_start(out=out[210:211, 0:512], in_=sg[1][0:1, :]), "d_out", 16)
            wt(sync, "d_out", CNT["d_out"])
        block.sync(_)
        blockctx.__exit__(None, None, None)
        es.close()
        return nc
    park_tag = 0
    xch_free_pe = 0
    for ch in range(NCH):
        x_ready = ph_load_x_transposed(ch, xch_free_pe, park_tag)
        ph_ln(x_ready)
        ph_ffn(state["gu_prev_free"])
        state["gu_prev_free"] = state["gu_free_pe"]
        if eff >= 2:
            ph_ln(lambda ht: ("c_dve", state["stt_tags"][ht]))
            ph_proj(ch)
            xch_free_pe = state["proj_last_pe"]
        else:
            xch_free_pe = state["dw_free_pe"]
        park_tag = ph_park(ch)
        if eff == 1:
            ph_out(ch)
            xch_free_pe = state["out_pe"]

    if eff >= 2:
        ph_attention()
        state["gu_prev_free"] = state["att_done_pe"]
        for ch in range(NCH):
            x_ready = ph_load_x_hmajor(ch, xp2, xch_free_pe)
            if eff >= 3:
                ph_ln(x_ready)
                ph_ffn(state["gu_prev_free"])
                state["gu_prev_free"] = state["gu_free_pe"]
            else:
                def _(vector, xr=x_ready):
                    s, v = xr(0)
                    wt(vector, s, v)
                    inc(vector.tensor_copy(out=sg[0][:, 0:8], in_=xch[:, 0:8]), "c_dve")
                block.vector(_)
                state["ffn_done_dve"] = CNT["c_dve"]
            ph_out(ch)
            xch_free_pe = state["out_pe"]

    def _(sync):
        if stage == 12:
            for s, v in state["att_done"].items():
                wt(sync, s, v)
            inc(sync.dma_start(out=dbg_q, in_=qpark[:, :]), "d_out", 16)
            inc(sync.dma_start(out=dbg_kv, in_=kv_recv[:, :]), "d_out", 16)
            inc(sync.dma_start(out=dbg_xp, in_=xp[:, :]), "d_out", 16)
        wt(sync, "d_out", CNT["d_out"])
        wt(sync, "d_pk", CNT["d_pk"])
    block.sync(_)

    if stage == 12:
        def _(vector):
            inc(vector.tensor_copy(out=sg[0][:, 0:16], in_=bexp[:, :]), "c_dve")
        block.vector(_)

        def _(sync):
            wt(sync, "c_dve", CNT["c_dve"])
            inc(sync.dma_start(out=dbg_bexp, in_=sg[0][:, 0:16]), "d_out", 16)
            wt(sync, "d_out", CNT["d_out"])
        block.sync(_)

    blockctx.__exit__(None, None, None)
    es.close()
    return nc



def _byte0(w):
    a = np.ascontiguousarray(w)
    assert a.dtype == np.int32
    return a.view(np.uint8).reshape(a.shape[0], a.shape[1], 4)[:, :, 0]


def kernel(L0_attn_hidden_states, L0_attn_Wq, L0_attn_bq, L0_attn_Wk, L0_attn_bk,
           L0_attn_Wv, L0_attn_bv, L0_ffn_gate_w, L0_ffn_gate_scale, L0_ffn_up_w,
           L0_ffn_up_scale, L0_ffn_down_w, L0_ffn_down_scale, _stage=3):
    bf = ml_dtypes.bfloat16
    x = np.ascontiguousarray(np.asarray(L0_attn_hidden_states, dtype=np.float32)).reshape(16384, D)
    wq = np.ascontiguousarray(np.asarray(L0_attn_Wq, dtype=np.float32))
    wk = np.ascontiguousarray(np.asarray(L0_attn_Wk, dtype=np.float32))
    wv = np.ascontiguousarray(np.asarray(L0_attn_Wv, dtype=np.float32))
    bq = np.ascontiguousarray(np.asarray(L0_attn_bq, dtype=np.float32).reshape(16, 128).T)
    bk = np.ascontiguousarray(np.asarray(L0_attn_bk, dtype=np.float32).reshape(4, 128).T)
    bv = np.ascontiguousarray(np.asarray(L0_attn_bv, dtype=np.float32).reshape(4, 128).T)
    gT = np.ascontiguousarray(_byte0(np.asarray(L0_ffn_gate_w)).T)
    uT = np.ascontiguousarray(_byte0(np.asarray(L0_ffn_up_w)).T)
    dT = np.ascontiguousarray(_byte0(np.asarray(L0_ffn_down_w)).T)
    g4 = np.ascontiguousarray(gT.reshape(16, 128, 32, 256).transpose(2, 1, 0, 3))
    u4 = np.ascontiguousarray(uT.reshape(16, 128, 32, 256).transpose(2, 1, 0, 3))
    d4 = np.ascontiguousarray(dT.reshape(64, 128, 16, 128).transpose(2, 1, 0, 3))
    gsc = np.ascontiguousarray(np.asarray(L0_ffn_gate_scale, np.float32).reshape(64, 128).T)
    usc = np.ascontiguousarray(np.asarray(L0_ffn_up_scale, np.float32).reshape(64, 128).T)
    dsc = np.ascontiguousarray(np.asarray(L0_ffn_down_scale, np.float32).reshape(16, 128).T)
    idf = np.eye(128, dtype=np.float32)
    idb = np.eye(128, dtype=np.float32).astype(bf)
    onef = np.ones((128, 128), np.float32)
    oneb = np.ones((128, 128), np.float32).astype(bf)
    mrow = np.full((1, 128), -128.0, np.float32).astype(bf)

    nc = build(stage=_stage)
    shared = {"wq": wq, "wk": wk, "wv": wv, "bq_in": bq, "bk_in": bk, "bv_in": bv,
              "g4": g4, "u4": u4, "d4": d4, "gsc_in": gsc, "usc_in": usc,
              "dsc_in": dsc, "idf_in": idf, "idb_in": idb, "onef_in": onef,
              "oneb_in": oneb, "mrow_in": mrow}
    in_maps = [dict(shared, x_in=np.ascontiguousarray(x[c * TPC:(c + 1) * TPC]))
               for c in range(8)]
    res = run_bass_kernel_spmd(nc, in_maps, core_ids=list(range(8)))
    full = np.concatenate([np.asarray(r["out"], np.float32) for r in res.results], axis=0)
    return full.reshape(4, 4096, D)


# version 11
# speedup vs baseline: 1.1971x; 1.0860x over previous
import contextlib
import numpy as np
import ml_dtypes
import concourse.bass as bass
import concourse.mybir as mybir
from concourse.bass_utils import run_bass_kernel_spmd

F32 = mybir.dt.float32
BF16 = mybir.dt.bfloat16
U8 = mybir.dt.uint8
AF = mybir.ActivationFunctionType
ALU = mybir.AluOpType

D = 2048
I = 8192
TPC = 2048
CH = 512
NCH = TPC // CH
NHT = D // 128
NIT = I // 128
HQ, HKV, HD = 16, 4, 128
SEQ = 4096
SCALE = 1.0 / float(np.sqrt(128.0))
EXP_OFF = 60.0

OFF_XCH = 0x04000
OFF_XN = 0x0C000
OFF_HB = 0x10000
OFF_KT = 0x10000
OFF_VT = 0x18000
OFF_XSTG = 0x10000
OFF_GW0 = 0x20000
OFF_GW1 = 0x22000
OFF_UW0 = 0x24000
OFF_UW1 = 0x26000
OFF_DW0 = 0x28000
OFF_DW1 = 0x2C000
OFF_PT = 0x20000
OFF_VTMP = 0x28000
OFF_QSQ = 0x28000
OFF_WQ0 = 0x30000
OFF_WQ1 = 0x31000
OFF_OSTG = 0x30000
OFF_IDF = 0x32000
OFF_IDB = 0x32200
OFF_ONEF = 0x32300
OFF_ONEB = 0x32500
OFF_MROW = 0x32600
OFF_BQ = 0x32700
OFF_BQS = 0x32740
OFF_BK = 0x32780
OFF_BV = 0x327A0
OFF_GSC = 0x327C0
OFF_USC = 0x328C0
OFF_DSC = 0x329C0
OFF_DSCH = 0x32A00
OFF_BEXP = 0x32A40
OFF_ROWSB = 0x32A80
OFF_ROWS = 0x33000
OFF_SG0 = 0x34800
OFF_SG1 = 0x35000
OFF_UU0 = 0x35800
OFF_UU1 = 0x36000
OFF_PST0 = 0x36800
OFF_PST1 = 0x36C00
OFF_SQ0 = 0x37000
OFF_SQ1 = 0x37400

SEM_NAMES = [
    "d_x", "d_c", "d_pk", "d_out", "d_kv",
    "d_q0", "d_q1", "d_gu0", "d_gu1", "d_dw0", "d_dw1", "d_wq0", "d_wq1",
    "d_qh0", "d_qh1", "d_x10", "d_x11", "d_x2a", "d_x2b0", "d_x2b1",
    "s_cc", "c_pe", "c_act", "c_dve",
]


def build(stage=3):
    eff = 2 if stage == 12 else stage
    nc = bass.Bass("TRN2", target_bir_lowering=False)

    x_in = nc.dram_tensor("x_in", [TPC, D], F32, kind="ExternalInput").ap()
    wq = nc.dram_tensor("wq", [D, D], F32, kind="ExternalInput").ap()
    wk = nc.dram_tensor("wk", [D, 512], F32, kind="ExternalInput").ap()
    wv = nc.dram_tensor("wv", [D, 512], F32, kind="ExternalInput").ap()
    bq_in = nc.dram_tensor("bq_in", [128, 16], F32, kind="ExternalInput").ap()
    bk_in = nc.dram_tensor("bk_in", [128, 4], F32, kind="ExternalInput").ap()
    bv_in = nc.dram_tensor("bv_in", [128, 4], F32, kind="ExternalInput").ap()
    g4 = nc.dram_tensor("g4", [32, 128, 16, 256], U8, kind="ExternalInput").ap()
    u4 = nc.dram_tensor("u4", [32, 128, 16, 256], U8, kind="ExternalInput").ap()
    d4 = nc.dram_tensor("d4", [16, 128, 64, 128], U8, kind="ExternalInput").ap()
    gsc_in = nc.dram_tensor("gsc_in", [128, 64], F32, kind="ExternalInput").ap()
    usc_in = nc.dram_tensor("usc_in", [128, 64], F32, kind="ExternalInput").ap()
    dsc_in = nc.dram_tensor("dsc_in", [128, 16], F32, kind="ExternalInput").ap()
    idf_in = nc.dram_tensor("idf_in", [128, 128], F32, kind="ExternalInput").ap()
    idb_in = nc.dram_tensor("idb_in", [128, 128], BF16, kind="ExternalInput").ap()
    onef_in = nc.dram_tensor("onef_in", [128, 128], F32, kind="ExternalInput").ap()
    oneb_in = nc.dram_tensor("oneb_in", [128, 128], BF16, kind="ExternalInput").ap()
    mrow_in = nc.dram_tensor("mrow_in", [1, 128], BF16, kind="ExternalInput").ap()
    out = nc.dram_tensor("out", [TPC, D], F32, kind="ExternalOutput").ap()
    if stage == 11:
        dbg_xn = nc.dram_tensor("dbg_xn", [128, NHT * CH], BF16, kind="ExternalOutput").ap()
        dbg_hb = nc.dram_tensor("dbg_hb", [128, NIT * CH], BF16, kind="ExternalOutput").ap()
        dbg_x = nc.dram_tensor("dbg_x", [128, NHT * CH], F32, kind="ExternalOutput").ap()
    if stage == 12:
        dbg_q = nc.dram_tensor("dbg_q", [D, TPC], BF16, kind="ExternalOutput").ap()
        dbg_kv = nc.dram_tensor("dbg_kv", [2048, TPC], BF16, kind="ExternalOutput").ap()
        dbg_bexp = nc.dram_tensor("dbg_bexp", [128, 16], F32, kind="ExternalOutput").ap()
        dbg_xp = nc.dram_tensor("dbg_xp", [D, TPC], F32, kind="ExternalOutput").ap()
    xp = nc.dram_tensor("xp", [D, TPC], F32, kind="Internal").ap()
    xp2 = nc.dram_tensor("xp2", [D, TPC], F32, kind="Internal").ap()
    qpark = nc.dram_tensor("qpark", [D, TPC], BF16, kind="Internal").ap()
    kv_send = nc.dram_tensor("kv_send", [1024, TPC], BF16, kind="Internal").ap()
    kv_recv = nc.dram_tensor("kv_recv", [2048, TPC], BF16, kind="Internal").ap()

    SH = 0x80
    _MAP_END = 0x37800 + SH
    nc.alloc_sbuf_tensor("mapspacer", [128, _MAP_END - nc.sbuf_base], U8)
    assert nc.sbuf_base >= _MAP_END, hex(nc.sbuf_base)

    def A(name, shape, dtype, offset):
        return nc.alloc_sbuf_tensor_at(name, shape, dtype, offset=offset + SH)

    xch = A("xch", [128, NHT * CH], F32, offset=OFF_XCH)
    xn = A("xn", [128, NHT * CH], BF16, offset=OFF_XN)
    hb = A("hb", [128, NIT * CH], BF16, offset=OFF_HB)
    kT = A("kTs", [128, 4 * SEQ], BF16, offset=OFF_KT)
    v_t = A("v_ts", [128, 4 * SEQ], BF16, offset=OFF_VT)
    xstg = A("xstg", [128, 4 * D], F32, offset=OFF_XSTG)
    gw = [A("gw0", [128, 4096], BF16, offset=OFF_GW0),
          A("gw1", [128, 4096], BF16, offset=OFF_GW1)]
    uw = [A("uw0", [128, 4096], BF16, offset=OFF_UW0),
          A("uw1", [128, 4096], BF16, offset=OFF_UW1)]
    dw = [A("dw0", [128, 8192], BF16, offset=OFF_DW0),
          A("dw1", [128, 8192], BF16, offset=OFF_DW1)]
    pT = A("pTs", [128, 32 * CH], BF16, offset=OFF_PT)
    pTP = A("pTP", [128, 64 * CH], BF16, offset=OFF_PT)
    vtmp = A("vtmp", [128, 4 * SEQ], BF16, offset=OFF_VTMP)
    qsq = A("qsq", [128, SEQ], BF16, offset=OFF_QSQ)
    wqb = [A("wqb0", [128, 2048], BF16, offset=OFF_WQ0),
           A("wqb1", [128, 2048], BF16, offset=OFF_WQ1)]
    qh = wqb
    ostg = A("ostg", [128, D], F32, offset=OFF_OSTG)
    idf = A("idf", [128, 128], F32, offset=OFF_IDF)
    idb = A("idb", [128, 128], BF16, offset=OFF_IDB)
    onef = A("onef", [128, 128], F32, offset=OFF_ONEF)
    oneb = A("oneb", [128, 128], BF16, offset=OFF_ONEB)
    mrow = A("mrow", [1, 128], BF16, offset=OFF_MROW)
    bq_s = A("bq_s", [128, 16], F32, offset=OFF_BQ)
    bqs = A("bqs", [128, 16], F32, offset=OFF_BQS)
    bk_s = A("bk_s", [128, 4], F32, offset=OFF_BK)
    bv_s = A("bv_s", [128, 4], F32, offset=OFF_BV)
    gsc = A("gsc", [128, 64], F32, offset=OFF_GSC)
    usc = A("usc", [128, 64], F32, offset=OFF_USC)
    dsc = A("dsc", [128, 16], F32, offset=OFF_DSC)
    dsch = A("dsch", [128, 16], F32, offset=OFF_DSCH)
    bexp = A("bexp", [128, 16], F32, offset=OFF_BEXP)
    rowsb = A("rowsb", [128, 512], BF16, offset=OFF_ROWSB)
    qmx = A("qmx", [128, 16], F32, offset=OFF_ROWSB + 0x400)
    rows = A("rows", [128, 1536], F32, offset=OFF_ROWS)
    lob = A("lob", [1, 512], BF16, offset=OFF_ROWS + 0x1000)
    sg = [A("sg0", [128, CH], F32, offset=OFF_SG0),
          A("sg1", [128, CH], F32, offset=OFF_SG1)]
    uu = [A("uu0", [128, CH], F32, offset=OFF_UU0),
          A("uu1", [128, CH], F32, offset=OFF_UU1)]
    pst = [A("pst0", [128, CH], BF16, offset=OFF_PST0),
           A("pst1", [128, CH], BF16, offset=OFF_PST1)]
    sqst = [A("sq0", [128, CH], BF16, offset=OFF_SQ0),
            A("sq1", [128, CH], BF16, offset=OFF_SQ1)]

    es = contextlib.ExitStack()
    pb01 = es.enter_context(nc.psum_tensor("pb01", [128, 1024], F32))
    pb23 = es.enter_context(nc.psum_tensor("pb23", [128, 1024], F32))
    pb45 = es.enter_context(nc.psum_tensor("pb45", [128, 1024], F32))
    pb6 = es.enter_context(nc.psum_tensor("pb6", [128, 512], F32))
    psb = es.enter_context(nc.psum_tensor("pb7", [128, 1024], BF16))
    psum = [pb01[:, 0:512], pb01[:, 512:1024], pb23[:, 0:512], pb23[:, 512:1024],
            pb45[:, 0:512], pb45[:, 512:1024], pb6[:, :]]
    pbS = [pb01, pb45]
    SEM = {n: es.enter_context(nc.semaphore(n)) for n in SEM_NAMES}
    CNT = {n: 0 for n in SEM_NAMES}
    LASTW = {}
    BANKR = [{"c_act": 0, "c_dve": 0} for _ in range(8)]

    def inc(ins, sem, k=1):
        ins.then_inc(SEM[sem], k)
        CNT[sem] += k
        return CNT[sem]

    def wt(eng, sem, val):
        if val is None or val <= 0:
            return
        key = (id(eng), sem)
        if LASTW.get(key, -1) < val:
            eng.wait_ge(SEM[sem], val)
            LASTW[key] = val

    def bank_guard(tensor, b):
        wt(tensor, "c_act", BANKR[b]["c_act"])
        wt(tensor, "c_dve", BANKR[b]["c_dve"])

    def read_bank(b, sem, tag):
        BANKR[b][sem] = max(BANKR[b][sem], tag)

    def xch_t(ht):
        return xch[:, ht * CH:(ht + 1) * CH]

    def xn_t(ht):
        return xn[:, ht * CH:(ht + 1) * CH]

    def hb_t(it):
        return hb[:, it * CH:(it + 1) * CH]

    def row(i):
        return rows[0:1, i * 512:(i + 1) * 512]

    def rcell(c0, n=1):
        return rows[0:1, c0:c0 + n]

    blockctx = nc.Block()
    block = blockctx.__enter__()
    state = {"xn_ready": 0, "ffn_done_dve": 0, "wq_free_pe": 0,
             "gu_prev_free": 0, "x2_store_tag": 0, "scores_done_pe": 0}

    def ph_consts():
        def _(sync):
            for dst, src in [(idf, idf_in), (idb, idb_in), (onef, onef_in),
                             (oneb, oneb_in), (bq_s, bq_in), (bk_s, bk_in),
                             (bv_s, bv_in), (gsc, gsc_in), (usc, usc_in),
                             (dsc, dsc_in)]:
                inc(sync.dma_start(out=dst[:, :], in_=src), "d_c", 16)
            inc(sync.dma_start(out=mrow[0:1, :], in_=mrow_in), "d_c", 16)
        block.sync(_)
        state["consts"] = CNT["d_c"]

        def _(vector):
            wt(vector, "d_c", state["consts"])
            inc(vector.tensor_scalar_mul(bqs[:, :], bq_s[:, :], SCALE), "c_dve")
            inc(vector.tensor_scalar_mul(dsch[:, :], dsc[:, :], 0.5), "c_dve")
            inc(vector.memset(qmx[:, :], 0.0), "c_dve")
        block.vector(_)
        state["consts_dve"] = CNT["c_dve"]

    def ph_ln(x_ready, pre=None):
        P0 = CNT["c_pe"]
        if pre is None:
            A0 = CNT["c_act"]

            def _(scalar):
                for ht in range(NHT):
                    s, v = x_ready(ht)
                    wt(scalar, s, v)
                    if ht >= 2:
                        wt(scalar, "c_pe", P0 + 2 * (ht - 2) + 2)
                    inc(scalar.activation(out=xn_t(ht), in_=xch_t(ht), func=AF.Copy),
                        "c_act")
                    inc(scalar.activation(out=sqst[ht % 2][:, :], in_=xch_t(ht),
                                          func=AF.Square), "c_act")
            block.scalar(_)

            def _(tensor):
                wt(tensor, "d_c", state["consts"])
                bank_guard(tensor, 2)
                bank_guard(tensor, 3)
                for ht in range(NHT):
                    wt(tensor, "c_act", A0 + 2 * ht + 1)
                    inc(tensor.matmul(psum[2][0:1, :], oneb[:, 0:1], xn_t(ht),
                                      start=(ht == 0), stop=(ht == NHT - 1)), "c_pe")
                    wt(tensor, "c_act", A0 + 2 * ht + 2)
                    inc(tensor.matmul(psum[3][0:1, :], oneb[:, 0:1], sqst[ht % 2][:, :],
                                      start=(ht == 0), stop=(ht == NHT - 1)), "c_pe")
            block.tensor(_)
            pe_stats = CNT["c_pe"]
            assert pe_stats == P0 + 2 * NHT
        else:
            pe_stats = pre

        def _(vector):
            wt(vector, "c_pe", pe_stats)
            if stage == 11:
                inc(vector.tensor_copy(out=ostg[0:1, 0:512], in_=psum[2][0:1, :]), "c_dve")
                inc(vector.tensor_copy(out=ostg[0:1, 512:1024], in_=psum[3][0:1, :]), "c_dve")
            t1 = inc(vector.tensor_scalar_mul(row(0), psum[2][0:1, :], 1.0 / D), "c_dve")
            t2 = inc(vector.tensor_scalar_mul(row(1), psum[3][0:1, :], 1.0 / D), "c_dve")
            wt(vector, "c_dve", t1)
            t3 = inc(vector.tensor_mul(row(2), row(0), row(0)), "c_dve")
            wt(vector, "c_dve", t3)
            inc(vector.scalar_tensor_tensor(out=row(1), in0=row(1), scalar=1e-5,
                                            in1=row(2), op0=ALU.add, op1=ALU.subtract),
                "c_dve")
        block.vector(_)
        dve_var = CNT["c_dve"]
        read_bank(2, "c_dve", dve_var)
        read_bank(3, "c_dve", dve_var)
        if stage == 11:
            def _(vector):
                wt(vector, "c_dve", dve_var)
                inc(vector.tensor_copy(out=ostg[0:1, 1024:1536], in_=row(1)), "c_dve")
            block.vector(_)
            dve_var = CNT["c_dve"]

        def _(scalar):
            wt(scalar, "c_dve", dve_var)
            a1 = inc(scalar.activation(out=row(2), in_=row(1), func=AF.Ln), "c_act")
            wt(scalar, "c_act", a1)
            inc(scalar.activation(out=row(1), in_=row(2), func=AF.Exp, scale=-0.5), "c_act")
        block.scalar(_)
        if stage == 11:
            def _(vector):
                wt(vector, "c_act", CNT["c_act"])
                inc(vector.tensor_copy(out=sg[1][0:1, :], in_=row(2)), "c_dve")
            block.vector(_)
        act_rstd = CNT["c_act"]
        if stage == 11:
            def _(vector):
                wt(vector, "c_act", act_rstd)
                inc(vector.tensor_copy(out=ostg[0:1, 1536:2048], in_=row(1)), "c_dve")
            block.vector(_)

        def _(vector):
            wt(vector, "c_act", act_rstd)
            inc(vector.scalar_tensor_tensor(out=row(2), in0=row(0), scalar=-1.0,
                                            in1=row(1), op0=ALU.mult, op1=ALU.mult), "c_dve")
        block.vector(_)
        dve_nmr = CNT["c_dve"]

        def _(tensor):
            wt(tensor, "c_dve", dve_nmr)
            bank_guard(tensor, 2)
            bank_guard(tensor, 3)
            inc(tensor.matmul(psum[2][:, :], onef[0:1, :], row(1), start=True, stop=True), "c_pe")
            inc(tensor.matmul(psum[3][:, :], onef[0:1, :], row(2), start=True, stop=True), "c_pe")
        block.tensor(_)
        pe_bc = CNT["c_pe"]

        def _(vector):
            wt(vector, "c_pe", pe_bc)
            prev_add = 0
            tags = []
            for ht in range(NHT):
                wt(vector, "c_dve", prev_add)
                t = inc(vector.tensor_mul(sg[0][:, :], xch_t(ht), psum[2][:, :]), "c_dve")
                wt(vector, "c_dve", t)
                prev_add = inc(vector.tensor_add(xn_t(ht), sg[0][:, :], psum[3][:, :]), "c_dve")
                tags.append(prev_add)
            state["xn_tags"] = tags
        block.vector(_)
        state["xn_ready"] = CNT["c_dve"]
        read_bank(2, "c_dve", state["xn_ready"])
        read_bank(3, "c_dve", state["xn_ready"])

    def ph_ffn(warena_free_pe):
        NB = 32
        P0 = CNT["c_pe"]
        A0 = CNT["c_act"]
        V0 = CNT["c_dve"]
        gu_base = [CNT["d_gu0"], CNT["d_gu1"]]

        def _pe_before(it):
            return 2 * it + max(0, it - 2)

        def tag_g(it):
            return P0 + _pe_before(it) + 1

        def tag_u(it):
            return P0 + _pe_before(it) + 2

        def tag_silu(it):
            return A0 + it + 1

        def tag_uu(it):
            return V0 + 2 * it + 1

        def tag_h(it):
            return V0 + 2 * it + 2

        def _(gpsimd):
            for ib in range(NB):
                if ib >= 2:
                    wt(gpsimd, "c_pe", tag_u(2 * ib - 3))
                else:
                    wt(gpsimd, "c_pe", warena_free_pe)
                sem = "d_gu0" if ib % 2 == 0 else "d_gu1"
                inc(gpsimd.dma_start(
                    out=gw[ib % 2][:].rearrange("p (n c) -> p n c", c=256), in_=g4[ib]), sem, 16)
                inc(gpsimd.dma_start(
                    out=uw[ib % 2][:].rearrange("p (n c) -> p n c", c=256), in_=u4[ib]), sem, 16)
        block.gpsimd(_)

        xtags = state["xn_tags"]

        def _(tensor):
            bank_guard(tensor, 6)
            for it in range(2 * NB):
                ib = it // 2
                sem = "d_gu0" if ib % 2 == 0 else "d_gu1"
                wt(tensor, sem, gu_base[ib % 2] + 32 * (ib // 2 + 1))
                if it >= 2:
                    wt(tensor, "c_act", tag_silu(it - 2))
                    wt(tensor, "c_dve", tag_uu(it - 2))
                else:
                    bank_guard(tensor, it % 2)
                    bank_guard(tensor, 2 + it % 2)
                sl = (it % 2) * 128
                for ht in range(NHT):
                    wt(tensor, "c_dve", xtags[ht])
                    ins = tensor.matmul(
                        psum[it % 2][:, :],
                        gw[ib % 2][:, ht * 256 + sl:ht * 256 + sl + 128],
                        xn_t(ht), start=(ht == 0), stop=(ht == NHT - 1))
                inc(ins, "c_pe")
                for ht in range(NHT):
                    ins = tensor.matmul(
                        psum[2 + it % 2][:, :],
                        uw[ib % 2][:, ht * 256 + sl:ht * 256 + sl + 128],
                        xn_t(ht), start=(ht == 0), stop=(ht == NHT - 1))
                inc(ins, "c_pe")
                if it >= 2:
                    wt(tensor, "c_dve", tag_h(it - 2))
                    inc(tensor.matmul(psum[6][0:1, :], oneb[:, 0:1], hb_t(it - 2),
                                      start=(it == 2), stop=False), "c_pe")
        block.tensor(_)

        def _(tensor):
            for it in range(2 * NB - 2, 2 * NB):
                wt(tensor, "c_dve", tag_h(it))
                inc(tensor.matmul(psum[6][0:1, :], oneb[:, 0:1], hb_t(it),
                                  start=False, stop=(it == 2 * NB - 1)), "c_pe")
        block.tensor(_)
        assert CNT["c_pe"] == P0 + 4 * NB + (2 * NB - 2) + 2, (CNT["c_pe"], P0)

        def _(scalar):
            wt(scalar, "d_c", state["consts"])
            for it in range(2 * NB):
                wt(scalar, "c_pe", tag_g(it))
                if it >= 2:
                    wt(scalar, "c_dve", tag_h(it - 2))
                inc(scalar.activation(out=sg[it % 2][:, :], in_=psum[it % 2][:, :],
                                      func=AF.Silu, scale=gsc[:, it:it + 1]), "c_act")
        block.scalar(_)
        assert CNT["c_act"] == A0 + 2 * NB

        def _(vector):
            wt(vector, "d_c", state["consts"])
            for it in range(2 * NB):
                wt(vector, "c_pe", tag_u(it))
                if it >= 2:
                    wt(vector, "c_dve", tag_h(it - 2))
                t = inc(vector.tensor_scalar_mul(uu[it % 2][:, :], psum[2 + it % 2][:, :],
                                                 usc[:, it:it + 1]), "c_dve")
                wt(vector, "c_act", tag_silu(it))
                wt(vector, "c_dve", t)
                inc(vector.tensor_mul(hb_t(it), sg[it % 2][:, :], uu[it % 2][:, :]), "c_dve")
        block.vector(_)
        assert CNT["c_dve"] == V0 + 4 * NB
        for b in range(4):
            read_bank(b, "c_act", A0 + 2 * NB)
            read_bank(b, "c_dve", V0 + 4 * NB)

        pe_csh = CNT["c_pe"]

        def _(vector):
            wt(vector, "c_pe", pe_csh)
            t1 = inc(vector.tensor_copy(out=row(0), in_=psum[6][0:1, :]), "c_dve")
            wt(vector, "c_dve", t1)
            t2 = inc(vector.tensor_copy(out=rowsb[0:1, :], in_=row(0)), "c_dve")
            wt(vector, "c_dve", t2)
            inc(vector.tensor_sub(lob[0:1, :], row(0), rowsb[0:1, :]), "c_dve")
        block.vector(_)
        csh_ready = CNT["c_dve"]
        read_bank(6, "c_dve", csh_ready)

        Pd0 = CNT["c_pe"]
        Vd0 = CNT["c_dve"]
        dw_base = [CNT["d_dw0"], CNT["d_dw1"]]

        def tag_d(hht):
            return Pd0 + hht + 1

        def tag_stt(hht):
            return Vd0 + hht + 1

        def _(gpsimd):
            for hht in range(NHT):
                if hht >= 2:
                    wt(gpsimd, "c_pe", tag_d(hht - 2))
                sem = "d_dw0" if hht % 2 == 0 else "d_dw1"
                inc(gpsimd.dma_start(
                    out=dw[hht % 2][:].rearrange("p (n c) -> p n c", c=128), in_=d4[hht]), sem, 16)
        block.gpsimd(_)

        def _(tensor):
            wt(tensor, "c_dve", csh_ready)
            bank_guard(tensor, 4)
            bank_guard(tensor, 5)
            for hht in range(NHT):
                sem = "d_dw0" if hht % 2 == 0 else "d_dw1"
                wt(tensor, sem, dw_base[hht % 2] + 16 * (hht // 2 + 1))
                if hht >= 2:
                    wt(tensor, "c_dve", tag_stt(hht - 2))
                tensor.matmul(psum[4 + hht % 2][:, :], mrow[0:1, :], rowsb[0:1, :],
                              start=True, stop=False)
                tensor.matmul(psum[4 + hht % 2][:, :], mrow[0:1, :], lob[0:1, :],
                              start=False, stop=False)
                for it in range(NIT):
                    ins = tensor.matmul(
                        psum[4 + hht % 2][:, :],
                        dw[hht % 2][:, it * 128:(it + 1) * 128],
                        hb_t(it), start=False, stop=(it == NIT - 1))
                inc(ins, "c_pe")
        block.tensor(_)
        assert CNT["c_pe"] == Pd0 + NHT

        def _(vector):
            for hht in range(NHT):
                wt(vector, "c_pe", tag_d(hht))
                inc(vector.scalar_tensor_tensor(
                    out=xch_t(hht), in0=psum[4 + hht % 2][:, :], scalar=dsch[:, hht:hht + 1],
                    in1=xch_t(hht), op0=ALU.mult, op1=ALU.add), "c_dve")
        block.vector(_)
        assert CNT["c_dve"] == Vd0 + NHT
        state["stt_tags"] = [tag_stt(h) for h in range(NHT)]
        state["ffn_done_dve"] = CNT["c_dve"]
        read_bank(4, "c_dve", Vd0 + NHT)
        read_bank(5, "c_dve", Vd0 + NHT)
        state["gu_free_pe"] = tag_u(2 * NB - 1)
        state["dw_free_pe"] = CNT["c_pe"]

    def ph_load_x_transposed(ch, xch_free_pe, park_tag):
        def _(sync):
            wt(sync, "c_pe", xch_free_pe)
            wt(sync, "d_pk", park_tag)
            for tt in range(4):
                inc(sync.dma_start(out=xstg[:, tt * D:(tt + 1) * D],
                                   in_=x_in[ch * CH + tt * 128: ch * CH + (tt + 1) * 128, :]),
                    "d_x", 16)
        block.sync(_)
        xload = CNT["d_x"]
        P0 = CNT["c_pe"]
        V0 = CNT["c_dve"]

        A0 = CNT["c_act"]

        def pe_tr(ht):
            return P0 + ht + 2 * max(0, ht - 2)

        def tag_tr(ht):
            return pe_tr(ht) + 1

        def tag_ssum(ht):
            if ht <= 13:
                return pe_tr(ht + 2) + 2
            return P0 + NHT + 2 * NHT - 2 * (15 - ht) - 1

        def tag_ssq(ht):
            return tag_ssum(ht) + 1

        def _(tensor):
            wt(tensor, "d_c", state["consts"])
            wt(tensor, "d_x", xload)
            bank_guard(tensor, 4)
            bank_guard(tensor, 5)
            bank_guard(tensor, 2)
            bank_guard(tensor, 3)
            for ht in range(NHT):
                if ht >= 2:
                    wt(tensor, "c_dve", V0 + ht - 1)
                    wt(tensor, "c_act", A0 + 2 * (ht - 2) + 2)
                for tt in range(4):
                    ins = tensor.transpose(
                        psum[4 + ht % 2][:, tt * 128:(tt + 1) * 128],
                        xstg[:, tt * D + ht * 128: tt * D + (ht + 1) * 128],
                        idf[:, :])
                inc(ins, "c_pe")
                if ht >= 2:
                    h2 = ht - 2
                    wt(tensor, "c_act", A0 + 2 * h2 + 1)
                    inc(tensor.matmul(psum[2][0:1, :], oneb[:, 0:1], xn_t(h2),
                                      start=(h2 == 0), stop=(h2 == NHT - 1)), "c_pe")
                    wt(tensor, "c_act", A0 + 2 * h2 + 2)
                    inc(tensor.matmul(psum[3][0:1, :], oneb[:, 0:1], sqst[h2 % 2][:, :],
                                      start=(h2 == 0), stop=(h2 == NHT - 1)), "c_pe")
            for h2 in (14, 15):
                wt(tensor, "c_act", A0 + 2 * h2 + 1)
                inc(tensor.matmul(psum[2][0:1, :], oneb[:, 0:1], xn_t(h2),
                                  start=False, stop=(h2 == NHT - 1)), "c_pe")
                wt(tensor, "c_act", A0 + 2 * h2 + 2)
                inc(tensor.matmul(psum[3][0:1, :], oneb[:, 0:1], sqst[h2 % 2][:, :],
                                  start=False, stop=(h2 == NHT - 1)), "c_pe")
        block.tensor(_)
        pe_stats = CNT["c_pe"]
        assert pe_stats == P0 + 3 * NHT

        def _(vector):
            for ht in range(NHT):
                wt(vector, "c_pe", tag_tr(ht))
                inc(vector.tensor_copy(out=xch_t(ht), in_=psum[4 + ht % 2][:, :]), "c_dve")
        block.vector(_)

        def _(scalar):
            wt(scalar, "d_c", state["consts"])
            for ht in range(NHT):
                wt(scalar, "c_pe", tag_tr(ht))
                wt(scalar, "c_dve", V0 + ht + 1)
                if ht >= 2:
                    wt(scalar, "c_pe", tag_ssq(ht - 2))
                inc(scalar.activation(out=xn_t(ht), in_=psum[4 + ht % 2][:, :],
                                      func=AF.Copy), "c_act")
                inc(scalar.activation(out=sqst[ht % 2][:, :], in_=psum[4 + ht % 2][:, :],
                                      func=AF.Square), "c_act")
        block.scalar(_)
        read_bank(4, "c_dve", V0 + NHT)
        read_bank(5, "c_dve", V0 + NHT)
        read_bank(4, "c_act", A0 + 2 * NHT - 1)
        read_bank(5, "c_act", A0 + 2 * NHT)
        state["preln"] = pe_stats
        return lambda ht: ("c_dve", V0 + ht + 1)

    def ph_load_x_hmajor(ch, src, xch_free_pe):
        def _(sync):
            wt(sync, "c_pe", xch_free_pe)
            for s, v in state["att_done"].items():
                wt(sync, s, v)
            inc(sync.dma_start(
                out=xch[:].rearrange("p (n c) -> p n c", c=CH),
                in_=src.rearrange("(n p) t -> p n t", p=128)[:, :, ch * CH:(ch + 1) * CH]),
                "d_x", 16)
        block.sync(_)
        xload = CNT["d_x"]
        return lambda ht: ("d_x", xload)

    def ph_proj(ch):
        NS = 24
        P0 = CNT["c_pe"]
        wq_base = [CNT["d_wq0"], CNT["d_wq1"]]
        q_base = [CNT["d_q0"], CNT["d_q1"]]

        def _pe_before(s):
            return s + min(max(0, s - 2), 16)

        def tag_p(s):
            return P0 + _pe_before(s) + 1

        def tag_nm(n):
            return P0 + _pe_before(n + 2) + 2

        def _(gpsimd):
            for s in range(NS):
                if s >= 2:
                    wt(gpsimd, "c_pe", tag_p(s - 2))
                else:
                    wt(gpsimd, "c_pe", state["wq_free_pe"])
                sem = "d_wq0" if s % 2 == 0 else "d_wq1"
                if s < 16:
                    src = wq.rearrange("(n p) c -> p n c", p=128)[:, :, s * 128:(s + 1) * 128]
                elif s < 20:
                    dt = s - 16
                    src = wk.rearrange("(n p) c -> p n c", p=128)[:, :, dt * 128:(dt + 1) * 128]
                else:
                    dt = s - 20
                    src = wv.rearrange("(n p) c -> p n c", p=128)[:, :, dt * 128:(dt + 1) * 128]
                inc(gpsimd.dma_start(
                    out=wqb[s % 2][:].rearrange("p (n c) -> p n c", c=128), in_=src), sem, 16)
        block.gpsimd(_)

        V0 = CNT["c_dve"]

        def _v_incs(s):
            n = 0
            for t in range(s):
                n += 1
                if t < 16:
                    n += 1
                if 2 <= t < 18:
                    n += 2
            return n

        def tag_e(s):
            return V0 + _v_incs(s) + 1

        def tag_sq(s):
            return tag_e(s) + 1

        def tag_red(n):
            return tag_e(n + 2) + 1 + (1 if (n + 2) < 16 else 0)

        xtags = state["xn_tags"]

        def _(tensor):
            bank_guard(tensor, 6)
            for s in range(NS):
                sem = "d_wq0" if s % 2 == 0 else "d_wq1"
                wt(tensor, sem, wq_base[s % 2] + 16 * (s // 2 + 1))
                if s >= 2:
                    wt(tensor, "c_dve", tag_e(s - 2))
                else:
                    bank_guard(tensor, s % 2)
                for ht in range(NHT):
                    wt(tensor, "c_dve", xtags[ht])
                    ins = tensor.matmul(
                        psum[s % 2][:, :],
                        wqb[s % 2][:, ht * 128:(ht + 1) * 128],
                        xn_t(ht), start=(ht == 0), stop=(ht == NHT - 1))
                inc(ins, "c_pe")
                assert CNT["c_pe"] == tag_p(s), (s, CNT["c_pe"], tag_p(s))
                if s >= 2 and s - 2 < 16:
                    n = s - 2
                    wt(tensor, "c_dve", tag_sq(n))
                    if n >= 1:
                        wt(tensor, "c_dve", tag_red(n - 1))
                    inc(tensor.matmul(psum[6][0:1, :], oneb[:, 0:1], sqst[n % 2][:, :],
                                      start=True, stop=True), "c_pe")
        block.tensor(_)

        def tag_ev(s):
            return tag_e(s)

        def _(vector):
            wt(vector, "c_dve", state["consts_dve"])
            for s in range(NS):
                wt(vector, "c_pe", tag_p(s))
                semd = "d_q0" if s % 2 == 0 else "d_q1"
                wt(vector, semd, q_base[s % 2] + 16 * (s // 2))
                if s < 16:
                    e = inc(vector.tensor_scalar(out=pst[s % 2][:, :], in0=psum[s % 2][:, :],
                                                 scalar1=SCALE, scalar2=bqs[:, s:s + 1],
                                                 op0=ALU.mult, op1=ALU.add), "c_dve")
                elif s < 20:
                    inc(vector.tensor_scalar_add(pst[s % 2][:, :], psum[s % 2][:, :],
                                                 bk_s[:, s - 16:s - 15]), "c_dve")
                else:
                    inc(vector.tensor_scalar_add(pst[s % 2][:, :], psum[s % 2][:, :],
                                                 bv_s[:, s - 20:s - 19]), "c_dve")
                assert CNT["c_dve"] == tag_e(s), (s, CNT["c_dve"], tag_e(s))
                if s < 16:
                    wt(vector, "c_dve", e)
                    wt(vector, "c_pe", tag_nm(s - 2) if s >= 2 else 0)
                    inc(vector.tensor_mul(sqst[s % 2][:, :], pst[s % 2][:, :],
                                          pst[s % 2][:, :]), "c_dve")
                if 2 <= s < 18:
                    n = s - 2
                    wt(vector, "c_pe", tag_nm(n))
                    if n >= 1:
                        wt(vector, "c_dve", tag_red(n - 1) + 1)
                    t = inc(vector.reduce_max(qmx[0:1, 1:2], psum[6][0:1, :],
                                              axis=mybir.AxisListType.X), "c_dve")
                    wt(vector, "c_dve", t)
                    inc(vector.tensor_max(qmx[0:1, 0:1], qmx[0:1, 0:1],
                                          qmx[0:1, 1:2]), "c_dve")
        block.vector(_)
        read_bank(6, "c_dve", CNT["c_dve"])

        def _(scalar):
            for s in range(NS):
                wt(scalar, "c_dve", tag_ev(s))
                sem = "d_q0" if s % 2 == 0 else "d_q1"
                if s < 16:
                    dst = qpark[s * 128:(s + 1) * 128, ch * CH:(ch + 1) * CH]
                elif s < 20:
                    dst = kv_send[(s - 16) * 128:(s - 15) * 128, ch * CH:(ch + 1) * CH]
                else:
                    dst = kv_send[512 + (s - 20) * 128:512 + (s - 19) * 128,
                                  ch * CH:(ch + 1) * CH]
                inc(scalar.dma_start(out=dst, in_=pst[s % 2][:, :]), sem, 16)
        block.scalar(_)
        read_bank(0, "c_dve", tag_ev(NS - 2))
        read_bank(1, "c_dve", tag_ev(NS - 1))
        state["wq_free_pe"] = tag_p(NS - 1)
        state["proj_last_pe"] = tag_p(NS - 1)

    def ph_park(ch):
        def _(sync):
            for half in range(2):
                wt(sync, "c_dve", state["stt_tags"][8 * half + 7])
                inc(sync.dma_start(
                    out=xp.rearrange("(n p) t -> p n t", p=128)[:, 8 * half:8 * half + 8,
                                                               ch * CH:(ch + 1) * CH],
                    in_=xch[:].rearrange("p (n c) -> p n c", c=CH)[:, 8 * half:8 * half + 8, :]),
                    "d_pk", 16)
        block.sync(_)
        return CNT["d_pk"]

    def ph_out(ch):
        P0 = CNT["c_pe"]
        V0 = CNT["c_dve"]
        O0 = CNT["d_out"]

        stt = state["stt_tags"]

        def _(tensor):
            wt(tensor, "d_c", state["consts"])
            bank_guard(tensor, 4)
            bank_guard(tensor, 5)
            gi = 0
            for tt in range(4):
                for grp in range(4):
                    wt(tensor, "c_dve", stt[grp * 4 + 3])
                    if gi >= 2:
                        wt(tensor, "c_dve", V0 + gi - 1)
                    bank = 4 + gi % 2
                    for j in range(4):
                        ht = grp * 4 + j
                        ins = tensor.transpose(
                            psum[bank][:, j * 128:(j + 1) * 128],
                            xch_t(ht)[:, tt * 128:(tt + 1) * 128],
                            idf[:, :])
                    inc(ins, "c_pe")
                    gi += 1
        block.tensor(_)

        def _(vector):
            gi = 0
            for tt in range(4):
                wt(vector, "d_out", O0 + 16 * tt)
                for grp in range(4):
                    wt(vector, "c_pe", P0 + gi + 1)
                    inc(vector.tensor_copy(out=ostg[:, grp * 512:(grp + 1) * 512],
                                           in_=psum[4 + gi % 2][:, :]), "c_dve")
                    gi += 1
        block.vector(_)

        def _(sync):
            for tt in range(4):
                wt(sync, "c_dve", V0 + 4 * tt + 4)
                inc(sync.dma_start(out=out[ch * CH + tt * 128: ch * CH + (tt + 1) * 128, :],
                                   in_=ostg[:, :]), "d_out", 16)
        block.sync(_)
        read_bank(4, "c_dve", V0 + 16)
        read_bank(5, "c_dve", V0 + 16)
        state["out_pe"] = CNT["c_pe"]
        state["out_last_dve"] = CNT["c_dve"]

    def ph_attention():
        def _(gpsimd):
            wt(gpsimd, "d_q0", CNT["d_q0"])
            wt(gpsimd, "d_q1", CNT["d_q1"])
            inc(gpsimd.collective_compute(
                "AllGather", ALU.bypass,
                replica_groups=[[0, 1], [2, 3], [4, 5], [6, 7]],
                ins=[kv_send[:, :]], outs=[kv_recv[:, :]]), "s_cc", 1)
        block.gpsimd(_)

        qh_base = [CNT["d_qh0"], CNT["d_qh1"]]

        def _(sync):
            wt(sync, "s_cc", CNT["s_cc"])
            inc(sync.dma_start(
                out=kT[:].rearrange("p (g t) -> p g t", t=SEQ)[:, :, 0:TPC],
                in_=kv_recv[0:512, :].rearrange("(g p) t -> p g t", p=128)), "d_kv", 16)
            inc(sync.dma_start(
                out=kT[:].rearrange("p (g t) -> p g t", t=SEQ)[:, :, TPC:SEQ],
                in_=kv_recv[1024:1536, :].rearrange("(g p) t -> p g t", p=128)), "d_kv", 16)
            inc(sync.dma_start(
                out=vtmp[:].rearrange("p (g t) -> p g t", t=SEQ)[:, :, 0:TPC],
                in_=kv_recv[512:1024, :].rearrange("(g p) t -> p g t", p=128)), "d_kv", 16)
            inc(sync.dma_start(
                out=vtmp[:].rearrange("p (g t) -> p g t", t=SEQ)[:, :, TPC:SEQ],
                in_=kv_recv[1536:2048, :].rearrange("(g p) t -> p g t", p=128)), "d_kv", 16)
        block.sync(_)
        kvload = CNT["d_kv"]

        P0 = CNT["c_pe"]
        V0 = CNT["c_dve"]

        def _(tensor):
            wt(tensor, "d_c", state["consts"])
            wt(tensor, "d_kv", kvload)
            for grp in range(16):
                if grp >= 1:
                    wt(tensor, "c_dve", V0 + 8 * grp)
                for j in range(8):
                    tile = grp * 8 + j
                    g, tkt = tile // 32, tile % 32
                    ins = tensor.transpose(
                        psb[:, j * 128:(j + 1) * 128],
                        vtmp[:, g * SEQ + tkt * 128: g * SEQ + (tkt + 1) * 128],
                        idb[:, :])
                inc(ins, "c_pe")
        block.tensor(_)
        state["vt_pe"] = CNT["c_pe"]

        def _(vector):
            for grp in range(16):
                wt(vector, "c_pe", P0 + grp + 1)
                for j in range(8):
                    tile = grp * 8 + j
                    g, tkt = tile // 32, tile % 32
                    inc(vector.tensor_copy(
                        out=v_t[:, g * SEQ + tkt * 128:g * SEQ + (tkt + 1) * 128],
                        in_=psb[:, j * 128:(j + 1) * 128]), "c_dve")
        block.vector(_)
        vt_done = CNT["c_dve"]
        read_bank(7, "c_dve", vt_done)

        prev_mm = 0
        for g in range(4):
            Vs = CNT["c_dve"]

            def _(vector, g=g, prev_mm=prev_mm):
                wt(vector, "c_dve", vt_done)
                wt(vector, "c_pe", prev_mm)
                inc(vector.tensor_mul(qsq[:, :], kT[:, g * SEQ:(g + 1) * SEQ],
                                      kT[:, g * SEQ:(g + 1) * SEQ]), "c_dve")
            block.vector(_)
            Pn = CNT["c_pe"]
            Vr0 = CNT["c_dve"]

            def _(tensor, Vs=Vs, Vr0=Vr0):
                wt(tensor, "c_dve", Vs + 1)
                bank_guard(tensor, 6)
                for q4 in range(8):
                    if q4 >= 1:
                        wt(tensor, "c_dve", Vr0 + q4)
                    inc(tensor.matmul(psum[6][0:1, :], oneb[:, 0:1],
                                      qsq[:, q4 * 512:(q4 + 1) * 512],
                                      start=True, stop=True), "c_pe")
            block.tensor(_)
            prev_mm = CNT["c_pe"]

            def _(vector, g=g, Pn=Pn):
                t = 0
                for q4 in range(8):
                    wt(vector, "c_pe", Pn + q4 + 1)
                    t = inc(vector.reduce_max(rcell(1024 + q4), psum[6][0:1, :],
                                              axis=mybir.AxisListType.X), "c_dve")
                wt(vector, "c_dve", t)
                inc(vector.reduce_max(rcell(1045 + g), rows[0:1, 1024:1032],
                                      axis=mybir.AxisListType.X), "c_dve")
            block.vector(_)
            read_bank(6, "c_dve", CNT["c_dve"])
        knorm_done = CNT["c_dve"]

        def _(vector):
            wt(vector, "c_dve", knorm_done)
            t = inc(vector.reduce_max(rcell(1049), rows[0:1, 1045:1049],
                                      axis=mybir.AxisListType.X), "c_dve")
            wt(vector, "c_dve", t)
            t = inc(vector.tensor_mul(rcell(1050), rcell(1049), qmx[0:1, 0:1]), "c_dve")
        block.vector(_)
        Vb = CNT["c_dve"]

        def _(scalar):
            wt(scalar, "c_dve", Vb)
            a = inc(scalar.activation(out=rcell(1051), in_=rcell(1050), func=AF.Ln),
                    "c_act")
            wt(scalar, "c_act", a)
            inc(scalar.activation(out=rcell(1052), in_=rcell(1051), func=AF.Exp,
                                  scale=0.5), "c_act")
        block.scalar(_)
        Ab = CNT["c_act"]

        def _(vector):
            wt(vector, "c_act", Ab)
            inc(vector.tensor_scalar(out=rcell(1053), in0=rcell(1052),
                                     scalar1=-1.001, scalar2=EXP_OFF,
                                     op0=ALU.mult, op1=ALU.add), "c_dve")
        block.vector(_)
        Vb2 = CNT["c_dve"]

        def _(tensor):
            wt(tensor, "c_dve", Vb2)
            bank_guard(tensor, 6)
            inc(tensor.matmul(psum[6][:, 0:1], onef[0:1, :], rcell(1053),
                              start=True, stop=True), "c_pe")
        block.tensor(_)
        Pb = CNT["c_pe"]

        def _(vector):
            wt(vector, "c_pe", Pb)
            inc(vector.tensor_copy(out=bexp[:, 0:1], in_=psum[6][:, 0:1]), "c_dve")
        block.vector(_)
        bexp_done = CNT["c_dve"]
        read_bank(6, "c_dve", bexp_done)

        vt_pe = state["vt_pe"]
        pk_all = CNT["d_pk"]
        for pi in range(8):
            g = pi % 4
            hd0 = g + 8 * (pi // 4)
            hd1 = hd0 + 4
            def _(sync, hd0=hd0, hd1=hd1):
                wt(sync, "c_pe", state.get("pair_scores_done", 0))
                inc(sync.dma_start(out=qh[0][:, :],
                                   in_=qpark[hd0 * 128:(hd0 + 1) * 128, :]), "d_qh0", 16)
                inc(sync.dma_start(out=qh[1][:, :],
                                   in_=qpark[hd1 * 128:(hd1 + 1) * 128, :]), "d_qh1", 16)
            block.sync(_)
            qt0 = CNT["d_qh0"]
            qt1 = CNT["d_qh1"]

            for blk in range(NCH):
                def _(sync, hd0=hd0, hd1=hd1, blk=blk):
                    wt(sync, "c_dve", state.get("x1_free_0", 0))
                    inc(sync.dma_start(
                        out=sg[0][:, :],
                        in_=xp[hd0 * 128:(hd0 + 1) * 128, blk * CH:(blk + 1) * CH]),
                        "d_x10", 16)
                    wt(sync, "c_dve", state.get("x1_free_1", 0))
                    inc(sync.dma_start(
                        out=sg[1][:, :],
                        in_=xp[hd1 * 128:(hd1 + 1) * 128, blk * CH:(blk + 1) * CH]),
                        "d_x11", 16)
                block.sync(_)
                x1t0 = CNT["d_x10"]
                x1t1 = CNT["d_x11"]

                P0s = CNT["c_pe"]
                A0s = CNT["c_act"]

                def scr_tag(tkt, h):
                    base = 2 * tkt + 2 * max(0, tkt - 2)
                    return P0s + base + h + 1

                def ctx_tag(t2):
                    return P0s + 2 * (t2 + 2) + 2 * t2 + 2 + 2

                def _(tensor, g=g, blk=blk, A0s=A0s, qt0=qt0, qt1=qt1,
                      bexp_done=bexp_done):
                    wt(tensor, "d_qh0", qt0)
                    wt(tensor, "d_qh1", qt1)
                    for b in (0, 1, 2, 3, 4, 5):
                        bank_guard(tensor, b)
                    for tkt in range(32):
                        if tkt >= 2:
                            wt(tensor, "c_act", A0s + tkt - 1)
                        lt = kT[:, g * SEQ + tkt * 128:g * SEQ + (tkt + 1) * 128]
                        inc(tensor.matmul(pbS[tkt % 2][:, 0:512], lt,
                                          qh[0][:, blk * CH:(blk + 1) * CH],
                                          start=True, stop=True), "c_pe")
                        inc(tensor.matmul(pbS[tkt % 2][:, 512:1024], lt,
                                          qh[1][:, blk * CH:(blk + 1) * CH],
                                          start=True, stop=True), "c_pe")
                        if tkt >= 2:
                            t2 = tkt - 2
                            wt(tensor, "c_act", A0s + t2 + 1)
                            vt = v_t[:, g * SEQ + t2 * 128:g * SEQ + (t2 + 1) * 128]
                            inc(tensor.matmul(psum[2][:, :], vt,
                                              pTP[:, t2 * 1024:t2 * 1024 + 512],
                                              start=(t2 == 0), stop=(t2 == 31)), "c_pe")
                            inc(tensor.matmul(psum[3][:, :], vt,
                                              pTP[:, t2 * 1024 + 512:(t2 + 1) * 1024],
                                              start=(t2 == 0), stop=(t2 == 31)), "c_pe")
                    for t2 in (30, 31):
                        wt(tensor, "c_act", A0s + t2 + 1)
                        vt = v_t[:, g * SEQ + t2 * 128:g * SEQ + (t2 + 1) * 128]
                        inc(tensor.matmul(psum[2][:, :], vt,
                                          pTP[:, t2 * 1024:t2 * 1024 + 512],
                                          start=(t2 == 0), stop=(t2 == 31)), "c_pe")
                        inc(tensor.matmul(psum[3][:, :], vt,
                                          pTP[:, t2 * 1024 + 512:(t2 + 1) * 1024],
                                          start=(t2 == 0), stop=(t2 == 31)), "c_pe")
                block.tensor(_)
                state["pair_scores_done"] = CNT["c_pe"]
                Pctx = CNT["c_pe"]

                def _(scalar, A0s=A0s, bexp_done=bexp_done, vt_pe=vt_pe):
                    wt(scalar, "c_dve", bexp_done)
                    wt(scalar, "c_pe", vt_pe)
                    for tkt in range(32):
                        wt(scalar, "c_pe", scr_tag(tkt, 1))
                        inc(scalar.activation(
                            out=pTP[:, tkt * 1024:(tkt + 1) * 1024],
                            in_=pbS[tkt % 2][:, 0:1024],
                            func=AF.Exp, bias=bexp[:, 0:1]), "c_act")
                block.scalar(_)
                assert CNT["c_act"] == A0s + 32
                for j, b in enumerate((0, 1, 4, 5)):
                    read_bank(b, "c_act", A0s + 31 + (j // 2))

                def _(tensor):
                    bank_guard(tensor, 6)
                    for tkt in range(32):
                        ins = tensor.matmul(psum[6][:, :], oneb[:, :],
                                            pTP[:, tkt * 1024:tkt * 1024 + 512],
                                            start=(tkt == 0), stop=(tkt == 31))
                    inc(ins, "c_pe")
                block.tensor(_)
                PdA = CNT["c_pe"]

                def _(vector, PdA=PdA, Pctx=Pctx, x1t0=x1t0):
                    wt(vector, "c_pe", PdA)
                    wt(vector, "d_x2a", CNT["d_x2a"])
                    wt(vector, "c_dve", state.get("epi_prev", 0))
                    t = inc(vector.reciprocal(uu[0][:, :], psum[6][:, :]), "c_dve")
                    wt(vector, "c_dve", t)
                    t = inc(vector.tensor_mul(uu[1][:, :], psum[2][:, :], uu[0][:, :]),
                            "c_dve")
                    wt(vector, "d_x10", x1t0)
                    wt(vector, "c_dve", t)
                    inc(vector.tensor_add(uu[0][:, :], uu[1][:, :], sg[0][:, :]), "c_dve")
                block.vector(_)
                recA = CNT["c_dve"] - 2
                state["x1_free_0"] = CNT["c_dve"]
                x2A = CNT["c_dve"]
                read_bank(2, "c_dve", x2A - 1)

                def _(sync, hd0=hd0, blk=blk, x2A=x2A):
                    wt(sync, "c_dve", x2A)
                    inc(sync.dma_start(
                        out=xp2[hd0 * 128:(hd0 + 1) * 128, blk * CH:(blk + 1) * CH],
                        in_=uu[0][:, :]), "d_x2a", 16)
                block.sync(_)

                def _(tensor, recA=recA):
                    wt(tensor, "c_dve", recA)
                    for tkt in range(32):
                        ins = tensor.matmul(psum[6][:, :], oneb[:, :],
                                            pTP[:, tkt * 1024 + 512:(tkt + 1) * 1024],
                                            start=(tkt == 0), stop=(tkt == 31))
                    inc(ins, "c_pe")
                block.tensor(_)
                PdB = CNT["c_pe"]
                bh = "B0" if blk % 2 == 0 else "B1"
                ohalf = (blk % 2) * 1024

                def _(vector, PdB=PdB, x1t1=x1t1, bh=bh, ohalf=ohalf):
                    wt(vector, "c_pe", PdB)
                    wt(vector, "d_pk", pk_all)
                    wt(vector, "d_x2b0" if bh == "B0" else "d_x2b1",
                       CNT["d_x2b0" if bh == "B0" else "d_x2b1"])
                    t = inc(vector.reciprocal(xch[:, ohalf:ohalf + 512], psum[6][:, :]),
                            "c_dve")
                    wt(vector, "c_dve", t)
                    t = inc(vector.tensor_mul(xch[:, ohalf + 512:ohalf + 1024],
                                              psum[3][:, :], xch[:, ohalf:ohalf + 512]),
                            "c_dve")
                    wt(vector, "d_x11", x1t1)
                    wt(vector, "c_dve", t)
                    inc(vector.tensor_add(xch[:, ohalf:ohalf + 512],
                                          xch[:, ohalf + 512:ohalf + 1024],
                                          sg[1][:, :]), "c_dve")
                block.vector(_)
                state["x1_free_1"] = CNT["c_dve"]
                state["epi_prev"] = CNT["c_dve"]
                x2B = CNT["c_dve"]
                read_bank(3, "c_dve", x2B - 1)
                read_bank(6, "c_dve", x2B - 2)

                def _(sync, hd1=hd1, blk=blk, x2B=x2B, ohalf=ohalf):
                    wt(sync, "c_dve", x2B)
                    inc(sync.dma_start(
                        out=xp2[hd1 * 128:(hd1 + 1) * 128, blk * CH:(blk + 1) * CH],
                        in_=xch[:, ohalf:ohalf + 512]),
                        "d_x2b0" if bh == "B0" else "d_x2b1", 16)
                block.sync(_)

        state["att_done"] = {s: CNT[s] for s in ("d_x2a", "d_x2b0", "d_x2b1")}
        state["att_done_pe"] = CNT["c_pe"]

    ph_consts()
    if stage == 11:
        x_ready = ph_load_x_transposed(0, 0, 0)
        ph_ln(x_ready)
        ph_ffn(0)
        def _(sync):
            wt(sync, "c_dve", state["ffn_done_dve"])
            inc(sync.dma_start(out=dbg_xn, in_=xn[:, :]), "d_out", 16)
            inc(sync.dma_start(out=dbg_hb, in_=hb[:, :]), "d_out", 16)
            inc(sync.dma_start(out=dbg_x, in_=xch[:, :]), "d_out", 16)
            inc(sync.dma_start(out=out[0:128, :], in_=xstg[:, 0:D]), "d_out", 16)
            inc(sync.dma_start(out=out[200:201, 0:2048], in_=ostg[0:1, :]), "d_out", 16)
            inc(sync.dma_start(out=out[210:211, 0:512], in_=sg[1][0:1, :]), "d_out", 16)
            wt(sync, "d_out", CNT["d_out"])
        block.sync(_)
        blockctx.__exit__(None, None, None)
        es.close()
        return nc
    park_tag = 0
    xch_free_pe = 0
    for ch in range(NCH):
        x_ready = ph_load_x_transposed(ch, xch_free_pe, park_tag)
        ph_ln(x_ready, pre=state["preln"])
        ph_ffn(state["gu_prev_free"])
        state["gu_prev_free"] = state["gu_free_pe"]
        if eff >= 2:
            ph_ln(lambda ht: ("c_dve", state["stt_tags"][ht]))
            ph_proj(ch)
            xch_free_pe = state["proj_last_pe"]
        else:
            xch_free_pe = state["dw_free_pe"]
        park_tag = ph_park(ch)
        if eff == 1:
            ph_out(ch)
            xch_free_pe = state["out_pe"]

    if eff >= 2:
        ph_attention()
        state["gu_prev_free"] = state["att_done_pe"]
        for ch in range(NCH):
            x_ready = ph_load_x_hmajor(ch, xp2, xch_free_pe)
            if eff >= 3:
                ph_ln(x_ready)
                ph_ffn(state["gu_prev_free"])
                state["gu_prev_free"] = state["gu_free_pe"]
            else:
                def _(vector, xr=x_ready):
                    s, v = xr(0)
                    wt(vector, s, v)
                    inc(vector.tensor_copy(out=sg[0][:, 0:8], in_=xch[:, 0:8]), "c_dve")
                block.vector(_)
                state["ffn_done_dve"] = CNT["c_dve"]
            ph_out(ch)
            xch_free_pe = state["out_pe"]

    def _(sync):
        if stage == 12:
            for s, v in state["att_done"].items():
                wt(sync, s, v)
            inc(sync.dma_start(out=dbg_q, in_=qpark[:, :]), "d_out", 16)
            inc(sync.dma_start(out=dbg_kv, in_=kv_recv[:, :]), "d_out", 16)
            inc(sync.dma_start(out=dbg_xp, in_=xp[:, :]), "d_out", 16)
        wt(sync, "d_out", CNT["d_out"])
        wt(sync, "d_pk", CNT["d_pk"])
    block.sync(_)

    if stage == 12:
        def _(vector):
            inc(vector.tensor_copy(out=sg[0][:, 0:16], in_=bexp[:, :]), "c_dve")
        block.vector(_)

        def _(sync):
            wt(sync, "c_dve", CNT["c_dve"])
            inc(sync.dma_start(out=dbg_bexp, in_=sg[0][:, 0:16]), "d_out", 16)
            wt(sync, "d_out", CNT["d_out"])
        block.sync(_)

    blockctx.__exit__(None, None, None)
    es.close()
    return nc



def _byte0(w):
    a = np.ascontiguousarray(w)
    assert a.dtype == np.int32
    return a.view(np.uint8).reshape(a.shape[0], a.shape[1], 4)[:, :, 0]


def kernel(L0_attn_hidden_states, L0_attn_Wq, L0_attn_bq, L0_attn_Wk, L0_attn_bk,
           L0_attn_Wv, L0_attn_bv, L0_ffn_gate_w, L0_ffn_gate_scale, L0_ffn_up_w,
           L0_ffn_up_scale, L0_ffn_down_w, L0_ffn_down_scale, _stage=3):
    bf = ml_dtypes.bfloat16
    x = np.ascontiguousarray(np.asarray(L0_attn_hidden_states, dtype=np.float32)).reshape(16384, D)
    wq = np.ascontiguousarray(np.asarray(L0_attn_Wq, dtype=np.float32))
    wk = np.ascontiguousarray(np.asarray(L0_attn_Wk, dtype=np.float32))
    wv = np.ascontiguousarray(np.asarray(L0_attn_Wv, dtype=np.float32))
    bq = np.ascontiguousarray(np.asarray(L0_attn_bq, dtype=np.float32).reshape(16, 128).T)
    bk = np.ascontiguousarray(np.asarray(L0_attn_bk, dtype=np.float32).reshape(4, 128).T)
    bv = np.ascontiguousarray(np.asarray(L0_attn_bv, dtype=np.float32).reshape(4, 128).T)
    gT = np.ascontiguousarray(_byte0(np.asarray(L0_ffn_gate_w)).T)
    uT = np.ascontiguousarray(_byte0(np.asarray(L0_ffn_up_w)).T)
    dT = np.ascontiguousarray(_byte0(np.asarray(L0_ffn_down_w)).T)
    g4 = np.ascontiguousarray(gT.reshape(16, 128, 32, 256).transpose(2, 1, 0, 3))
    u4 = np.ascontiguousarray(uT.reshape(16, 128, 32, 256).transpose(2, 1, 0, 3))
    d4 = np.ascontiguousarray(dT.reshape(64, 128, 16, 128).transpose(2, 1, 0, 3))
    gsc = np.ascontiguousarray(np.asarray(L0_ffn_gate_scale, np.float32).reshape(64, 128).T)
    usc = np.ascontiguousarray(np.asarray(L0_ffn_up_scale, np.float32).reshape(64, 128).T)
    dsc = np.ascontiguousarray(np.asarray(L0_ffn_down_scale, np.float32).reshape(16, 128).T)
    idf = np.eye(128, dtype=np.float32)
    idb = np.eye(128, dtype=np.float32).astype(bf)
    onef = np.ones((128, 128), np.float32)
    oneb = np.ones((128, 128), np.float32).astype(bf)
    mrow = np.full((1, 128), -128.0, np.float32).astype(bf)

    nc = build(stage=_stage)
    shared = {"wq": wq, "wk": wk, "wv": wv, "bq_in": bq, "bk_in": bk, "bv_in": bv,
              "g4": g4, "u4": u4, "d4": d4, "gsc_in": gsc, "usc_in": usc,
              "dsc_in": dsc, "idf_in": idf, "idb_in": idb, "onef_in": onef,
              "oneb_in": oneb, "mrow_in": mrow}
    in_maps = [dict(shared, x_in=np.ascontiguousarray(x[c * TPC:(c + 1) * TPC]))
               for c in range(8)]
    res = run_bass_kernel_spmd(nc, in_maps, core_ids=list(range(8)))
    full = np.concatenate([np.asarray(r["out"], np.float32) for r in res.results], axis=0)
    return full.reshape(4, 4096, D)
